# version 12
# baseline (speedup 1.0000x reference)
import numpy as np
import ml_dtypes
import concourse.bass as bass
import concourse.mybir as mybir

F32 = mybir.dt.float32
BF16 = mybir.dt.bfloat16
U32 = mybir.dt.uint32
AF = mybir.ActivationFunctionType
ALU = mybir.AluOpType
AX = mybir.AxisListType.X
G4 = [[0, 1, 2, 3], [4, 5, 6, 7]]
G8 = [list(range(8))]
ENGINES = ("sync", "gpsimd", "tensor", "vector", "scalar")

SEMS = ("cn", "umx", "am2", "ami", "xA", "xB", "wc", "xnt", "x2t", "fmx", "fmao", "fmx2",
        "sl0", "sl1", "sl2", "sl3", "evA", "evB", "evC", "evD",
        "q2w", "tsp", "tspA", "tspB",
        "kqA", "kqB", "vv",
        "at0A", "at0B", "at1A", "at1B", "at2A", "at2B", "at3A", "at3B",
        "rsA", "rsB", "rsC", "rsD", "rs2s", "fcA", "fcB", "prA", "prB", "obA", "obB",
        "tapA", "tapB", "yme",
        "scc", "stt", "svv", "saa")


class Prog:
    def __init__(self, nc):
        self.nc = nc
        self.streams = {k: [] for k in ENGINES}
        self.sems = {}
        self.cnt = {}
        for s in SEMS:
            self.sems[s] = nc.semaphore(s).__enter__()
            self.cnt[s] = 0

    def on(self, eng, fn):
        self.streams[eng].append(fn)

    def wait(self, eng, sem, tgt):
        assert tgt <= self.cnt[sem], f"wait on future {sem}:{tgt}>{self.cnt[sem]}"
        if tgt <= 0:
            return
        s = self.sems[sem]
        self.on(eng, lambda e, s=s, t=tgt: e.wait_ge(s, t))

    def waitall(self, eng, sem):
        self.wait(eng, sem, self.cnt[sem])

    def emit(self, eng, fn, inc=None, k=1):
        if inc is None:
            self.on(eng, fn)
            return None
        self.cnt[inc] += k
        tgt = self.cnt[inc]
        s = self.sems[inc]
        self.on(eng, lambda e, f=fn, s=s, kk=k: f(e).then_inc(s, kk))
        return tgt

    def dma(self, eng, sem, out, in_, **kw):
        return self.emit(eng, lambda e, o=out, i=in_, kws=dict(kw): e.dma_start(out=o, in_=i, **kws),
                         inc=sem, k=16)


def build(stage=4, taps=()):
    nc = bass.Bass("TRN2", target_bir_lowering=False, debug=False)
    P = Prog(nc)

    x_q = nc.dram_tensor("x_q", [1024, 2048], F32, kind="ExternalInput")
    wq_d = nc.dram_tensor("wq", [2048, 2048], F32, kind="ExternalInput")
    wk_d = nc.dram_tensor("wk", [2048, 2048], F32, kind="ExternalInput")
    wv_d = nc.dram_tensor("wv", [2048, 2048], F32, kind="ExternalInput")
    wo_d = nc.dram_tensor("wo", [2048, 2048], F32, kind="ExternalInput")
    wfc_sl = nc.dram_tensor("wfc_sl", [640, 2048], F32, kind="ExternalInput")
    wpr_sl = nc.dram_tensor("wpr_sl", [256, 5120], F32, kind="ExternalInput")
    meta = nc.dram_tensor("meta", [1, 1], U32, kind="ExternalInput")
    tril = nc.dram_tensor("tril", [128, 128], F32, kind="ExternalInput")
    umask = nc.dram_tensor("umask", [32, 4096], F32, kind="ExternalInput")
    ident = nc.dram_tensor("ident", [128, 128], F32, kind="ExternalInput")
    y = nc.dram_tensor("y", [1024, 2048], F32, kind="ExternalOutput")

    g1i = nc.dram_tensor("g1i", [4, 4, 3, 128, 1024], BF16)
    g1o = nc.dram_tensor("g1o", [4, 4, 4, 3, 128, 1024], BF16)
    rsV = nc.dram_tensor("rsV", [4, 4, 8, 128, 128], BF16)
    xnt = nc.dram_tensor("xnt", [1024, 2048], BF16)
    x2t = nc.dram_tensor("x2t", [1024, 2048], BF16)
    g2i = nc.dram_tensor("g2i", [4, 4, 1024, 128], BF16)
    g2o = nc.dram_tensor("g2o", [4, 4, 4, 1024, 128], BF16)
    rs2 = nc.dram_tensor("rs2", [4, 4, 1024, 128], BF16)
    q2f = nc.dram_tensor("q2f", [16, 640, 128], BF16)
    q2p = nc.dram_tensor("q2p", [40, 256, 128], BF16)
    q2fT_i = nc.dram_tensor("q2fT_i", [5, 16, 128, 128], BF16)
    q2fT_o = nc.dram_tensor("q2fT_o", [8, 5, 16, 128, 128], BF16, addr_space="Shared")
    q2pT_i = nc.dram_tensor("q2pT_i", [5120, 256], BF16)
    q2pT_o = nc.dram_tensor("q2pT_o", [8, 5120, 256], BF16, addr_space="Shared")
    wkb_d = nc.dram_tensor("wkb", [128, 32768], BF16)
    wvb_d = nc.dram_tensor("wvb", [128, 32768], BF16)
    ami_d = nc.dram_tensor("ami", [2, 1], F32)
    amo_d = nc.dram_tensor("amo", [2, 1], F32)

    T = {}
    if "rs1" in taps:
        T["t_rs1"] = nc.dram_tensor("t_rs1", [4, 4, 8, 128, 128], F32, kind="ExternalOutput")
    if "rs2" in taps:
        T["t_rs2"] = nc.dram_tensor("t_rs2", [4, 4, 1024, 128], F32, kind="ExternalOutput")
    if "scal" in taps:
        T["t_scal"] = nc.dram_tensor("t_scal", [128, 8], F32, kind="ExternalOutput")

    SHIFT = 17408

    def A(name, shape, dt, offset):
        return nc.alloc_sbuf_tensor_at(name, shape, dt, offset=offset + SHIFT)
    x_res = A("x_res", [128, 16384], F32, offset=0)
    xn_fm = A("xn_fm", [128, 16384], BF16, offset=65536)
    K_sb = A("K_sb", [128, 8192], BF16, offset=65536)
    Q_sb = A("Q_sb", [128, 8192], BF16, offset=65536 + 16384)
    w_sb = A("w_sb", [128, 32768], BF16, offset=98304)
    h_sb = A("h_sb", [128, 40960], BF16, offset=98304)
    V_sb = A("V_sb", [128, 4608], BF16, offset=163840)
    xn_stage = A("xn_stage", [128, 2048], BF16, offset=173056)
    ev_sb = A("ev_sb", [128, 2048], BF16, offset=173056)
    ln_s = A("ln_s", [128, 64], F32, offset=179200)
    wslab = A("wslab", [128, 4096], F32, offset=180224)
    um_sb = A("um_sb", [32, 4096], F32, offset=180224)
    rl_sb = A("rl_sb", [128, 2048], F32, offset=180224)
    prw_sb = A("prw_sb", [128, 2048], BF16, offset=180224)
    ob_sb = A("ob_sb", [128, 1024], F32, offset=184320)
    Vn_sb = A("Vn_sb", [128, 4608], BF16, offset=98304)
    fcw_sb = A("fcw_sb", [128, 4096], BF16, offset=188416)
    P_sb = A("P_sb", [128, 2048], BF16, offset=196608)
    qa_sb = A("qa_sb", [128, 1024], BF16, offset=196608)
    qb_sb = A("qb_sb", [128, 1024], BF16, offset=198656)
    diag_sb = A("diag_sb", [128, 512], F32, offset=200704)
    att_sb = A("att_sb", [128, 1024], BF16, offset=202752)
    rc_sb = A("rc_sb", [128, 8], F32, offset=204800)
    vs_sm = A("vs_sm", [32, 129], F32, offset=204832)
    vsfm = A("vsfm", [128, 64], F32, offset=205376)
    tril_sb = A("tril_sb", [128, 128], F32, offset=206848)
    ident_sb = A("ident_sb", [128, 128], F32, offset=207360)
    scal_sb = A("scal_sb", [128, 8], F32, offset=207872)
    amred = A("amred", [128, 24], F32, offset=207904)
    vals_sb = A("vals_sb", [1, 16], F32, offset=208000)
    am2_sb = A("am2_sb", [1, 4], F32, offset=208064)
    ones_r = A("ones_r", [1, 128], F32, offset=208096)
    ones_c = A("ones_c", [128, 1], F32, offset=208640)
    ms_sb = A("ms_sb", [1, 1], U32, offset=208672)
    tsp_sb = A("tsp_sb", [128, 1280], BF16, offset=208704)
    onesb = A("onesb", [128, 1], BF16, offset=211648)
    ones2d = A("ones2d", [128, 128], BF16, offset=211264)
    itril_bf = A("itril_bf", [128, 128], mybir.dt.uint8, offset=211520)

    psA = nc.alloc_psum_tensor("psA", [128, 4096], F32)

    C_THRF, C_THRP, C_SFC, C_SPR, C_NTHRF, C_NTHRP = 2, 3, 4, 5, 6, 7
    INV64 = 1.0 / 64.0
    SL = ("sl0", "sl1", "sl2", "sl3")

    P.dma("sync", "cn", tril_sb[:, :], tril[:, :])
    P.dma("sync", "cn", ident_sb[:, :], ident[:, :])
    P.dma("gpsimd", "tapA", ms_sb[:, :], meta[:, :])
    P.emit("vector", lambda e: e.memset(ones_r[:, :], 1.0))
    P.emit("vector", lambda e: e.memset(ones_c[:, :], 1.0))
    t_vones = P.emit("vector", lambda e: e.memset(
        bass.AP(V_sb, 128, [[4608, 128], [144, 32]]), 1.0), inc="svv")
    t_vscnt = P.emit("vector", lambda e: e.memset(vs_sm[:, 128:129], 128.0), inc="svv")
    t_onesb = P.emit("vector", lambda e: e.memset(onesb[:, :], 1.0), inc="svv")

    P.dma("sync", "xA", bass.AP(x_res, 0, [[16384, 128], [2048, 4], [1, 2048]]),
          bass.AP(x_q, 0, [[2048, 128], [262144, 4], [1, 2048]]))
    P.dma("sync", "xB", bass.AP(x_res, 8192, [[16384, 128], [2048, 4], [1, 2048]]),
          bass.AP(x_q, 1048576, [[2048, 128], [262144, 4], [1, 2048]]))
    t_ones2d = P.emit("vector", lambda e: e.memset(ones2d[:, :], 1.0), inc="svv")

    def wcast(dram):
        return P.dma("gpsimd", "wc",
                     bass.AP(w_sb, 0, [[32768, 128], [2048, 16], [1, 2048]]),
                     bass.AP(dram, 0, [[2048, 128], [262144, 16], [1, 2048]]))
    wcast(wq_d)
    P.dma("gpsimd", "wc", bass.AP(wkb_d, 0, [[32768, 128], [2048, 16], [1, 2048]]),
          bass.AP(wk_d, 0, [[2048, 128], [262144, 16], [1, 2048]]))
    for s in range(4):
        for d in range(4):
            P.dma("gpsimd", "wc",
                  bass.AP(wvb_d, 512 * s + 128 * d, [[32768, 128], [2048, 16], [1, 128]]),
                  bass.AP(wv_d, (4 * d + s) * 128, [[2048, 128], [262144, 16], [1, 128]]))

    def ln_block(tb, pre_wait, stage_sem, out_dram, ap_prev):
        sl = 32 * (tb % 2)
        pre_wait()
        if ap_prev[tb % 2]:
            P.wait("vector", "svv", ap_prev[tb % 2])
        for s4 in range(4):
            P.emit("vector", lambda e, tb=tb, sl=sl, s4=s4: e.bn_stats(
                ln_s[:, sl + 6 * s4: sl + 6 * s4 + 6],
                x_res[:, 2048 * tb + 512 * s4: 2048 * tb + 512 * (s4 + 1)]),
                inc="svv" if s4 == 3 else None)
        P.wait("vector", "svv", P.cnt["svv"])
        t2 = P.emit("vector", lambda e, sl=sl: e.bn_aggr(ln_s[:, sl + 24:sl + 26], ln_s[:, sl:sl + 24]), inc="svv")
        P.wait("vector", "svv", t2)
        t3 = P.emit("vector", lambda e, sl=sl: e.tensor_scalar(
            ln_s[:, sl + 26:sl + 27], ln_s[:, sl + 25:sl + 26], 1e-5, None, ALU.add), inc="svv")
        P.wait("vector", "svv", t3)
        t4 = P.emit("vector", lambda e, sl=sl: e.reciprocal(ln_s[:, sl + 27:sl + 28], ln_s[:, sl + 26:sl + 27]), inc="svv")
        P.wait("scalar", "svv", t4)
        t5 = P.emit("scalar", lambda e, sl=sl: e.activation(
            ln_s[:, sl + 28:sl + 29], ln_s[:, sl + 27:sl + 28], AF.Sqrt), inc="saa")
        P.wait("vector", "saa", t5)
        P.waitall("vector", stage_sem)
        t_ap = P.emit("vector", lambda e, tb=tb, sl=sl: e.tensor_scalar(
            xn_stage[:, :], x_res[:, 2048 * tb:2048 * (tb + 1)],
            ln_s[:, sl + 24:sl + 25], ln_s[:, sl + 28:sl + 29], ALU.subtract, ALU.mult), inc="svv")
        ap_prev[tb % 2] = t_ap
        P.wait("sync", "svv", t_ap)
        P.dma("sync", stage_sem, out_dram[128 * tb:128 * (tb + 1), :], xn_stage[:, :])
        return t_ap

    P.waitall("vector", "cn")
    t_trilbf = P.emit("vector", lambda e: e.tensor_scalar(
        itril_bf[:, :], tril_sb[:, :], 0.0, None, ALU.is_equal), inc="svv")
    ap_prev = [0, 0]
    for tb in range(8):
        ln_block(tb, (lambda tb=tb: P.waitall("vector", "xA" if tb < 4 else "xB")), "xnt", xnt, ap_prev)
    P.waitall("sync", "xnt")
    for ic in range(16):
        P.dma("sync", "fmx", xn_fm[:, 1024 * ic:1024 * (ic + 1)],
              xnt[:, 128 * ic:128 * (ic + 1)], transpose=True)

    pieces = [(wfc_sl, i, cp) for i in range(5) for cp in range(2)] + \
             [(wpr_sl, i, cp) for i in range(2) for cp in range(5)]
    t_red = [0] * 20
    for n, (src_d, i, cp) in enumerate(pieces):
        slot = n % 4
        if n >= 4:
            P.wait("scalar", "svv", t_red[n - 4])
        P.dma("scalar", SL[slot], wslab[:, 1024 * slot:1024 * (slot + 1)],
              src_d[128 * i:128 * (i + 1), 1024 * cp:1024 * (cp + 1)])
        P.waitall("vector", SL[slot])
        t_red[n] = P.emit("vector", lambda e, n=n: e.tensor_reduce(
            amred[:, n:n + 1], wslab[:, 1024 * (n % 4):1024 * (n % 4 + 1)],
            AX, ALU.add, apply_absolute_value=True), inc="svv")
    P.wait("vector", "svv", t_red[19])
    P.emit("vector", lambda e: e.tensor_reduce(amred[:, 20:21], amred[:, 0:10], AX, ALU.add))
    tv2 = P.emit("vector", lambda e: e.tensor_reduce(amred[:, 21:22], amred[:, 10:20], AX, ALU.add), inc="svv")
    P.wait("tensor", "svv", tv2)
    tt_am = P.emit("tensor", lambda e: e.matmul(psA[0:2, 3584:3585], amred[:, 20:22], ones_c[:, :],
                                                start=True, stop=True), inc="stt")
    P.wait("vector", "stt", tt_am)
    tv3 = P.emit("vector", lambda e: e.tensor_copy(amred[0:2, 22:23], psA[0:2, 3584:3585]), inc="svv")
    P.wait("sync", "svv", tv3)
    P.dma("sync", "ami", ami_d[:, :], amred[0:2, 22:23])
    P.waitall("gpsimd", "ami")
    t_ar = P.emit("gpsimd", lambda e: e.collective_compute(
        "AllReduce", ALU.add, replica_groups=G8,
        ins=[ami_d.ap().opt()], outs=[amo_d.ap().opt()]), inc="scc")
    t_qkv_mm = {}
    last_pair_ev = {}
    ev_sem4 = ("evA", "evB", "evC", "evD")
    t_ag1k = [0, 0, 0, 0]
    t_sign = [0] * 20
    t_tadd = [0] * 20

    def tern_piece(n):
        src_d, i, cp = pieces[n]
        slot = n % 4
        if n < 4:
            P.wait("scalar", "svv", t_red[16 + n])
        else:
            P.wait("scalar", "saa", t_sign[n - 4])
        P.dma("scalar", SL[slot], wslab[:, 1024 * slot:1024 * (slot + 1)],
              src_d[128 * i:128 * (i + 1), 1024 * cp:1024 * (cp + 1)])
        P.waitall("scalar", SL[slot])
        if n == 0:
            P.wait("scalar", "svv", tv_scal)
        else:
            P.wait("scalar", "svv", t_tadd[n - 1])
            P.waitall("scalar", "q2w")
        cT = C_THRF if src_d is wfc_sl else C_THRP
        cN = C_NTHRF if src_d is wfc_sl else C_NTHRP
        P.emit("scalar", lambda e, slot=slot, cT=cT: e.activation(
            qa_sb[:, :], wslab[:, 1024 * slot:1024 * (slot + 1)], AF.Sign, bias=scal_sb[:, cT:cT + 1]))
        t_sign[n] = P.emit("scalar", lambda e, slot=slot, cN=cN: e.activation(
            qb_sb[:, :], wslab[:, 1024 * slot:1024 * (slot + 1)], AF.Sign, bias=scal_sb[:, cN:cN + 1]), inc="saa")
        P.wait("vector", "saa", t_sign[n])
        t_tadd[n] = P.emit("vector", lambda e: e.tensor_tensor(
            qb_sb[:, :], qa_sb[:, :], qb_sb[:, :], ALU.add), inc="svv")
        P.wait("gpsimd", "svv", t_tadd[n])
        if src_d is wfc_sl:
            dst = bass.AP(q2f, 8 * cp * 81920 + 128 * i * 128, [[128, 128], [81920, 8], [1, 128]])
        else:
            dst = bass.AP(q2p, 8 * cp * 32768 + 128 * i * 128, [[128, 128], [32768, 8], [1, 128]])
        P.dma("gpsimd", "q2w", dst, bass.AP(qb_sb, 0, [[1024, 128], [128, 8], [1, 128]]))

    for p in range(2):
        P.waitall("tensor", "wc")
        if p == 0:
            P.waitall("tensor", "fmx")
        for oc in range(16):
            it = 16 * p + oc
            pr = oc % 2
            if pr in last_pair_ev:
                P.wait("tensor", "svv", last_pair_ev[pr])
            for ic in range(16):
                for hf in range(2):
                    P.emit("tensor", lambda e, pr=pr, hf=hf, ic=ic, oc=oc: e.matmul(
                        psA[:, 512 * (2 * pr + hf):512 * (2 * pr + hf + 1)],
                        w_sb[:, 2048 * ic + 128 * oc:2048 * ic + 128 * (oc + 1)],
                        xn_fm[:, 1024 * ic + 512 * hf:1024 * ic + 512 * (hf + 1)],
                        start=(ic == 0), stop=(ic == 15)),
                        inc="stt" if (ic == 15 and hf == 1) else None)
            t_qkv_mm[(p, oc)] = P.cnt["stt"]
            evs = "evA" if it % 2 == 0 else "evB"
            ecol = 1024 * (it % 2)
            P.wait("vector", "stt", t_qkv_mm[(p, oc)])
            if it == 0:
                P.waitall("vector", "xnt")
            P.waitall("vector", evs)
            for hf in range(2):
                P.emit("vector", lambda e, pr=pr, hf=hf, ecol=ecol: e.tensor_copy(
                    ev_sb[:, ecol + 512 * hf:ecol + 512 * (hf + 1)],
                    psA[:, 512 * (2 * pr + hf):512 * (2 * pr + hf + 1)]),
                    inc="svv" if hf == 1 else None)
            t_ev = P.cnt["svv"]
            last_pair_ev[pr] = t_ev
            P.wait("sync", "svv", t_ev)
            P.dma("sync", evs, g1i[oc % 4, oc // 4, p], ev_sb[:, ecol:ecol + 1024])
            if p == 1:
                for n in range(20):
                    if n * 16 // 20 == oc:
                        tern_piece(n)
        P.wait("gpsimd", "stt", t_qkv_mm[(p, 15)])
        if p == 0:
            P.dma("gpsimd", "wc", bass.AP(w_sb, 0, [[32768, 128], [1, 32768]]),
                  bass.AP(wkb_d, 0, [[32768, 128], [1, 32768]]))
            P.wait("sync", "scc", t_ar)
            P.dma("sync", "am2", am2_sb[0:1, 0:2], amo_d[:, :])
            P.waitall("vector", "am2")
            u1 = P.emit("vector", lambda e: e.tensor_scalar(
                vals_sb[0:1, 0:2], am2_sb[0:1, 0:2], 1.0 / 10485760.0, 1e-5, ALU.mult, ALU.max), inc="svv")
            P.wait("vector", "svv", u1)
            P.emit("vector", lambda e: e.tensor_scalar(vals_sb[0:1, 2:4], vals_sb[0:1, 0:2], 0.7, None, ALU.mult))
            u3 = P.emit("vector", lambda e: e.tensor_scalar(vals_sb[0:1, 4:6], vals_sb[0:1, 0:2], 0.5, None, ALU.mult), inc="svv")
            P.wait("vector", "svv", u3)
            u4 = P.emit("vector", lambda e: e.tensor_scalar(vals_sb[0:1, 6:8], vals_sb[0:1, 2:4], -1.0, None, ALU.mult), inc="svv")
            P.wait("tensor", "svv", u4)
            P.wait("tensor", "svv", tv3)
            tt_bc = P.emit("tensor", lambda e: e.matmul(psA[:, 3584:3592], ones_r[:, :], vals_sb[0:1, 0:8],
                                                        start=True, stop=True), inc="stt")
            P.wait("vector", "stt", tt_bc)
            tv_scal = P.emit("vector", lambda e: e.tensor_copy(scal_sb[:, :], psA[:, 3584:3592]), inc="svv")
        else:
            P.dma("gpsimd", "wc", bass.AP(w_sb, 0, [[32768, 128], [1, 32768]]),
                  bass.AP(wvb_d, 0, [[32768, 128], [1, 32768]]))
    t_tern_lastadd = t_tadd[19]
    t_sign_last = t_sign[19]
    q2w_total = P.cnt["q2w"]

    P.waitall("tensor", "wc")
    P.waitall("tensor", "ami")
    last_pair6 = {}
    t_last_ev = 0
    for u in range(32):
        s, tb = u // 8, u % 8
        pr6 = u % 6
        if pr6 in last_pair6:
            P.wait("tensor", "svv", last_pair6[pr6])
        for ic in range(16):
            P.emit("tensor", lambda e, pr6=pr6, ic=ic, s=s, tb=tb: e.matmul(
                psA[:, 512 * pr6:512 * (pr6 + 1)],
                xn_fm[:, 1024 * ic + 128 * tb:1024 * ic + 128 * (tb + 1)],
                w_sb[:, 2048 * ic + 512 * s:2048 * ic + 512 * (s + 1)],
                start=(ic == 0), stop=(ic == 15)),
                inc="stt" if ic == 15 else None)
        t_qkv_mm[(2, u)] = P.cnt["stt"]
        es = ev_sem4[u % 4]
        ecol = 512 * (u % 4)
        P.wait("vector", "stt", t_qkv_mm[(2, u)])
        if u < 4:
            P.waitall("vector", "evA")
            P.waitall("vector", "evB")
        P.waitall("vector", es)
        t_ev = P.emit("vector", lambda e, pr6=pr6, ecol=ecol: e.tensor_copy(
            ev_sb[:, ecol:ecol + 512], psA[:, 512 * pr6:512 * (pr6 + 1)]), inc="svv")
        last_pair6[pr6] = t_ev
        t_last_ev = t_ev
        P.wait("sync", "svv", t_ev)
        for d in range(4):
            P.dma("sync", es,
                  bass.AP(g1i, (4 * s + d) * 393216 + 262144 + tb * 16384, [[128, 128], [1, 128]]),
                  bass.AP(ev_sb, ecol + 128 * d, [[2048, 128], [1, 128]]))
        if tb == 7:
            for esem in ev_sem4:
                P.waitall("gpsimd", esem)
            t_ag1k[s] = P.emit("gpsimd", lambda e, s=s: e.collective_compute(
                "AllGather", ALU.bypass, replica_groups=G4,
                ins=[g1i[s].opt()], outs=[g1o[s].opt()]), inc="scc")
    t_qkv_mm_last = t_qkv_mm[(2, 31)]
    P.wait("vector", "stt", t_qkv_mm_last)
    t_vones2 = P.emit("vector", lambda e: e.memset(
        bass.AP(Vn_sb, 128, [[4608, 128], [144, 32]]), 1.0), inc="svv")

    P.waitall("scalar", "q2w")
    tsp_rounds = [("f", ic) for ic in range(16)] + [("p", hc) for hc in range(40)]
    for ri, (kind, ic) in enumerate(tsp_rounds):
        slot = ri % 2
        ssem = "tspA" if slot == 0 else "tspB"
        P.waitall("scalar", ssem)
        if kind == "f":
            t_tr = P.dma("scalar", ssem, tsp_sb[:, 640 * slot:640 * (slot + 1)],
                         q2f[ic], transpose=True)
            P.wait("scalar", ssem, t_tr)
            for l in range(5):
                P.dma("scalar", ssem, q2fT_i[l, ic],
                      tsp_sb[:, 640 * slot + 128 * l:640 * slot + 128 * (l + 1)])
        else:
            t_tr = P.dma("scalar", ssem, tsp_sb[:, 640 * slot:640 * slot + 256],
                         q2p[ic], transpose=True)
            P.wait("scalar", ssem, t_tr)
            P.dma("scalar", ssem, q2pT_i[128 * ic:128 * (ic + 1), :],
                  tsp_sb[:, 640 * slot:640 * slot + 256])



    P.waitall("gpsimd", "tapA")
    RSK = ("rsA", "rsB", "rsC", "rsD")

    def go_rs1(e):
        for k in range(4):
            e.wait_ge(P.sems["scc"], t_ag1k[k])
            with e.register(f"rV{k}") as rV:
                e.reg_load(rV, ms_sb[0:1, 0:1])
                e.reg_mul(rV, rV, 393216)
                e.reg_add(rV, rV, k * 6291456 + 262144)
                e.dma_start(out=bass.AP(rsV, k * 131072, [[524288, 4], [1, 131072]]),
                            in_=bass.AP(g1o, rV, [[1572864, 4], [1, 131072]])
                            ).then_inc(P.sems[RSK[k]], 16)
    for k in range(4):
        P.cnt[RSK[k]] += 16
    P.on("gpsimd", go_rs1)

    P.waitall("gpsimd", "tspA")
    P.waitall("gpsimd", "tspB")
    t_agwf = P.emit("gpsimd", lambda e: e.collective_compute(
        "AllGather", ALU.bypass, replica_groups=G8,
        ins=[q2fT_i.ap().opt()], outs=[q2fT_o.ap().opt()]), inc="scc")
    t_agwp = P.emit("gpsimd", lambda e: e.collective_compute(
        "AllGather", ALU.bypass, replica_groups=G8,
        ins=[q2pT_i.ap().opt()], outs=[q2pT_o.ap().opt()]), inc="scc")


    if stage == 1:
        if "t_rs1" in T:
            for k in range(4):
                P.waitall("gpsimd", RSK[k])
            P.dma("gpsimd", "tapA", bass.AP(T["t_rs1"], 0, [[1024, 2048], [1, 1024]]),
                  bass.AP(rsV, 0, [[1024, 2048], [1, 1024]]))
            P.waitall("gpsimd", "tapA")
        P.dma("sync", "yme", bass.AP(y, 0, [[2048, 128], [262144, 8], [1, 2048]]),
              bass.AP(x_res, 0, [[16384, 128], [2048, 8], [1, 2048]]))
        P.waitall("sync", "yme")
        return finish(nc, P)

    last_exp_par = [0, 0]
    last_pv_pb = [0, 0, 0, 0]
    last_diag_pb = [0, 0, 0, 0]
    t_oevac = 0
    head_done = [0, 0, 0, 0]
    t_vs_copy = 0
    t_suffix_last = 0
    t_vstr_prev = 0
    t_vsf_prev = 0
    first_P_use = [True]
    KQ = ("kqA", "kqB")
    ATTK = (("at0A", "at0B"), ("at1A", "at1B"), ("at2A", "at2B"), ("at3A", "at3B"))

    t_kq_done = {}
    P.waitall("sync", "tapA")
    P.wait("sync", "stt", t_qkv_mm_last)
    for k in range(4):
        bufp = k % 2
        P.wait("sync", "scc", t_ag1k[k])
        if k >= 2:
            P.wait("sync", "stt", head_done[k - 2])

        def kq_loads(e, k=k, bufp=bufp, which=1, dst=None):
            with e.register(f"rkq{k}{'KQ'[0 if which else 1]}") as ro:
                e.reg_load(ro, ms_sb[0:1, 0:1])
                e.reg_mul(ro, ro, 393216)
                e.reg_add(ro, ro, k * 6291456 + which * 131072)
                e.dma_start(out=bass.AP(dst, 4096 * bufp, [[8192, 128], [1024, 4], [1, 1024]]),
                            in_=bass.AP(g1o, ro, [[1024, 128], [1572864, 4], [1, 1024]])
                            ).then_inc(P.sems[KQ[bufp]], 16)
        P.cnt[KQ[bufp]] += 16
        P.on("sync", lambda e, k=k, bufp=bufp: kq_loads(e, k, bufp, 1, K_sb))
        P.cnt[KQ[bufp]] += 16
        P.on("sync", lambda e, k=k, bufp=bufp: kq_loads(e, k, bufp, 0, Q_sb))
        vb = V_sb if k % 2 == 0 else Vn_sb
        P.waitall("sync", RSK[k])
        for r in range(4):
            P.dma("sync", "vv", bass.AP(vb, 1152 * r, [[4608, 128], [144, 8], [1, 128]]),
                  bass.AP(rsV, r * 524288 + k * 131072, [[128, 128], [16384, 8], [1, 128]]))
        if k == 0:
            P.wait("sync", "saa", t_sign_last)
            P.dma("sync", "umx", um_sb[:, :], umask[:, :])

        P.waitall("tensor", "vv")
        if k == 0:
            P.waitall("tensor", "cn")
            P.wait("tensor", "svv", t_last_ev)
            P.wait("tensor", "svv", t_onesb)
        if t_vs_copy:
            P.wait("tensor", "svv", t_vs_copy)
        for j in range(32):
            P.emit("tensor", lambda e, j=j, vb=vb: e.matmul(
                psA[:, 3200 + j:3201 + j], vb[:, 144 * j:144 * j + 128], onesb[:, 0:1],
                start=True, stop=True), inc="stt" if j == 31 else None)
        t_vmm = P.cnt["stt"]
        P.wait("vector", "stt", t_vmm)
        if t_vstr_prev:
            P.wait("vector", "stt", t_vstr_prev)
        t_vsf_prev = P.emit("vector", lambda e: e.tensor_copy(
            vsfm[:, 0:32], psA[:, 3200:3232]), inc="svv")
        P.wait("tensor", "svv", t_vsf_prev)
        tt_tr = P.emit("tensor", lambda e: e.transpose(
            psA[0:32, 3072:3200], vsfm[:, 0:32], ident_sb[:, :]), inc="stt")
        t_vstr_prev = tt_tr
        P.wait("vector", "stt", tt_tr)
        if t_suffix_last:
            P.wait("vector", "stt", t_suffix_last)
        P.wait("vector", "svv", t_vscnt)
        t_vs_copy = P.emit("vector", lambda e: e.tensor_copy(vs_sm[:, 0:128], psA[0:32, 3072:3200]), inc="svv")

        P.waitall("tensor", KQ[bufp])
        P.waitall("tensor", "vv")
        if k == 0:
            P.wait("tensor", "svv", t_vones)
        if k == 1:
            P.wait("tensor", "svv", t_vones2)

        def emit_pvs(grp, w, exp_tgt, sel_tgt):
            P.wait("tensor", "saa", exp_tgt)
            if sel_tgt:
                P.wait("tensor", "svv", sel_tgt)
            if grp[0] == 0 and t_oevac:
                P.wait("tensor", "svv", t_oevac)
            for j in grp:
                for i in range(max(j, 4 * w), 4 * w + 4):
                    P.emit("tensor", lambda e, j=j, i=i, w=w, vb=vb: e.matmul(
                        psA[:, 512 * (i % 4):512 * (i % 4) + 129],
                        P_sb[:, 512 * (j % 4) + 128 * (i - 4 * w):512 * (j % 4) + 128 * (i - 4 * w) + 128],
                        vb[:, 144 * j:144 * j + 129],
                        start=(j == 0), stop=False),
                        inc="stt" if (j == grp[-1] and i == 4 * w + 3) else None)
            for j in grp:
                last_pv_pb[j % 4] = P.cnt["stt"]

        for w in range(8):
            groups = []
            jj = 0
            while jj + 1 <= 4 * w - 1:
                groups.append((jj, jj + 1))
                jj += 2
            while jj <= 4 * w + 3:
                groups.append((jj,))
                jj += 1
            pend = None
            for grp in groups:
                tt_qk = 0
                for j in grp:
                    par = j % 2
                    c0 = max(128 * j, 512 * w)
                    N = 512 * (w + 1) - c0
                    P.wait("tensor", "saa", last_exp_par[par])
                    tt_qk = P.emit("tensor", lambda e, par=par, bufp=bufp, j=j, c0=c0, N=N: e.matmul(
                        psA[:, 512 * (4 + par):512 * (4 + par) + N],
                        K_sb[:, 4096 * bufp + 128 * j:4096 * bufp + 128 * (j + 1)],
                        Q_sb[:, 4096 * bufp + c0:4096 * bufp + c0 + N],
                        start=True, stop=True), inc="stt")
                if pend is not None:
                    emit_pvs(*pend)
                if first_P_use[0]:
                    P.wait("scalar", "svv", t_tern_lastadd)
                    P.wait("scalar", "q2w", q2w_total)
                    first_P_use[0] = False
                P.wait("scalar", "stt", tt_qk)
                for j in grp:
                    P.wait("scalar", "stt", last_pv_pb[j % 4])
                if len(grp) == 2:
                    jA = grp[0]
                    t_exp = P.emit("scalar", lambda e, jA=jA: e.activation(
                        P_sb[:, 512 * (jA % 4):512 * (jA % 4) + 1024],
                        psA[:, 2048:3072], AF.Exp, scale=INV64), inc="saa")
                    last_exp_par[0] = t_exp
                    last_exp_par[1] = t_exp
                    sel_tgt = 0
                else:
                    j = grp[0]
                    par = j % 2
                    c0 = max(128 * j, 512 * w)
                    N = 512 * (w + 1) - c0
                    pcol0 = c0 - 512 * w
                    t_exp = P.emit("scalar", lambda e, j=j, par=par, pcol0=pcol0, N=N: e.activation(
                        P_sb[:, 512 * (j % 4) + pcol0:512 * (j % 4) + pcol0 + N],
                        psA[:, 512 * (4 + par):512 * (4 + par) + N], AF.Exp, scale=INV64), inc="saa")
                    last_exp_par[par] = t_exp
                    sel_tgt = 0
                    if j >= 4 * w:
                        P.wait("vector", "saa", t_exp)
                        sel_tgt = P.emit("vector", lambda e, j=j, pcol0=pcol0: e.copy_predicated(
                            P_sb[:, 512 * (j % 4) + pcol0:512 * (j % 4) + pcol0 + 128],
                            itril_bf[:, :], ones2d[:, :]), inc="svv")
                pend = (grp, w, t_exp, sel_tgt)
            emit_pvs(*pend)
            P.wait("tensor", "svv", t_vs_copy)
            if k == 0 and w == 0:
                P.waitall("tensor", "umx")
                P.wait("tensor", "svv", t_ones2d)
                P.wait("tensor", "svv", t_trilbf)
            for i in range(4 * w, 4 * w + 4):
                P.emit("tensor", lambda e, i=i: e.matmul(
                    psA[:, 512 * (i % 4):512 * (i % 4) + 129],
                    um_sb[:, 128 * i:128 * (i + 1)], vs_sm[:, :],
                    start=False, stop=True), inc="stt" if i == 4 * w + 3 else None)
            t_wave = P.cnt["stt"]
            if w == 7:
                t_suffix_last = t_wave
            ab = w % 2
            P.wait("vector", "stt", t_wave)
            if t_oevac:
                P.wait("vector", "svv", t_oevac)
            for u in range(4):
                P.emit("vector", lambda e, ab=ab, u=u: e.reciprocal(
                    rc_sb[:, 4 * ab + u:4 * ab + u + 1], psA[:, 512 * u + 128:512 * u + 129]),
                    inc="svv" if u == 3 else None)
            P.wait("vector", "svv", P.cnt["svv"])
            P.waitall("vector", ATTK[k][ab])
            if w < 2 and k >= 1:
                P.waitall("vector", ATTK[k - 1][ab])
            for u in range(4):
                P.emit("vector", lambda e, ab=ab, u=u: e.tensor_scalar(
                    att_sb[:, 512 * ab + 128 * u:512 * ab + 128 * (u + 1)],
                    psA[:, 512 * u:512 * u + 128], rc_sb[:, 4 * ab + u:4 * ab + u + 1],
                    None, ALU.mult), inc="svv" if u == 3 else None)
            t_oevac = P.cnt["svv"]
            P.wait("sync", "svv", t_oevac)
            for u in range(4):
                i = 4 * w + u
                P.dma("sync", ATTK[k][ab], g2i[k, i // 8][128 * (i % 8):128 * (i % 8 + 1), :],
                      att_sb[:, 512 * ab + 128 * u:512 * ab + 128 * (u + 1)])
            t_attdma_unused = 0
        head_done[k] = P.cnt["stt"]

    t_ag2k = []
    for k in range(4):
        P.waitall("gpsimd", ATTK[k][0])
        P.waitall("gpsimd", ATTK[k][1])
        t_ag2k.append(P.emit("gpsimd", lambda e, k=k: e.collective_compute(
            "AllGather", ALU.bypass, replica_groups=G4,
            ins=[g2i[k].opt()], outs=[g2o[k].opt()]), inc="scc"))

    def go_rs2(e):
        with e.register("ro2") as ro2:
            for k in range(4):
                e.wait_ge(P.sems["scc"], t_ag2k[k])
                e.reg_load(ro2, ms_sb[0:1, 0:1])
                e.reg_mul(ro2, ro2, 131072)
                if k:
                    e.reg_add(ro2, ro2, k * 2097152)
                e.dma_start(out=bass.AP(rs2, k * 524288, [[131072, 4], [1, 131072]]),
                            in_=bass.AP(g2o, ro2, [[524288, 4], [1, 131072]])
                            ).then_inc(P.sems["rs2s"], 16)
    P.cnt["rs2s"] += 4 * 16
    P.on("gpsimd", go_rs2)
    P.wait("gpsimd", "stt", head_done[3])
    wcast(wo_d)

    if stage == 2:
        if "t_rs2" in T:
            P.waitall("gpsimd", "rs2s")
            P.dma("gpsimd", "tapA", bass.AP(T["t_rs2"], 0, [[1024, 2048], [1, 1024]]),
                  bass.AP(rs2, 0, [[1024, 2048], [1, 1024]]))
            P.waitall("gpsimd", "tapA")
        if "t_scal" in T:
            P.wait("sync", "svv", tv_scal)
            P.dma("sync", "tapB", T["t_scal"][:, :], scal_sb[:, :])
            P.waitall("sync", "tapB")
        P.dma("sync", "yme", bass.AP(y, 0, [[2048, 128], [262144, 8], [1, 2048]]),
              bass.AP(x_res, 0, [[16384, 128], [2048, 8], [1, 2048]]))
        P.waitall("sync", "yme")
        return finish(nc, P)

    P.waitall("sync", "rs2s")
    for g in range(16):
        P.dma("sync", "fmao", xn_fm[:, 1024 * g:1024 * (g + 1)], rs2[g % 4, g // 4], transpose=True)

    t_wo_ev = {}
    t_womm = 0
    P.waitall("tensor", "wc")
    P.waitall("tensor", "fmao")
    ap2_prev = [0, 0]
    for tb in range(8):
        b0 = 4 * (tb % 2)
        if tb >= 2:
            P.wait("tensor", "svv", t_wo_ev[tb - 2])
        for ic in range(16):
            for og in range(4):
                P.emit("tensor", lambda e, b0=b0, ic=ic, og=og, tb=tb: e.matmul(
                    psA[:, 512 * (b0 + og):512 * (b0 + og + 1)],
                    xn_fm[:, 1024 * ic + 128 * tb:1024 * ic + 128 * (tb + 1)],
                    w_sb[:, 2048 * ic + 512 * og:2048 * ic + 512 * (og + 1)],
                    start=(ic == 0), stop=(ic == 15)),
                    inc="stt" if (ic == 15 and og == 3) else None)
        t_womm = P.cnt["stt"]
        P.wait("vector", "stt", t_womm)
        for og in range(4):
            P.emit("vector", lambda e, b0=b0, og=og, tb=tb: e.tensor_tensor(
                x_res[:, 2048 * tb + 512 * og:2048 * tb + 512 * (og + 1)],
                psA[:, 512 * (b0 + og):512 * (b0 + og + 1)],
                x_res[:, 2048 * tb + 512 * og:2048 * tb + 512 * (og + 1)], ALU.add),
                inc="svv" if og == 3 else None)
        t_wo_ev[tb] = P.cnt["svv"]
        ln_block(tb, (lambda tb=tb: P.wait("vector", "svv", t_wo_ev[tb])), "x2t", x2t, ap2_prev)

    P.waitall("sync", "x2t")
    P.wait("sync", "stt", t_womm)
    for ic in range(16):
        P.dma("sync", "fmx2", xn_fm[:, 1024 * ic:1024 * (ic + 1)],
              x2t[:, 128 * ic:128 * (ic + 1)], transpose=True)

    if stage == 3:
        P.wait("sync", "svv", t_wo_ev[7])
        P.dma("sync", "yme", bass.AP(y, 0, [[2048, 128], [262144, 8], [1, 2048]]),
              bass.AP(x_res, 0, [[16384, 128], [2048, 8], [1, 2048]]))
        P.waitall("sync", "yme")
        return finish(nc, P)

    FC = ("fcA", "fcB")
    t_fc_mm = {}
    t_sq = {}
    t_relu = {}
    P.wait("sync", "scc", t_agwf)
    for hc in range(40):
        slot = hc % 2
        r, l = hc // 5, hc % 5
        if hc >= 2:
            P.wait("sync", "stt", t_fc_mm[hc - 2])
        P.dma("sync", FC[slot], fcw_sb[:, 2048 * slot:2048 * (slot + 1)],
              bass.AP(q2fT_o, (r * 5 + l) * 16 * 128 * 128, [[128, 128], [16384, 16], [1, 128]]))
        P.waitall("tensor", FC[slot])
        P.waitall("tensor", "fmx2")
        if hc >= 2:
            P.wait("tensor", "saa", t_relu[hc - 2])
        pr = hc % 2
        for ic in range(16):
            for hf in range(2):
                P.emit("tensor", lambda e, pr=pr, hf=hf, ic=ic, slot=slot: e.matmul(
                    psA[:, 512 * (2 * pr + hf):512 * (2 * pr + hf + 1)],
                    fcw_sb[:, 2048 * slot + 128 * ic:2048 * slot + 128 * (ic + 1)],
                    xn_fm[:, 1024 * ic + 512 * hf:1024 * ic + 512 * (hf + 1)],
                    start=(ic == 0), stop=(ic == 15)),
                    inc="stt" if (ic == 15 and hf == 1) else None)
        t_fc_mm[hc] = P.cnt["stt"]
        P.wait("scalar", "stt", t_fc_mm[hc])
        if hc >= 2:
            P.wait("scalar", "svv", t_sq[hc - 2])
        for hf in range(2):
            P.emit("scalar", lambda e, pr=pr, hf=hf, slot=slot: e.activation(
                rl_sb[:, 1024 * slot + 512 * hf:1024 * slot + 512 * (hf + 1)],
                psA[:, 512 * (2 * pr + hf):512 * (2 * pr + hf + 1)],
                AF.Relu, scale=scal_sb[:, C_SFC:C_SFC + 1]),
                inc="saa" if hf == 1 else None)
        t_relu[hc] = P.cnt["saa"]
        P.wait("vector", "saa", t_relu[hc])
        t_sq[hc] = P.emit("vector", lambda e, hc=hc, slot=slot: e.tensor_tensor(
            h_sb[:, 1024 * hc:1024 * (hc + 1)],
            rl_sb[:, 1024 * slot:1024 * (slot + 1)],
            rl_sb[:, 1024 * slot:1024 * (slot + 1)], ALU.mult), inc="svv")
    t_h_done = t_sq[39]

    PR = ("prA", "prB")
    OB = ("obA", "obB")
    P.wait("sync", "scc", t_agwp)
    P.wait("sync", "svv", t_h_done)
    t_quad_ev = 0
    t_prw_mm = {}
    for th in range(2):
        for dh2 in range(2):
            qd = 2 * th + dh2
            for hc in range(40):
                slot = hc % 2
                if hc >= 2:
                    P.wait("sync", "stt", t_prw_mm[(qd, hc - 2)])
                elif qd > 0:
                    P.wait("sync", "stt", t_prw_mm[(qd - 1, 38 + hc)])
                P.dma("sync", PR[slot], bass.AP(prw_sb, 1024 * slot, [[2048, 128], [256, 4], [1, 256]]),
                      bass.AP(q2pT_o, 4 * dh2 * 1310720 + 128 * hc * 256, [[256, 128], [1310720, 4], [1, 256]]))
                P.waitall("tensor", PR[slot])
                if hc == 0:
                    P.wait("tensor", "svv", t_h_done)
                    if t_quad_ev:
                        P.wait("tensor", "svv", t_quad_ev)
                for tb in range(4):
                    tbg = 4 * th + tb
                    for bd in range(2):
                        P.emit("tensor", lambda e, tb=tb, bd=bd, hc=hc, slot=slot, tbg=tbg: e.matmul(
                            psA[:, 512 * (2 * tb + bd):512 * (2 * tb + bd + 1)],
                            h_sb[:, 1024 * hc + 128 * tbg:1024 * hc + 128 * (tbg + 1)],
                            prw_sb[:, 1024 * slot + 512 * bd:1024 * slot + 512 * (bd + 1)],
                            start=(hc == 0), stop=(hc == 39)),
                            inc="stt" if (tb == 3 and bd == 1) else None)
                t_prw_mm[(qd, hc)] = P.cnt["stt"]
            P.wait("vector", "stt", t_prw_mm[(qd, 39)])
            for bi in range(8):
                tb, bd = bi // 2, bi % 2
                tbg = 4 * th + tb
                obs = bi % 2
                P.waitall("vector", OB[obs])
                tq = P.emit("vector", lambda e, bi=bi, tbg=tbg, bd=bd, dh2=dh2, obs=obs: e.scalar_tensor_tensor(
                    ob_sb[:, 512 * obs:512 * (obs + 1)],
                    psA[:, 512 * bi:512 * (bi + 1)],
                    scal_sb[:, C_SPR:C_SPR + 1],
                    x_res[:, 2048 * tbg + 1024 * dh2 + 512 * bd:2048 * tbg + 1024 * dh2 + 512 * (bd + 1)],
                    ALU.mult, ALU.add), inc="svv")
                P.wait("sync", "svv", tq)
                P.dma("sync", OB[obs], y[128 * tbg:128 * (tbg + 1),
                                         1024 * dh2 + 512 * bd:1024 * dh2 + 512 * (bd + 1)],
                      ob_sb[:, 512 * obs:512 * (obs + 1)])
            t_quad_ev = P.cnt["svv"]

    P.waitall("sync", "obA")
    P.waitall("sync", "obB")
    return finish(nc, P)


def finish(nc, P):
    with nc.Block() as block:
        for eng in ENGINES:
            fns = P.streams[eng]

            def runner(e, fns=fns):
                for f in fns:
                    f(e)
            getattr(block, eng)(runner)
    assert nc.sbuf_base <= 17408, f"bass bump allocator grew into overlay: {nc.sbuf_base}"
    return nc


def make_consts():
    s = np.arange(128)[:, None]
    q = np.arange(128)[None, :]
    tril_T = (s <= q).astype(np.float32)
    j = np.arange(32)[:, None]
    i = (np.arange(4096)[None, :] // 128)
    um = (j > i).astype(np.float32)
    return tril_T, um, np.eye(128, dtype=np.float32)


def make_in_maps(inputs):
    x = np.asarray(inputs["L0_attn_x"])
    wq = np.ascontiguousarray(np.asarray(inputs["L0_attn_Wq"], dtype=np.float32))
    wk = np.ascontiguousarray(np.asarray(inputs["L0_attn_Wk"], dtype=np.float32))
    wv = np.ascontiguousarray(np.asarray(inputs["L0_attn_Wv"], dtype=np.float32))
    wo = np.ascontiguousarray(np.asarray(inputs["L0_attn_Wo"], dtype=np.float32))
    wfc = np.asarray(inputs["L0_ffn_w_fc"], dtype=np.float32)
    wpr = np.asarray(inputs["L0_ffn_w_proj"], dtype=np.float32)
    tril_T, um, ident = make_consts()
    maps = []
    for c in range(8):
        r = c % 4
        b = c // 4
        maps.append({
            "x_q": np.ascontiguousarray(x[b, 1024 * r:1024 * (r + 1), :], dtype=np.float32),
            "wq": wq, "wk": wk, "wv": wv, "wo": wo,
            "wfc_sl": np.ascontiguousarray(wfc[640 * c:640 * (c + 1), :]),
            "wpr_sl": np.ascontiguousarray(wpr[256 * c:256 * (c + 1), :]),
            "meta": np.array([[r]], dtype=np.uint32),
            "tril": tril_T, "umask": um, "ident": ident,
        })
    return maps


from concourse.bass_utils import run_bass_kernel_spmd

_cache = {}


def kernel(**inputs):
    if "nc" not in _cache:
        _cache["nc"] = build(stage=4, taps=())
    nc = _cache["nc"]
    maps = make_in_maps(inputs)
    res = run_bass_kernel_spmd(nc, maps, core_ids=list(range(8)))
    out = np.empty((2, 4096, 2048), np.float32)
    for c in range(8):
        r = c % 4
        out[c // 4, 1024 * r:1024 * (r + 1), :] = np.asarray(res.results[c]["y"]).reshape(1024, 2048)
    return out


# version 13
# speedup vs baseline: 1.0503x; 1.0503x over previous
import numpy as np
import ml_dtypes
import concourse.bass as bass
import concourse.mybir as mybir

F32 = mybir.dt.float32
BF16 = mybir.dt.bfloat16
U32 = mybir.dt.uint32
AF = mybir.ActivationFunctionType
ALU = mybir.AluOpType
AX = mybir.AxisListType.X
G4 = [[0, 1, 2, 3], [4, 5, 6, 7]]
G8 = [list(range(8))]
ENGINES = ("sync", "gpsimd", "tensor", "vector", "scalar")

SEMS = ("cn", "umx", "am2", "ami", "xA", "xB", "wc", "xnt", "x2t", "fmx", "fmao", "fmx2",
        "sl0", "sl1", "sl2", "sl3", "evA", "evB", "evC", "evD",
        "ts0", "ts1", "ts2", "ts3", "ts4", "ts5", "ts6", "ts7",
        "tw0", "tw1", "tw2", "tw3", "tw4", "tw5", "tw6", "tw7",
        "kqA", "kqB", "vv",
        "at0A", "at0B", "at1A", "at1B", "at2A", "at2B", "at3A", "at3B",
        "rsA", "rsB", "rsC", "rsD", "rs2s", "fcA", "fcB", "prA", "prB", "obA", "obB",
        "tapA", "tapB", "yme",
        "scc", "stt", "svv", "saa")


class Prog:
    def __init__(self, nc):
        self.nc = nc
        self.streams = {k: [] for k in ENGINES}
        self.sems = {}
        self.cnt = {}
        for s in SEMS:
            self.sems[s] = nc.semaphore(s).__enter__()
            self.cnt[s] = 0

    def on(self, eng, fn):
        self.streams[eng].append(fn)

    def wait(self, eng, sem, tgt):
        assert tgt <= self.cnt[sem], f"wait on future {sem}:{tgt}>{self.cnt[sem]}"
        if tgt <= 0:
            return
        s = self.sems[sem]
        self.on(eng, lambda e, s=s, t=tgt: e.wait_ge(s, t))

    def waitall(self, eng, sem):
        self.wait(eng, sem, self.cnt[sem])

    def emit(self, eng, fn, inc=None, k=1):
        if inc is None:
            self.on(eng, fn)
            return None
        self.cnt[inc] += k
        tgt = self.cnt[inc]
        s = self.sems[inc]
        self.on(eng, lambda e, f=fn, s=s, kk=k: f(e).then_inc(s, kk))
        return tgt

    def dma(self, eng, sem, out, in_, **kw):
        return self.emit(eng, lambda e, o=out, i=in_, kws=dict(kw): e.dma_start(out=o, in_=i, **kws),
                         inc=sem, k=16)


def build(stage=4, taps=()):
    nc = bass.Bass("TRN2", target_bir_lowering=False, debug=False)
    P = Prog(nc)

    x_q = nc.dram_tensor("x_q", [1024, 2048], F32, kind="ExternalInput")
    wq_d = nc.dram_tensor("wq", [2048, 2048], F32, kind="ExternalInput")
    wk_d = nc.dram_tensor("wk", [2048, 2048], F32, kind="ExternalInput")
    wv_d = nc.dram_tensor("wv", [2048, 2048], F32, kind="ExternalInput")
    wo_d = nc.dram_tensor("wo", [2048, 2048], F32, kind="ExternalInput")
    wfc_sl = nc.dram_tensor("wfc_sl", [640, 2048], F32, kind="ExternalInput")
    wpr_sl = nc.dram_tensor("wpr_sl", [256, 5120], F32, kind="ExternalInput")
    meta = nc.dram_tensor("meta", [1, 1], U32, kind="ExternalInput")
    tril = nc.dram_tensor("tril", [128, 128], F32, kind="ExternalInput")
    umask = nc.dram_tensor("umask", [32, 4096], F32, kind="ExternalInput")
    ident = nc.dram_tensor("ident", [128, 128], F32, kind="ExternalInput")
    y = nc.dram_tensor("y", [1024, 2048], F32, kind="ExternalOutput")

    g1i = nc.dram_tensor("g1i", [4, 4, 3, 128, 1024], BF16)
    g1o = nc.dram_tensor("g1o", [4, 4, 4, 3, 128, 1024], BF16)
    rsV = nc.dram_tensor("rsV", [4, 4, 8, 128, 128], BF16)
    xnt = nc.dram_tensor("xnt", [1024, 2048], BF16)
    x2t = nc.dram_tensor("x2t", [1024, 2048], BF16)
    g2i = nc.dram_tensor("g2i", [4, 4, 1024, 128], BF16)
    g2o = nc.dram_tensor("g2o", [4, 4, 4, 1024, 128], BF16)
    rs2 = nc.dram_tensor("rs2", [4, 4, 1024, 128], BF16)
    q2fT_i = nc.dram_tensor("q2fT_i", [5, 128, 16, 128], BF16)
    q2fT_o = nc.dram_tensor("q2fT_o", [8, 5, 128, 16, 128], BF16, addr_space="Shared")
    q2pT_i = nc.dram_tensor("q2pT_i", [5120, 256], BF16)
    q2pT_o = nc.dram_tensor("q2pT_o", [8, 5120, 256], BF16, addr_space="Shared")
    ami_d = nc.dram_tensor("ami", [2, 1], F32)
    amo_d = nc.dram_tensor("amo", [2, 1], F32)

    T = {}
    if "rs1" in taps:
        T["t_rs1"] = nc.dram_tensor("t_rs1", [4, 4, 8, 128, 128], F32, kind="ExternalOutput")
    if "rs2" in taps:
        T["t_rs2"] = nc.dram_tensor("t_rs2", [4, 4, 1024, 128], F32, kind="ExternalOutput")
    if "scal" in taps:
        T["t_scal"] = nc.dram_tensor("t_scal", [128, 8], F32, kind="ExternalOutput")

    SHIFT = 17408

    def A(name, shape, dt, offset):
        return nc.alloc_sbuf_tensor_at(name, shape, dt, offset=offset + SHIFT)
    x_res = A("x_res", [128, 16384], F32, offset=0)
    xn_fm = A("xn_fm", [128, 16384], BF16, offset=65536)
    K_sb = A("K_sb", [128, 8192], BF16, offset=65536)
    Q_sb = A("Q_sb", [128, 8192], BF16, offset=65536 + 16384)
    w_sb = A("w_sb", [128, 32768], BF16, offset=98304)
    h_sb = A("h_sb", [128, 40960], BF16, offset=98304)
    V_sb = A("V_sb", [128, 4608], BF16, offset=163840)
    xn_stage = A("xn_stage", [128, 2048], BF16, offset=173056)
    ev_sb = A("ev_sb", [128, 2048], BF16, offset=173056)
    ln_s = A("ln_s", [128, 64], F32, offset=179200)
    wslab = A("wslab", [128, 4096], F32, offset=180224)
    um_sb = A("um_sb", [32, 4096], F32, offset=180224)
    rl_sb = A("rl_sb", [128, 2048], F32, offset=180224)
    prw_sb = A("prw_sb", [128, 2048], BF16, offset=180224)
    ob_sb = A("ob_sb", [128, 1024], F32, offset=184320)
    Vn_sb = A("Vn_sb", [128, 4608], BF16, offset=98304)
    fcw_sb = A("fcw_sb", [128, 4096], BF16, offset=188416)
    P_sb = A("P_sb", [128, 2048], BF16, offset=196608)
    qa_sb = A("qa_sb", [128, 1024], BF16, offset=196608)
    qb_sb = A("qb_sb", [128, 1024], BF16, offset=198656)
    diag_sb = A("diag_sb", [128, 512], F32, offset=200704)
    att_sb = A("att_sb", [128, 1024], BF16, offset=202752)
    rc_sb = A("rc_sb", [128, 8], F32, offset=204800)
    vs_sm = A("vs_sm", [32, 129], F32, offset=204832)
    vsfm = A("vsfm", [128, 64], F32, offset=205376)
    tril_sb = A("tril_sb", [128, 128], F32, offset=206848)
    ident_sb = A("ident_sb", [128, 128], F32, offset=207360)
    scal_sb = A("scal_sb", [128, 8], F32, offset=207872)
    amred = A("amred", [128, 24], F32, offset=207904)
    vals_sb = A("vals_sb", [1, 16], F32, offset=208000)
    am2_sb = A("am2_sb", [1, 4], F32, offset=208064)
    ones_r = A("ones_r", [1, 128], F32, offset=208096)
    ones_c = A("ones_c", [128, 1], F32, offset=208640)
    ms_sb = A("ms_sb", [1, 1], U32, offset=208672)
    tsp_sb = A("tsp_sb", [128, 1280], BF16, offset=208704)
    onesb = A("onesb", [128, 1], BF16, offset=211648)
    ones2d = A("ones2d", [128, 128], BF16, offset=211264)
    itril_bf = A("itril_bf", [128, 128], mybir.dt.uint8, offset=211520)

    psA = nc.alloc_psum_tensor("psA", [128, 4096], F32)

    C_THRF, C_THRP, C_SFC, C_SPR, C_NTHRF, C_NTHRP = 2, 3, 4, 5, 6, 7
    INV64 = 1.0 / 64.0
    SL = ("sl0", "sl1", "sl2", "sl3")

    P.dma("sync", "cn", tril_sb[:, :], tril[:, :])
    P.dma("sync", "cn", ident_sb[:, :], ident[:, :])
    P.dma("gpsimd", "tapA", ms_sb[:, :], meta[:, :])
    P.emit("vector", lambda e: e.memset(ones_r[:, :], 1.0))
    P.emit("vector", lambda e: e.memset(ones_c[:, :], 1.0))
    t_vones = P.emit("vector", lambda e: e.memset(
        bass.AP(V_sb, 128, [[4608, 128], [144, 32]]), 1.0), inc="svv")
    t_vscnt = P.emit("vector", lambda e: e.memset(vs_sm[:, 128:129], 128.0), inc="svv")
    t_onesb = P.emit("vector", lambda e: e.memset(onesb[:, :], 1.0), inc="svv")

    P.dma("sync", "xA", bass.AP(x_res, 0, [[16384, 128], [2048, 4], [1, 2048]]),
          bass.AP(x_q, 0, [[2048, 128], [262144, 4], [1, 2048]]))
    P.dma("sync", "xB", bass.AP(x_res, 8192, [[16384, 128], [2048, 4], [1, 2048]]),
          bass.AP(x_q, 1048576, [[2048, 128], [262144, 4], [1, 2048]]))
    t_ones2d = P.emit("vector", lambda e: e.memset(ones2d[:, :], 1.0), inc="svv")

    def wcast(dram):
        return P.dma("gpsimd", "wc",
                     bass.AP(w_sb, 0, [[32768, 128], [2048, 16], [1, 2048]]),
                     bass.AP(dram, 0, [[2048, 128], [262144, 16], [1, 2048]]))
    def wcast_v(dram):
        for s in range(4):
            for d in range(4):
                P.dma("gpsimd", "wc",
                      bass.AP(w_sb, 512 * s + 128 * d, [[32768, 128], [2048, 16], [1, 128]]),
                      bass.AP(dram, (4 * d + s) * 128, [[2048, 128], [262144, 16], [1, 128]]))
    wcast(wq_d)

    def ln_block(tb, pre_wait, stage_sem, out_dram, ap_prev):
        sl = 32 * (tb % 2)
        pre_wait()
        if ap_prev[tb % 2]:
            P.wait("vector", "svv", ap_prev[tb % 2])
        for s4 in range(4):
            P.emit("vector", lambda e, tb=tb, sl=sl, s4=s4: e.bn_stats(
                ln_s[:, sl + 6 * s4: sl + 6 * s4 + 6],
                x_res[:, 2048 * tb + 512 * s4: 2048 * tb + 512 * (s4 + 1)]),
                inc="svv" if s4 == 3 else None)
        P.wait("vector", "svv", P.cnt["svv"])
        t2 = P.emit("vector", lambda e, sl=sl: e.bn_aggr(ln_s[:, sl + 24:sl + 26], ln_s[:, sl:sl + 24]), inc="svv")
        P.wait("vector", "svv", t2)
        t3 = P.emit("vector", lambda e, sl=sl: e.tensor_scalar(
            ln_s[:, sl + 26:sl + 27], ln_s[:, sl + 25:sl + 26], 1e-5, None, ALU.add), inc="svv")
        P.wait("vector", "svv", t3)
        t4 = P.emit("vector", lambda e, sl=sl: e.reciprocal(ln_s[:, sl + 27:sl + 28], ln_s[:, sl + 26:sl + 27]), inc="svv")
        P.wait("scalar", "svv", t4)
        t5 = P.emit("scalar", lambda e, sl=sl: e.activation(
            ln_s[:, sl + 28:sl + 29], ln_s[:, sl + 27:sl + 28], AF.Sqrt), inc="saa")
        P.wait("vector", "saa", t5)
        P.waitall("vector", stage_sem)
        t_ap = P.emit("vector", lambda e, tb=tb, sl=sl: e.tensor_scalar(
            xn_stage[:, :], x_res[:, 2048 * tb:2048 * (tb + 1)],
            ln_s[:, sl + 24:sl + 25], ln_s[:, sl + 28:sl + 29], ALU.subtract, ALU.mult), inc="svv")
        ap_prev[tb % 2] = t_ap
        P.wait("sync", "svv", t_ap)
        P.dma("sync", stage_sem, out_dram[128 * tb:128 * (tb + 1), :], xn_stage[:, :])
        return t_ap

    P.waitall("vector", "cn")
    t_trilbf = P.emit("vector", lambda e: e.tensor_scalar(
        itril_bf[:, :], tril_sb[:, :], 0.0, None, ALU.is_equal), inc="svv")
    ap_prev = [0, 0]
    for tb in range(8):
        ln_block(tb, (lambda tb=tb: P.waitall("vector", "xA" if tb < 4 else "xB")), "xnt", xnt, ap_prev)
    P.waitall("sync", "xnt")
    for ic in range(16):
        P.dma("sync", "fmx", xn_fm[:, 1024 * ic:1024 * (ic + 1)],
              xnt[:, 128 * ic:128 * (ic + 1)], transpose=True)

    pieces = [(wfc_sl, i, cp) for i in range(5) for cp in range(2)] + \
             [(wpr_sl, i, cp) for i in range(2) for cp in range(5)]
    t_red = [0] * 20
    for n, (src_d, i, cp) in enumerate(pieces):
        slot = n % 4
        if n >= 4:
            P.wait("scalar", "svv", t_red[n - 4])
        P.dma("scalar", SL[slot], wslab[:, 1024 * slot:1024 * (slot + 1)],
              src_d[128 * i:128 * (i + 1), 1024 * cp:1024 * (cp + 1)])
        P.waitall("vector", SL[slot])
        t_red[n] = P.emit("vector", lambda e, n=n: e.tensor_reduce(
            amred[:, n:n + 1], wslab[:, 1024 * (n % 4):1024 * (n % 4 + 1)],
            AX, ALU.add, apply_absolute_value=True), inc="svv")
    P.wait("vector", "svv", t_red[19])
    P.emit("vector", lambda e: e.tensor_reduce(amred[:, 20:21], amred[:, 0:10], AX, ALU.add))
    tv2 = P.emit("vector", lambda e: e.tensor_reduce(amred[:, 21:22], amred[:, 10:20], AX, ALU.add), inc="svv")
    P.wait("tensor", "svv", tv2)
    tt_am = P.emit("tensor", lambda e: e.matmul(psA[0:2, 3584:3585], amred[:, 20:22], ones_c[:, :],
                                                start=True, stop=True), inc="stt")
    P.wait("vector", "stt", tt_am)
    tv3 = P.emit("vector", lambda e: e.tensor_copy(amred[0:2, 22:23], psA[0:2, 3584:3585]), inc="svv")
    P.wait("sync", "svv", tv3)
    P.dma("sync", "ami", ami_d[:, :], amred[0:2, 22:23])
    P.waitall("gpsimd", "ami")
    t_ar = P.emit("gpsimd", lambda e: e.collective_compute(
        "AllReduce", ALU.add, replica_groups=G8,
        ins=[ami_d.ap().opt()], outs=[amo_d.ap().opt()]), inc="scc")
    t_qkv_mm = {}
    last_pair_ev = {}
    ev_sem4 = ("evA", "evB", "evC", "evD")
    t_ag1k = [0, 0, 0, 0]
    t_sign = [0] * 20
    t_tadd = [0] * 20
    TS = ("ts0", "ts1", "ts2", "ts3", "ts4", "ts5", "ts6", "ts7")
    TW = ("tw0", "tw1", "tw2", "tw3", "tw4", "tw5", "tw6", "tw7")

    def tern_piece(n):
        src_d, i, cp = pieces[n]
        slot = n % 4
        if n < 4:
            P.wait("scalar", "svv", t_red[16 + n])
        else:
            P.wait("scalar", "saa", t_sign[n - 4])
        P.dma("scalar", SL[slot], wslab[:, 1024 * slot:1024 * (slot + 1)],
              src_d[128 * i:128 * (i + 1), 1024 * cp:1024 * (cp + 1)])
        P.waitall("scalar", SL[slot])
        if n == 0:
            P.wait("scalar", "svv", tv_scal)
        else:
            P.wait("scalar", "svv", t_tadd[n - 1])
            for b in range(8):
                P.waitall("scalar", TS[b])
        cT = C_THRF if src_d is wfc_sl else C_THRP
        cN = C_NTHRF if src_d is wfc_sl else C_NTHRP
        P.emit("scalar", lambda e, slot=slot, cT=cT: e.activation(
            qa_sb[:, :], wslab[:, 1024 * slot:1024 * (slot + 1)], AF.Sign, bias=scal_sb[:, cT:cT + 1]))
        t_sign[n] = P.emit("scalar", lambda e, slot=slot, cN=cN: e.activation(
            qb_sb[:, :], wslab[:, 1024 * slot:1024 * (slot + 1)], AF.Sign, bias=scal_sb[:, cN:cN + 1]), inc="saa")
        P.wait("vector", "saa", t_sign[n])
        t_tadd[n] = P.emit("vector", lambda e: e.tensor_tensor(
            qb_sb[:, :], qa_sb[:, :], qb_sb[:, :], ALU.add), inc="svv")
        P.wait("scalar", "svv", t_tadd[n])
        for b in range(8):
            P.waitall("scalar", TW[b])
            P.dma("scalar", TS[b], tsp_sb[:, 128 * b:128 * (b + 1)],
                  qb_sb[:, 128 * b:128 * (b + 1)], transpose=True)
        for b in range(8):
            P.waitall("gpsimd", TS[b])
            if src_d is wfc_sl:
                dst = bass.AP(q2fT_i, i * 262144 + (8 * cp + b) * 128, [[2048, 128], [1, 128]])
            else:
                dst = bass.AP(q2pT_i, (8 * cp + b) * 32768 + 128 * i, [[256, 128], [1, 128]])
            P.dma("gpsimd", TW[b], dst, tsp_sb[:, 128 * b:128 * (b + 1)])

    for p in range(2):
        P.waitall("tensor", "wc")
        if p == 0:
            P.waitall("tensor", "fmx")
        for oc in range(16):
            it = 16 * p + oc
            pr = oc % 2
            if pr in last_pair_ev:
                P.wait("tensor", "svv", last_pair_ev[pr])
            for ic in range(16):
                for hf in range(2):
                    P.emit("tensor", lambda e, pr=pr, hf=hf, ic=ic, oc=oc: e.matmul(
                        psA[:, 512 * (2 * pr + hf):512 * (2 * pr + hf + 1)],
                        w_sb[:, 2048 * ic + 128 * oc:2048 * ic + 128 * (oc + 1)],
                        xn_fm[:, 1024 * ic + 512 * hf:1024 * ic + 512 * (hf + 1)],
                        start=(ic == 0), stop=(ic == 15)),
                        inc="stt" if (ic == 15 and hf == 1) else None)
            t_qkv_mm[(p, oc)] = P.cnt["stt"]
            evs = "evA" if it % 2 == 0 else "evB"
            ecol = 1024 * (it % 2)
            P.wait("vector", "stt", t_qkv_mm[(p, oc)])
            if it == 0:
                P.waitall("vector", "xnt")
            P.waitall("vector", evs)
            for hf in range(2):
                P.emit("vector", lambda e, pr=pr, hf=hf, ecol=ecol: e.tensor_copy(
                    ev_sb[:, ecol + 512 * hf:ecol + 512 * (hf + 1)],
                    psA[:, 512 * (2 * pr + hf):512 * (2 * pr + hf + 1)]),
                    inc="svv" if hf == 1 else None)
            t_ev = P.cnt["svv"]
            last_pair_ev[pr] = t_ev
            P.wait("sync", "svv", t_ev)
            P.dma("sync", evs, g1i[oc % 4, oc // 4, p], ev_sb[:, ecol:ecol + 1024])
            if p == 1:
                for n in range(20):
                    if n * 16 // 20 == oc:
                        tern_piece(n)
        P.wait("gpsimd", "stt", t_qkv_mm[(p, 15)])
        if p == 0:
            wcast(wk_d)
            P.wait("sync", "scc", t_ar)
            P.dma("sync", "am2", am2_sb[0:1, 0:2], amo_d[:, :])
            P.waitall("vector", "am2")
            u1 = P.emit("vector", lambda e: e.tensor_scalar(
                vals_sb[0:1, 0:2], am2_sb[0:1, 0:2], 1.0 / 10485760.0, 1e-5, ALU.mult, ALU.max), inc="svv")
            P.wait("vector", "svv", u1)
            P.emit("vector", lambda e: e.tensor_scalar(vals_sb[0:1, 2:4], vals_sb[0:1, 0:2], 0.7, None, ALU.mult))
            u3 = P.emit("vector", lambda e: e.tensor_scalar(vals_sb[0:1, 4:6], vals_sb[0:1, 0:2], 0.5, None, ALU.mult), inc="svv")
            P.wait("vector", "svv", u3)
            u4 = P.emit("vector", lambda e: e.tensor_scalar(vals_sb[0:1, 6:8], vals_sb[0:1, 2:4], -1.0, None, ALU.mult), inc="svv")
            P.wait("tensor", "svv", u4)
            P.wait("tensor", "svv", tv3)
            tt_bc = P.emit("tensor", lambda e: e.matmul(psA[:, 3584:3592], ones_r[:, :], vals_sb[0:1, 0:8],
                                                        start=True, stop=True), inc="stt")
            P.wait("vector", "stt", tt_bc)
            tv_scal = P.emit("vector", lambda e: e.tensor_copy(scal_sb[:, :], psA[:, 3584:3592]), inc="svv")
        else:
            wcast_v(wv_d)
    t_tern_lastadd = t_tadd[19]
    t_sign_last = t_sign[19]

    P.waitall("tensor", "wc")
    P.waitall("tensor", "ami")
    last_pair6 = {}
    t_last_ev = 0
    for u in range(32):
        s, tb = u // 8, u % 8
        pr6 = u % 6
        if pr6 in last_pair6:
            P.wait("tensor", "svv", last_pair6[pr6])
        for ic in range(16):
            P.emit("tensor", lambda e, pr6=pr6, ic=ic, s=s, tb=tb: e.matmul(
                psA[:, 512 * pr6:512 * (pr6 + 1)],
                xn_fm[:, 1024 * ic + 128 * tb:1024 * ic + 128 * (tb + 1)],
                w_sb[:, 2048 * ic + 512 * s:2048 * ic + 512 * (s + 1)],
                start=(ic == 0), stop=(ic == 15)),
                inc="stt" if ic == 15 else None)
        t_qkv_mm[(2, u)] = P.cnt["stt"]
        es = ev_sem4[u % 4]
        ecol = 512 * (u % 4)
        P.wait("vector", "stt", t_qkv_mm[(2, u)])
        if u < 4:
            P.waitall("vector", "evA")
            P.waitall("vector", "evB")
        P.waitall("vector", es)
        t_ev = P.emit("vector", lambda e, pr6=pr6, ecol=ecol: e.tensor_copy(
            ev_sb[:, ecol:ecol + 512], psA[:, 512 * pr6:512 * (pr6 + 1)]), inc="svv")
        last_pair6[pr6] = t_ev
        t_last_ev = t_ev
        P.wait("sync", "svv", t_ev)
        for d in range(4):
            P.dma("sync", es,
                  bass.AP(g1i, (4 * s + d) * 393216 + 262144 + tb * 16384, [[128, 128], [1, 128]]),
                  bass.AP(ev_sb, ecol + 128 * d, [[2048, 128], [1, 128]]))
        if tb == 7:
            for esem in ev_sem4:
                P.waitall("gpsimd", esem)
            t_ag1k[s] = P.emit("gpsimd", lambda e, s=s: e.collective_compute(
                "AllGather", ALU.bypass, replica_groups=G4,
                ins=[g1i[s].opt()], outs=[g1o[s].opt()]), inc="scc")
    t_qkv_mm_last = t_qkv_mm[(2, 31)]
    P.wait("vector", "stt", t_qkv_mm_last)
    t_vones2 = P.emit("vector", lambda e: e.memset(
        bass.AP(Vn_sb, 128, [[4608, 128], [144, 32]]), 1.0), inc="svv")

    P.waitall("gpsimd", "tapA")
    RSK = ("rsA", "rsB", "rsC", "rsD")

    def go_rs1(e):
        for k in range(4):
            e.wait_ge(P.sems["scc"], t_ag1k[k])
            with e.register(f"rV{k}") as rV:
                e.reg_load(rV, ms_sb[0:1, 0:1])
                e.reg_mul(rV, rV, 393216)
                e.reg_add(rV, rV, k * 6291456 + 262144)
                e.dma_start(out=bass.AP(rsV, k * 131072, [[524288, 4], [1, 131072]]),
                            in_=bass.AP(g1o, rV, [[1572864, 4], [1, 131072]])
                            ).then_inc(P.sems[RSK[k]], 16)
    for k in range(4):
        P.cnt[RSK[k]] += 16
    P.on("gpsimd", go_rs1)


    if stage == 1:
        if "t_rs1" in T:
            for k in range(4):
                P.waitall("gpsimd", RSK[k])
            P.dma("gpsimd", "tapA", bass.AP(T["t_rs1"], 0, [[1024, 2048], [1, 1024]]),
                  bass.AP(rsV, 0, [[1024, 2048], [1, 1024]]))
            P.waitall("gpsimd", "tapA")
        P.dma("sync", "yme", bass.AP(y, 0, [[2048, 128], [262144, 8], [1, 2048]]),
              bass.AP(x_res, 0, [[16384, 128], [2048, 8], [1, 2048]]))
        P.waitall("sync", "yme")
        return finish(nc, P)

    last_exp_par = [0, 0]
    last_pv_pb = [0, 0, 0, 0]
    last_diag_pb = [0, 0, 0, 0]
    t_oevac = 0
    head_done = [0, 0, 0, 0]
    t_vs_copy = 0
    t_suffix_last = 0
    t_vstr_prev = 0
    t_vsf_prev = 0
    first_P_use = [True]
    KQ = ("kqA", "kqB")
    ATTK = (("at0A", "at0B"), ("at1A", "at1B"), ("at2A", "at2B"), ("at3A", "at3B"))

    t_kq_done = {}
    P.waitall("sync", "tapA")
    P.wait("sync", "stt", t_qkv_mm_last)
    for k in range(4):
        bufp = k % 2
        P.wait("sync", "scc", t_ag1k[k])
        if k >= 2:
            P.wait("sync", "stt", head_done[k - 2])

        def kq_loads(e, k=k, bufp=bufp, which=1, dst=None):
            with e.register(f"rkq{k}{'KQ'[0 if which else 1]}") as ro:
                e.reg_load(ro, ms_sb[0:1, 0:1])
                e.reg_mul(ro, ro, 393216)
                e.reg_add(ro, ro, k * 6291456 + which * 131072)
                e.dma_start(out=bass.AP(dst, 4096 * bufp, [[8192, 128], [1024, 4], [1, 1024]]),
                            in_=bass.AP(g1o, ro, [[1024, 128], [1572864, 4], [1, 1024]])
                            ).then_inc(P.sems[KQ[bufp]], 16)
        P.cnt[KQ[bufp]] += 16
        P.on("sync", lambda e, k=k, bufp=bufp: kq_loads(e, k, bufp, 1, K_sb))
        P.cnt[KQ[bufp]] += 16
        P.on("sync", lambda e, k=k, bufp=bufp: kq_loads(e, k, bufp, 0, Q_sb))
        vb = V_sb if k % 2 == 0 else Vn_sb
        P.waitall("sync", RSK[k])
        for r in range(4):
            P.dma("sync", "vv", bass.AP(vb, 1152 * r, [[4608, 128], [144, 8], [1, 128]]),
                  bass.AP(rsV, r * 524288 + k * 131072, [[128, 128], [16384, 8], [1, 128]]))
        if k == 0:
            P.wait("sync", "saa", t_sign_last)
            P.dma("sync", "umx", um_sb[:, :], umask[:, :])

        P.waitall("tensor", "vv")
        if k == 0:
            P.waitall("tensor", "cn")
            P.wait("tensor", "svv", t_last_ev)
            P.wait("tensor", "svv", t_onesb)
        if t_vs_copy:
            P.wait("tensor", "svv", t_vs_copy)
        for j in range(32):
            P.emit("tensor", lambda e, j=j, vb=vb: e.matmul(
                psA[:, 3200 + j:3201 + j], vb[:, 144 * j:144 * j + 128], onesb[:, 0:1],
                start=True, stop=True), inc="stt" if j == 31 else None)
        t_vmm = P.cnt["stt"]
        P.wait("vector", "stt", t_vmm)
        if t_vstr_prev:
            P.wait("vector", "stt", t_vstr_prev)
        t_vsf_prev = P.emit("vector", lambda e: e.tensor_copy(
            vsfm[:, 0:32], psA[:, 3200:3232]), inc="svv")
        P.wait("tensor", "svv", t_vsf_prev)
        tt_tr = P.emit("tensor", lambda e: e.transpose(
            psA[0:32, 3072:3200], vsfm[:, 0:32], ident_sb[:, :]), inc="stt")
        t_vstr_prev = tt_tr
        P.wait("vector", "stt", tt_tr)
        if t_suffix_last:
            P.wait("vector", "stt", t_suffix_last)
        P.wait("vector", "svv", t_vscnt)
        t_vs_copy = P.emit("vector", lambda e: e.tensor_copy(vs_sm[:, 0:128], psA[0:32, 3072:3200]), inc="svv")

        P.waitall("tensor", KQ[bufp])
        P.waitall("tensor", "vv")
        if k == 0:
            P.wait("tensor", "svv", t_vones)
        if k == 1:
            P.wait("tensor", "svv", t_vones2)

        def emit_pvs(grp, w, exp_tgt, sel_tgt):
            P.wait("tensor", "saa", exp_tgt)
            if sel_tgt:
                P.wait("tensor", "svv", sel_tgt)
            if grp[0] == 0 and t_oevac:
                P.wait("tensor", "svv", t_oevac)
            for j in grp:
                for i in range(max(j, 4 * w), 4 * w + 4):
                    P.emit("tensor", lambda e, j=j, i=i, w=w, vb=vb: e.matmul(
                        psA[:, 512 * (i % 4):512 * (i % 4) + 129],
                        P_sb[:, 512 * (j % 4) + 128 * (i - 4 * w):512 * (j % 4) + 128 * (i - 4 * w) + 128],
                        vb[:, 144 * j:144 * j + 129],
                        start=(j == 0), stop=False),
                        inc="stt" if (j == grp[-1] and i == 4 * w + 3) else None)
            for j in grp:
                last_pv_pb[j % 4] = P.cnt["stt"]

        for w in range(8):
            groups = []
            jj = 0
            while jj + 1 <= 4 * w - 1:
                groups.append((jj, jj + 1))
                jj += 2
            while jj <= 4 * w + 3:
                groups.append((jj,))
                jj += 1
            pend = None
            for grp in groups:
                tt_qk = 0
                for j in grp:
                    par = j % 2
                    c0 = max(128 * j, 512 * w)
                    N = 512 * (w + 1) - c0
                    P.wait("tensor", "saa", last_exp_par[par])
                    tt_qk = P.emit("tensor", lambda e, par=par, bufp=bufp, j=j, c0=c0, N=N: e.matmul(
                        psA[:, 512 * (4 + par):512 * (4 + par) + N],
                        K_sb[:, 4096 * bufp + 128 * j:4096 * bufp + 128 * (j + 1)],
                        Q_sb[:, 4096 * bufp + c0:4096 * bufp + c0 + N],
                        start=True, stop=True), inc="stt")
                if pend is not None:
                    emit_pvs(*pend)
                if first_P_use[0]:
                    P.wait("scalar", "svv", t_tern_lastadd)
                    for b in range(8):
                        P.waitall("scalar", TS[b])
                    first_P_use[0] = False
                P.wait("scalar", "stt", tt_qk)
                for j in grp:
                    P.wait("scalar", "stt", last_pv_pb[j % 4])
                if len(grp) == 2:
                    jA = grp[0]
                    t_exp = P.emit("scalar", lambda e, jA=jA: e.activation(
                        P_sb[:, 512 * (jA % 4):512 * (jA % 4) + 1024],
                        psA[:, 2048:3072], AF.Exp, scale=INV64), inc="saa")
                    last_exp_par[0] = t_exp
                    last_exp_par[1] = t_exp
                    sel_tgt = 0
                else:
                    j = grp[0]
                    par = j % 2
                    c0 = max(128 * j, 512 * w)
                    N = 512 * (w + 1) - c0
                    pcol0 = c0 - 512 * w
                    t_exp = P.emit("scalar", lambda e, j=j, par=par, pcol0=pcol0, N=N: e.activation(
                        P_sb[:, 512 * (j % 4) + pcol0:512 * (j % 4) + pcol0 + N],
                        psA[:, 512 * (4 + par):512 * (4 + par) + N], AF.Exp, scale=INV64), inc="saa")
                    last_exp_par[par] = t_exp
                    sel_tgt = 0
                    if j >= 4 * w:
                        P.wait("vector", "saa", t_exp)
                        sel_tgt = P.emit("vector", lambda e, j=j, pcol0=pcol0: e.copy_predicated(
                            P_sb[:, 512 * (j % 4) + pcol0:512 * (j % 4) + pcol0 + 128],
                            itril_bf[:, :], ones2d[:, :]), inc="svv")
                pend = (grp, w, t_exp, sel_tgt)
            emit_pvs(*pend)
            P.wait("tensor", "svv", t_vs_copy)
            if k == 0 and w == 0:
                P.waitall("tensor", "umx")
                P.wait("tensor", "svv", t_ones2d)
                P.wait("tensor", "svv", t_trilbf)
            for i in range(4 * w, 4 * w + 4):
                P.emit("tensor", lambda e, i=i: e.matmul(
                    psA[:, 512 * (i % 4):512 * (i % 4) + 129],
                    um_sb[:, 128 * i:128 * (i + 1)], vs_sm[:, :],
                    start=False, stop=True), inc="stt" if i == 4 * w + 3 else None)
            t_wave = P.cnt["stt"]
            if w == 7:
                t_suffix_last = t_wave
            ab = w % 2
            P.wait("vector", "stt", t_wave)
            if t_oevac:
                P.wait("vector", "svv", t_oevac)
            for u in range(4):
                P.emit("vector", lambda e, ab=ab, u=u: e.reciprocal(
                    rc_sb[:, 4 * ab + u:4 * ab + u + 1], psA[:, 512 * u + 128:512 * u + 129]),
                    inc="svv" if u == 3 else None)
            P.wait("vector", "svv", P.cnt["svv"])
            P.waitall("vector", ATTK[k][ab])
            if w < 2 and k >= 1:
                P.waitall("vector", ATTK[k - 1][ab])
            for u in range(4):
                P.emit("vector", lambda e, ab=ab, u=u: e.tensor_scalar(
                    att_sb[:, 512 * ab + 128 * u:512 * ab + 128 * (u + 1)],
                    psA[:, 512 * u:512 * u + 128], rc_sb[:, 4 * ab + u:4 * ab + u + 1],
                    None, ALU.mult), inc="svv" if u == 3 else None)
            t_oevac = P.cnt["svv"]
            P.wait("sync", "svv", t_oevac)
            for u in range(4):
                i = 4 * w + u
                P.dma("sync", ATTK[k][ab], g2i[k, i // 8][128 * (i % 8):128 * (i % 8 + 1), :],
                      att_sb[:, 512 * ab + 128 * u:512 * ab + 128 * (u + 1)])
            t_attdma_unused = 0
        head_done[k] = P.cnt["stt"]

    t_ag2k = []
    for k in range(4):
        P.waitall("gpsimd", ATTK[k][0])
        P.waitall("gpsimd", ATTK[k][1])
        t_ag2k.append(P.emit("gpsimd", lambda e, k=k: e.collective_compute(
            "AllGather", ALU.bypass, replica_groups=G4,
            ins=[g2i[k].opt()], outs=[g2o[k].opt()]), inc="scc"))

    def go_rs2(e):
        with e.register("ro2") as ro2:
            for k in range(4):
                e.wait_ge(P.sems["scc"], t_ag2k[k])
                e.reg_load(ro2, ms_sb[0:1, 0:1])
                e.reg_mul(ro2, ro2, 131072)
                if k:
                    e.reg_add(ro2, ro2, k * 2097152)
                e.dma_start(out=bass.AP(rs2, k * 524288, [[131072, 4], [1, 131072]]),
                            in_=bass.AP(g2o, ro2, [[524288, 4], [1, 131072]])
                            ).then_inc(P.sems["rs2s"], 16)
    P.cnt["rs2s"] += 4 * 16
    P.on("gpsimd", go_rs2)
    for b in range(8):
        P.waitall("gpsimd", TW[b])
    t_agwf = P.emit("gpsimd", lambda e: e.collective_compute(
        "AllGather", ALU.bypass, replica_groups=G8,
        ins=[q2fT_i.ap().opt()], outs=[q2fT_o.ap().opt()]), inc="scc")
    t_agwp = P.emit("gpsimd", lambda e: e.collective_compute(
        "AllGather", ALU.bypass, replica_groups=G8,
        ins=[q2pT_i.ap().opt()], outs=[q2pT_o.ap().opt()]), inc="scc")
    P.wait("gpsimd", "stt", head_done[3])
    wcast(wo_d)

    if stage == 2:
        if "t_rs2" in T:
            P.waitall("gpsimd", "rs2s")
            P.dma("gpsimd", "tapA", bass.AP(T["t_rs2"], 0, [[1024, 2048], [1, 1024]]),
                  bass.AP(rs2, 0, [[1024, 2048], [1, 1024]]))
            P.waitall("gpsimd", "tapA")
        if "t_scal" in T:
            P.wait("sync", "svv", tv_scal)
            P.dma("sync", "tapB", T["t_scal"][:, :], scal_sb[:, :])
            P.waitall("sync", "tapB")
        P.dma("sync", "yme", bass.AP(y, 0, [[2048, 128], [262144, 8], [1, 2048]]),
              bass.AP(x_res, 0, [[16384, 128], [2048, 8], [1, 2048]]))
        P.waitall("sync", "yme")
        return finish(nc, P)

    P.waitall("sync", "rs2s")
    for g in range(16):
        P.dma("sync", "fmao", xn_fm[:, 1024 * g:1024 * (g + 1)], rs2[g % 4, g // 4], transpose=True)

    t_wo_ev = {}
    t_womm = 0
    P.waitall("tensor", "wc")
    P.waitall("tensor", "fmao")
    ap2_prev = [0, 0]
    for tb in range(8):
        b0 = 4 * (tb % 2)
        if tb >= 2:
            P.wait("tensor", "svv", t_wo_ev[tb - 2])
        for ic in range(16):
            for og in range(4):
                P.emit("tensor", lambda e, b0=b0, ic=ic, og=og, tb=tb: e.matmul(
                    psA[:, 512 * (b0 + og):512 * (b0 + og + 1)],
                    xn_fm[:, 1024 * ic + 128 * tb:1024 * ic + 128 * (tb + 1)],
                    w_sb[:, 2048 * ic + 512 * og:2048 * ic + 512 * (og + 1)],
                    start=(ic == 0), stop=(ic == 15)),
                    inc="stt" if (ic == 15 and og == 3) else None)
        t_womm = P.cnt["stt"]
        P.wait("vector", "stt", t_womm)
        for og in range(4):
            P.emit("vector", lambda e, b0=b0, og=og, tb=tb: e.tensor_tensor(
                x_res[:, 2048 * tb + 512 * og:2048 * tb + 512 * (og + 1)],
                psA[:, 512 * (b0 + og):512 * (b0 + og + 1)],
                x_res[:, 2048 * tb + 512 * og:2048 * tb + 512 * (og + 1)], ALU.add),
                inc="svv" if og == 3 else None)
        t_wo_ev[tb] = P.cnt["svv"]
        ln_block(tb, (lambda tb=tb: P.wait("vector", "svv", t_wo_ev[tb])), "x2t", x2t, ap2_prev)

    P.waitall("sync", "x2t")
    P.wait("sync", "stt", t_womm)
    for ic in range(16):
        P.dma("sync", "fmx2", xn_fm[:, 1024 * ic:1024 * (ic + 1)],
              x2t[:, 128 * ic:128 * (ic + 1)], transpose=True)

    if stage == 3:
        P.wait("sync", "svv", t_wo_ev[7])
        P.dma("sync", "yme", bass.AP(y, 0, [[2048, 128], [262144, 8], [1, 2048]]),
              bass.AP(x_res, 0, [[16384, 128], [2048, 8], [1, 2048]]))
        P.waitall("sync", "yme")
        return finish(nc, P)

    FC = ("fcA", "fcB")
    t_fc_mm = {}
    t_sq = {}
    t_relu = {}
    P.wait("sync", "scc", t_agwf)
    for hc in range(40):
        slot = hc % 2
        r, l = hc // 5, hc % 5
        if hc >= 2:
            P.wait("sync", "stt", t_fc_mm[hc - 2])
        P.dma("sync", FC[slot], fcw_sb[:, 2048 * slot:2048 * (slot + 1)],
              bass.AP(q2fT_o, (r * 5 + l) * 262144, [[2048, 128], [1, 2048]]))
        P.waitall("tensor", FC[slot])
        P.waitall("tensor", "fmx2")
        if hc >= 2:
            P.wait("tensor", "saa", t_relu[hc - 2])
        pr = hc % 2
        for ic in range(16):
            for hf in range(2):
                P.emit("tensor", lambda e, pr=pr, hf=hf, ic=ic, slot=slot: e.matmul(
                    psA[:, 512 * (2 * pr + hf):512 * (2 * pr + hf + 1)],
                    fcw_sb[:, 2048 * slot + 128 * ic:2048 * slot + 128 * (ic + 1)],
                    xn_fm[:, 1024 * ic + 512 * hf:1024 * ic + 512 * (hf + 1)],
                    start=(ic == 0), stop=(ic == 15)),
                    inc="stt" if (ic == 15 and hf == 1) else None)
        t_fc_mm[hc] = P.cnt["stt"]
        P.wait("scalar", "stt", t_fc_mm[hc])
        if hc >= 2:
            P.wait("scalar", "svv", t_sq[hc - 2])
        for hf in range(2):
            P.emit("scalar", lambda e, pr=pr, hf=hf, slot=slot: e.activation(
                rl_sb[:, 1024 * slot + 512 * hf:1024 * slot + 512 * (hf + 1)],
                psA[:, 512 * (2 * pr + hf):512 * (2 * pr + hf + 1)],
                AF.Relu, scale=scal_sb[:, C_SFC:C_SFC + 1]),
                inc="saa" if hf == 1 else None)
        t_relu[hc] = P.cnt["saa"]
        P.wait("vector", "saa", t_relu[hc])
        t_sq[hc] = P.emit("vector", lambda e, hc=hc, slot=slot: e.tensor_tensor(
            h_sb[:, 1024 * hc:1024 * (hc + 1)],
            rl_sb[:, 1024 * slot:1024 * (slot + 1)],
            rl_sb[:, 1024 * slot:1024 * (slot + 1)], ALU.mult), inc="svv")
    t_h_done = t_sq[39]

    PR = ("prA", "prB")
    OB = ("obA", "obB")
    P.wait("sync", "scc", t_agwp)
    P.wait("sync", "svv", t_h_done)
    t_quad_ev = 0
    t_prw_mm = {}
    for th in range(2):
        for dh2 in range(2):
            qd = 2 * th + dh2
            for hc in range(40):
                slot = hc % 2
                if hc >= 2:
                    P.wait("sync", "stt", t_prw_mm[(qd, hc - 2)])
                elif qd > 0:
                    P.wait("sync", "stt", t_prw_mm[(qd - 1, 38 + hc)])
                P.dma("sync", PR[slot], bass.AP(prw_sb, 1024 * slot, [[2048, 128], [256, 4], [1, 256]]),
                      bass.AP(q2pT_o, 4 * dh2 * 1310720 + 128 * hc * 256, [[256, 128], [1310720, 4], [1, 256]]))
                P.waitall("tensor", PR[slot])
                if hc == 0:
                    P.wait("tensor", "svv", t_h_done)
                    if t_quad_ev:
                        P.wait("tensor", "svv", t_quad_ev)
                for tb in range(4):
                    tbg = 4 * th + tb
                    for bd in range(2):
                        P.emit("tensor", lambda e, tb=tb, bd=bd, hc=hc, slot=slot, tbg=tbg: e.matmul(
                            psA[:, 512 * (2 * tb + bd):512 * (2 * tb + bd + 1)],
                            h_sb[:, 1024 * hc + 128 * tbg:1024 * hc + 128 * (tbg + 1)],
                            prw_sb[:, 1024 * slot + 512 * bd:1024 * slot + 512 * (bd + 1)],
                            start=(hc == 0), stop=(hc == 39)),
                            inc="stt" if (tb == 3 and bd == 1) else None)
                t_prw_mm[(qd, hc)] = P.cnt["stt"]
            P.wait("vector", "stt", t_prw_mm[(qd, 39)])
            for bi in range(8):
                tb, bd = bi // 2, bi % 2
                tbg = 4 * th + tb
                obs = bi % 2
                P.waitall("vector", OB[obs])
                tq = P.emit("vector", lambda e, bi=bi, tbg=tbg, bd=bd, dh2=dh2, obs=obs: e.scalar_tensor_tensor(
                    ob_sb[:, 512 * obs:512 * (obs + 1)],
                    psA[:, 512 * bi:512 * (bi + 1)],
                    scal_sb[:, C_SPR:C_SPR + 1],
                    x_res[:, 2048 * tbg + 1024 * dh2 + 512 * bd:2048 * tbg + 1024 * dh2 + 512 * (bd + 1)],
                    ALU.mult, ALU.add), inc="svv")
                P.wait("sync", "svv", tq)
                P.dma("sync", OB[obs], y[128 * tbg:128 * (tbg + 1),
                                         1024 * dh2 + 512 * bd:1024 * dh2 + 512 * (bd + 1)],
                      ob_sb[:, 512 * obs:512 * (obs + 1)])
            t_quad_ev = P.cnt["svv"]

    P.waitall("sync", "obA")
    P.waitall("sync", "obB")
    return finish(nc, P)


def finish(nc, P):
    with nc.Block() as block:
        for eng in ENGINES:
            fns = P.streams[eng]

            def runner(e, fns=fns):
                for f in fns:
                    f(e)
            getattr(block, eng)(runner)
    assert nc.sbuf_base <= 17408, f"bass bump allocator grew into overlay: {nc.sbuf_base}"
    return nc


def make_consts():
    s = np.arange(128)[:, None]
    q = np.arange(128)[None, :]
    tril_T = (s <= q).astype(np.float32)
    j = np.arange(32)[:, None]
    i = (np.arange(4096)[None, :] // 128)
    um = (j > i).astype(np.float32)
    return tril_T, um, np.eye(128, dtype=np.float32)


def make_in_maps(inputs):
    x = np.asarray(inputs["L0_attn_x"])
    wq = np.ascontiguousarray(np.asarray(inputs["L0_attn_Wq"], dtype=np.float32))
    wk = np.ascontiguousarray(np.asarray(inputs["L0_attn_Wk"], dtype=np.float32))
    wv = np.ascontiguousarray(np.asarray(inputs["L0_attn_Wv"], dtype=np.float32))
    wo = np.ascontiguousarray(np.asarray(inputs["L0_attn_Wo"], dtype=np.float32))
    wfc = np.asarray(inputs["L0_ffn_w_fc"], dtype=np.float32)
    wpr = np.asarray(inputs["L0_ffn_w_proj"], dtype=np.float32)
    tril_T, um, ident = make_consts()
    maps = []
    for c in range(8):
        r = c % 4
        b = c // 4
        maps.append({
            "x_q": np.ascontiguousarray(x[b, 1024 * r:1024 * (r + 1), :], dtype=np.float32),
            "wq": wq, "wk": wk, "wv": wv, "wo": wo,
            "wfc_sl": np.ascontiguousarray(wfc[640 * c:640 * (c + 1), :]),
            "wpr_sl": np.ascontiguousarray(wpr[256 * c:256 * (c + 1), :]),
            "meta": np.array([[r]], dtype=np.uint32),
            "tril": tril_T, "umask": um, "ident": ident,
        })
    return maps


from concourse.bass_utils import run_bass_kernel_spmd

_cache = {}


def kernel(**inputs):
    if "nc" not in _cache:
        _cache["nc"] = build(stage=4, taps=())
    nc = _cache["nc"]
    maps = make_in_maps(inputs)
    res = run_bass_kernel_spmd(nc, maps, core_ids=list(range(8)))
    out = np.empty((2, 4096, 2048), np.float32)
    for c in range(8):
        r = c % 4
        out[c // 4, 1024 * r:1024 * (r + 1), :] = np.asarray(res.results[c]["y"]).reshape(1024, 2048)
    return out


# version 14
# speedup vs baseline: 1.0727x; 1.0213x over previous
import numpy as np
import ml_dtypes
import concourse.bass as bass
import concourse.mybir as mybir

F32 = mybir.dt.float32
BF16 = mybir.dt.bfloat16
U32 = mybir.dt.uint32
AF = mybir.ActivationFunctionType
ALU = mybir.AluOpType
AX = mybir.AxisListType.X
G4 = [[0, 1, 2, 3], [4, 5, 6, 7]]
G8 = [list(range(8))]
ENGINES = ("sync", "gpsimd", "tensor", "vector", "scalar")

SEMS = ("cn", "umx", "am2", "ami", "xA", "xB", "wc", "xnt", "x2t", "fmx", "fmao", "fmx2",
        "sl0", "sl1", "sl2", "sl3", "evA", "evB", "evC", "evD",
        "ts0", "ts1", "ts2", "ts3", "ts4", "ts5", "ts6", "ts7",
        "tw0", "tw1", "tw2", "tw3", "tw4", "tw5", "tw6", "tw7",
        "kqA", "kqB", "vv",
        "at0A", "at0B", "at1A", "at1B", "at2A", "at2B", "at3A", "at3B",
        "rsA", "rsB", "rsC", "rsD", "rs2s", "fcA", "fcB", "prA", "prB", "obA", "obB",
        "tapA", "tapB", "yme",
        "scc", "stt", "svv", "saa")


class Prog:
    def __init__(self, nc):
        self.nc = nc
        self.streams = {k: [] for k in ENGINES}
        self.sems = {}
        self.cnt = {}
        for s in SEMS:
            self.sems[s] = nc.semaphore(s).__enter__()
            self.cnt[s] = 0

    def on(self, eng, fn):
        self.streams[eng].append(fn)

    def wait(self, eng, sem, tgt):
        assert tgt <= self.cnt[sem], f"wait on future {sem}:{tgt}>{self.cnt[sem]}"
        if tgt <= 0:
            return
        s = self.sems[sem]
        self.on(eng, lambda e, s=s, t=tgt: e.wait_ge(s, t))

    def waitall(self, eng, sem):
        self.wait(eng, sem, self.cnt[sem])

    def emit(self, eng, fn, inc=None, k=1):
        if inc is None:
            self.on(eng, fn)
            return None
        self.cnt[inc] += k
        tgt = self.cnt[inc]
        s = self.sems[inc]
        self.on(eng, lambda e, f=fn, s=s, kk=k: f(e).then_inc(s, kk))
        return tgt

    def dma(self, eng, sem, out, in_, **kw):
        return self.emit(eng, lambda e, o=out, i=in_, kws=dict(kw): e.dma_start(out=o, in_=i, **kws),
                         inc=sem, k=16)


def build(stage=4, taps=()):
    nc = bass.Bass("TRN2", target_bir_lowering=False, debug=False)
    P = Prog(nc)

    x_q = nc.dram_tensor("x_q", [1024, 2048], F32, kind="ExternalInput")
    wq_d = nc.dram_tensor("wq", [2048, 2048], F32, kind="ExternalInput")
    wk_d = nc.dram_tensor("wk", [2048, 2048], F32, kind="ExternalInput")
    wv_d = nc.dram_tensor("wv", [2048, 2048], F32, kind="ExternalInput")
    wo_d = nc.dram_tensor("wo", [2048, 2048], F32, kind="ExternalInput")
    wfc_sl = nc.dram_tensor("wfc_sl", [640, 2048], F32, kind="ExternalInput")
    wpr_sl = nc.dram_tensor("wpr_sl", [256, 5120], F32, kind="ExternalInput")
    meta = nc.dram_tensor("meta", [1, 1], U32, kind="ExternalInput")
    tril = nc.dram_tensor("tril", [128, 128], F32, kind="ExternalInput")
    umask = nc.dram_tensor("umask", [32, 4096], F32, kind="ExternalInput")
    ident = nc.dram_tensor("ident", [128, 128], F32, kind="ExternalInput")
    y = nc.dram_tensor("y", [1024, 2048], F32, kind="ExternalOutput")

    g1i = nc.dram_tensor("g1i", [4, 4, 3, 128, 1024], BF16)
    g1o = nc.dram_tensor("g1o", [4, 4, 4, 3, 128, 1024], BF16)
    rsV = nc.dram_tensor("rsV", [4, 4, 8, 128, 128], BF16)
    xnt = nc.dram_tensor("xnt", [1024, 2048], BF16)
    x2t = nc.dram_tensor("x2t", [1024, 2048], BF16)
    g2i = nc.dram_tensor("g2i", [4, 4, 1024, 128], BF16)
    g2o = nc.dram_tensor("g2o", [4, 4, 4, 1024, 128], BF16)
    rs2 = nc.dram_tensor("rs2", [4, 4, 1024, 128], BF16)
    q2fT_i = nc.dram_tensor("q2fT_i", [5, 128, 16, 128], BF16)
    q2fT_o = nc.dram_tensor("q2fT_o", [8, 5, 128, 16, 128], BF16, addr_space="Shared")
    q2pT_i = nc.dram_tensor("q2pT_i", [5120, 256], BF16)
    q2pT_o = nc.dram_tensor("q2pT_o", [8, 5120, 256], BF16, addr_space="Shared")
    ami_d = nc.dram_tensor("ami", [2, 1], F32)
    amo_d = nc.dram_tensor("amo", [2, 1], F32)

    T = {}
    if "rs1" in taps:
        T["t_rs1"] = nc.dram_tensor("t_rs1", [4, 4, 8, 128, 128], F32, kind="ExternalOutput")
    if "rs2" in taps:
        T["t_rs2"] = nc.dram_tensor("t_rs2", [4, 4, 1024, 128], F32, kind="ExternalOutput")
    if "scal" in taps:
        T["t_scal"] = nc.dram_tensor("t_scal", [128, 8], F32, kind="ExternalOutput")

    SHIFT = 17408

    def A(name, shape, dt, offset):
        return nc.alloc_sbuf_tensor_at(name, shape, dt, offset=offset + SHIFT)
    x_res = A("x_res", [128, 16384], F32, offset=0)
    xn_fm = A("xn_fm", [128, 16384], BF16, offset=65536)
    K_sb = A("K_sb", [128, 8192], BF16, offset=65536)
    Q_sb = A("Q_sb", [128, 8192], BF16, offset=65536 + 16384)
    w_sb = A("w_sb", [128, 32768], BF16, offset=98304)
    h_sb = A("h_sb", [128, 40960], BF16, offset=98304)
    V_sb = A("V_sb", [128, 4608], BF16, offset=163840)
    xn_stage = A("xn_stage", [128, 2048], BF16, offset=173056)
    ev_sb = A("ev_sb", [128, 2048], BF16, offset=173056)
    ln_s = A("ln_s", [128, 64], F32, offset=179200)
    wslab = A("wslab", [128, 4096], F32, offset=180224)
    um_sb = A("um_sb", [32, 4096], F32, offset=180224)
    rl_sb = A("rl_sb", [128, 2048], F32, offset=180224)
    prw_sb = A("prw_sb", [128, 2048], BF16, offset=180224)
    ob_sb = A("ob_sb", [128, 1024], F32, offset=184320)
    Vn_sb = A("Vn_sb", [128, 4608], BF16, offset=98304)
    fcw_sb = A("fcw_sb", [128, 4096], BF16, offset=188416)
    P_sb = A("P_sb", [128, 2048], BF16, offset=196608)
    qa_sb = A("qa_sb", [128, 1024], BF16, offset=196608)
    qb_sb = A("qb_sb", [128, 1024], BF16, offset=198656)
    qb1_sb = A("qb1_sb", [128, 1024], BF16, offset=200704)
    att_sb = A("att_sb", [128, 1024], BF16, offset=202752)
    rc_sb = A("rc_sb", [128, 8], F32, offset=204800)
    vs_sm = A("vs_sm", [32, 129], F32, offset=204832)
    vsfm = A("vsfm", [128, 64], F32, offset=205376)
    tril_sb = A("tril_sb", [128, 128], F32, offset=206848)
    ident_sb = A("ident_sb", [128, 128], F32, offset=207360)
    scal_sb = A("scal_sb", [128, 8], F32, offset=207872)
    amred = A("amred", [128, 24], F32, offset=207904)
    vals_sb = A("vals_sb", [1, 16], F32, offset=208000)
    am2_sb = A("am2_sb", [1, 4], F32, offset=208064)
    ones_r = A("ones_r", [1, 128], F32, offset=208096)
    ones_c = A("ones_c", [128, 1], F32, offset=208640)
    ms_sb = A("ms_sb", [1, 1], U32, offset=208672)
    tsp_sb = A("tsp_sb", [128, 1280], BF16, offset=208704)
    onesb = A("onesb", [128, 1], BF16, offset=211648)
    ones2d = A("ones2d", [128, 128], BF16, offset=211264)
    itril_bf = A("itril_bf", [128, 128], mybir.dt.uint8, offset=211520)

    psA = nc.alloc_psum_tensor("psA", [128, 4096], F32)

    C_THRF, C_THRP, C_SFC, C_SPR, C_NTHRF, C_NTHRP = 2, 3, 4, 5, 6, 7
    INV64 = 1.0 / 64.0
    SL = ("sl0", "sl1", "sl2", "sl3")

    P.dma("sync", "cn", tril_sb[:, :], tril[:, :])
    P.dma("sync", "cn", ident_sb[:, :], ident[:, :])
    P.dma("gpsimd", "tapA", ms_sb[:, :], meta[:, :])
    P.emit("vector", lambda e: e.memset(ones_r[:, :], 1.0))
    P.emit("vector", lambda e: e.memset(ones_c[:, :], 1.0))
    t_vones = P.emit("vector", lambda e: e.memset(
        bass.AP(V_sb, 128, [[4608, 128], [144, 32]]), 1.0), inc="svv")
    t_vscnt = P.emit("vector", lambda e: e.memset(vs_sm[:, 128:129], 128.0), inc="svv")
    t_onesb = P.emit("vector", lambda e: e.memset(onesb[:, :], 1.0), inc="svv")

    P.dma("sync", "xA", bass.AP(x_res, 0, [[16384, 128], [2048, 4], [1, 2048]]),
          bass.AP(x_q, 0, [[2048, 128], [262144, 4], [1, 2048]]))
    P.dma("sync", "xB", bass.AP(x_res, 8192, [[16384, 128], [2048, 4], [1, 2048]]),
          bass.AP(x_q, 1048576, [[2048, 128], [262144, 4], [1, 2048]]))
    t_ones2d = P.emit("vector", lambda e: e.memset(ones2d[:, :], 1.0), inc="svv")

    def wcast(dram):
        return P.dma("gpsimd", "wc",
                     bass.AP(w_sb, 0, [[32768, 128], [2048, 16], [1, 2048]]),
                     bass.AP(dram, 0, [[2048, 128], [262144, 16], [1, 2048]]))
    def wcast_v(dram):
        for s in range(4):
            for d in range(4):
                P.dma("gpsimd", "wc",
                      bass.AP(w_sb, 512 * s + 128 * d, [[32768, 128], [2048, 16], [1, 128]]),
                      bass.AP(dram, (4 * d + s) * 128, [[2048, 128], [262144, 16], [1, 128]]))
    wcast(wq_d)

    def ln_block(tb, pre_wait, stage_sem, out_dram, ap_prev):
        sl = 32 * (tb % 2)
        pre_wait()
        if ap_prev[tb % 2]:
            P.wait("vector", "svv", ap_prev[tb % 2])
        for s4 in range(4):
            P.emit("vector", lambda e, tb=tb, sl=sl, s4=s4: e.bn_stats(
                ln_s[:, sl + 6 * s4: sl + 6 * s4 + 6],
                x_res[:, 2048 * tb + 512 * s4: 2048 * tb + 512 * (s4 + 1)]),
                inc="svv" if s4 == 3 else None)
        P.wait("vector", "svv", P.cnt["svv"])
        t2 = P.emit("vector", lambda e, sl=sl: e.bn_aggr(ln_s[:, sl + 24:sl + 26], ln_s[:, sl:sl + 24]), inc="svv")
        P.wait("vector", "svv", t2)
        t3 = P.emit("vector", lambda e, sl=sl: e.tensor_scalar(
            ln_s[:, sl + 26:sl + 27], ln_s[:, sl + 25:sl + 26], 1e-5, None, ALU.add), inc="svv")
        P.wait("vector", "svv", t3)
        t4 = P.emit("vector", lambda e, sl=sl: e.reciprocal(ln_s[:, sl + 27:sl + 28], ln_s[:, sl + 26:sl + 27]), inc="svv")
        P.wait("scalar", "svv", t4)
        t5 = P.emit("scalar", lambda e, sl=sl: e.activation(
            ln_s[:, sl + 28:sl + 29], ln_s[:, sl + 27:sl + 28], AF.Sqrt), inc="saa")
        P.wait("vector", "saa", t5)
        P.waitall("vector", stage_sem)
        t_ap = P.emit("vector", lambda e, tb=tb, sl=sl: e.tensor_scalar(
            xn_stage[:, :], x_res[:, 2048 * tb:2048 * (tb + 1)],
            ln_s[:, sl + 24:sl + 25], ln_s[:, sl + 28:sl + 29], ALU.subtract, ALU.mult), inc="svv")
        ap_prev[tb % 2] = t_ap
        P.wait("sync", "svv", t_ap)
        P.dma("sync", stage_sem, out_dram[128 * tb:128 * (tb + 1), :], xn_stage[:, :])
        return t_ap

    P.waitall("vector", "cn")
    t_trilbf = P.emit("vector", lambda e: e.tensor_scalar(
        itril_bf[:, :], tril_sb[:, :], 0.0, None, ALU.is_equal), inc="svv")
    ap_prev = [0, 0]
    for tb in range(8):
        ln_block(tb, (lambda tb=tb: P.waitall("vector", "xA" if tb < 4 else "xB")), "xnt", xnt, ap_prev)
    P.waitall("sync", "xnt")
    for ic in range(16):
        P.dma("sync", "fmx", xn_fm[:, 1024 * ic:1024 * (ic + 1)],
              xnt[:, 128 * ic:128 * (ic + 1)], transpose=True)

    pieces = [(wfc_sl, i, cp) for i in range(5) for cp in range(2)] + \
             [(wpr_sl, i, cp) for i in range(2) for cp in range(5)]
    t_red = [0] * 20
    for n, (src_d, i, cp) in enumerate(pieces):
        slot = n % 4
        if n >= 4:
            P.wait("scalar", "svv", t_red[n - 4])
        P.dma("scalar", SL[slot], wslab[:, 1024 * slot:1024 * (slot + 1)],
              src_d[128 * i:128 * (i + 1), 1024 * cp:1024 * (cp + 1)])
        P.waitall("vector", SL[slot])
        t_red[n] = P.emit("vector", lambda e, n=n: e.tensor_reduce(
            amred[:, n:n + 1], wslab[:, 1024 * (n % 4):1024 * (n % 4 + 1)],
            AX, ALU.add, apply_absolute_value=True), inc="svv")
    P.wait("vector", "svv", t_red[19])
    P.emit("vector", lambda e: e.tensor_reduce(amred[:, 20:21], amred[:, 0:10], AX, ALU.add))
    tv2 = P.emit("vector", lambda e: e.tensor_reduce(amred[:, 21:22], amred[:, 10:20], AX, ALU.add), inc="svv")
    P.wait("tensor", "svv", tv2)
    tt_am = P.emit("tensor", lambda e: e.matmul(psA[0:2, 3584:3585], amred[:, 20:22], ones_c[:, :],
                                                start=True, stop=True), inc="stt")
    P.wait("vector", "stt", tt_am)
    tv3 = P.emit("vector", lambda e: e.tensor_copy(amred[0:2, 22:23], psA[0:2, 3584:3585]), inc="svv")
    P.wait("sync", "svv", tv3)
    P.dma("sync", "ami", ami_d[:, :], amred[0:2, 22:23])
    P.waitall("gpsimd", "ami")
    t_ar = P.emit("gpsimd", lambda e: e.collective_compute(
        "AllReduce", ALU.add, replica_groups=G8,
        ins=[ami_d.ap().opt()], outs=[amo_d.ap().opt()]), inc="scc")
    t_qkv_mm = {}
    last_pair_ev = {}
    ev_sem4 = ("evA", "evB", "evC", "evD")
    t_ag1k = [0, 0, 0, 0]
    t_sign = [0] * 20
    t_tadd = [0] * 20
    TS = ("ts0", "ts1", "ts2", "ts3", "ts4", "ts5", "ts6", "ts7")
    TW = ("tw0", "tw1", "tw2", "tw3", "tw4", "tw5", "tw6", "tw7")
    ts_after = {}

    def tern_piece(n):
        src_d, i, cp = pieces[n]
        slot = n % 4
        if n < 4:
            P.wait("scalar", "svv", t_red[16 + n])
        else:
            P.wait("scalar", "saa", t_sign[n - 4])
        P.dma("scalar", SL[slot], wslab[:, 1024 * slot:1024 * (slot + 1)],
              src_d[128 * i:128 * (i + 1), 1024 * cp:1024 * (cp + 1)])
        P.waitall("scalar", SL[slot])
        qb = qb_sb if n % 2 == 0 else qb1_sb
        if n == 0:
            P.wait("scalar", "svv", tv_scal)
        else:
            P.wait("scalar", "svv", t_tadd[n - 1])
        if n >= 2:
            for b in range(8):
                P.wait("scalar", TS[b], ts_after[n - 2][b])
        cT = C_THRF if src_d is wfc_sl else C_THRP
        cN = C_NTHRF if src_d is wfc_sl else C_NTHRP
        P.emit("scalar", lambda e, slot=slot, cT=cT: e.activation(
            qa_sb[:, :], wslab[:, 1024 * slot:1024 * (slot + 1)], AF.Sign, bias=scal_sb[:, cT:cT + 1]))
        t_sign[n] = P.emit("scalar", lambda e, slot=slot, cN=cN, qb=qb: e.activation(
            qb[:, :], wslab[:, 1024 * slot:1024 * (slot + 1)], AF.Sign, bias=scal_sb[:, cN:cN + 1]), inc="saa")
        P.wait("vector", "saa", t_sign[n])
        t_tadd[n] = P.emit("vector", lambda e, qb=qb: e.tensor_tensor(
            qb[:, :], qa_sb[:, :], qb[:, :], ALU.add), inc="svv")
        P.wait("scalar", "svv", t_tadd[n])
        for b in range(8):
            P.waitall("scalar", TW[b])
            P.dma("scalar", TS[b], tsp_sb[:, 128 * b:128 * (b + 1)],
                  qb[:, 128 * b:128 * (b + 1)], transpose=True)
        ts_after[n] = [P.cnt[TS[b]] for b in range(8)]
        for b in range(8):
            P.waitall("gpsimd", TS[b])
            if src_d is wfc_sl:
                dst = bass.AP(q2fT_i, i * 262144 + (8 * cp + b) * 128, [[2048, 128], [1, 128]])
            else:
                dst = bass.AP(q2pT_i, (8 * cp + b) * 32768 + 128 * i, [[256, 128], [1, 128]])
            P.dma("gpsimd", TW[b], dst, tsp_sb[:, 128 * b:128 * (b + 1)])

    for p in range(2):
        P.waitall("tensor", "wc")
        if p == 0:
            P.waitall("tensor", "fmx")
        for oc in range(16):
            it = 16 * p + oc
            pr = oc % 2
            if pr in last_pair_ev:
                P.wait("tensor", "svv", last_pair_ev[pr])
            for ic in range(16):
                for hf in range(2):
                    P.emit("tensor", lambda e, pr=pr, hf=hf, ic=ic, oc=oc: e.matmul(
                        psA[:, 512 * (2 * pr + hf):512 * (2 * pr + hf + 1)],
                        w_sb[:, 2048 * ic + 128 * oc:2048 * ic + 128 * (oc + 1)],
                        xn_fm[:, 1024 * ic + 512 * hf:1024 * ic + 512 * (hf + 1)],
                        start=(ic == 0), stop=(ic == 15)),
                        inc="stt" if (ic == 15 and hf == 1) else None)
            t_qkv_mm[(p, oc)] = P.cnt["stt"]
            evs = "evA" if it % 2 == 0 else "evB"
            ecol = 1024 * (it % 2)
            P.wait("vector", "stt", t_qkv_mm[(p, oc)])
            if it == 0:
                P.waitall("vector", "xnt")
            P.waitall("vector", evs)
            for hf in range(2):
                P.emit("vector", lambda e, pr=pr, hf=hf, ecol=ecol: e.tensor_copy(
                    ev_sb[:, ecol + 512 * hf:ecol + 512 * (hf + 1)],
                    psA[:, 512 * (2 * pr + hf):512 * (2 * pr + hf + 1)]),
                    inc="svv" if hf == 1 else None)
            t_ev = P.cnt["svv"]
            last_pair_ev[pr] = t_ev
            P.wait("sync", "svv", t_ev)
            P.dma("sync", evs, g1i[oc % 4, oc // 4, p], ev_sb[:, ecol:ecol + 1024])
            if p == 1:
                for n in range(20):
                    if n * 16 // 20 == oc:
                        tern_piece(n)
        P.wait("gpsimd", "stt", t_qkv_mm[(p, 15)])
        if p == 0:
            wcast(wk_d)
            P.wait("sync", "scc", t_ar)
            P.dma("sync", "am2", am2_sb[0:1, 0:2], amo_d[:, :])
            P.waitall("vector", "am2")
            u1 = P.emit("vector", lambda e: e.tensor_scalar(
                vals_sb[0:1, 0:2], am2_sb[0:1, 0:2], 1.0 / 10485760.0, 1e-5, ALU.mult, ALU.max), inc="svv")
            P.wait("vector", "svv", u1)
            P.emit("vector", lambda e: e.tensor_scalar(vals_sb[0:1, 2:4], vals_sb[0:1, 0:2], 0.7, None, ALU.mult))
            u3 = P.emit("vector", lambda e: e.tensor_scalar(vals_sb[0:1, 4:6], vals_sb[0:1, 0:2], 0.5, None, ALU.mult), inc="svv")
            P.wait("vector", "svv", u3)
            u4 = P.emit("vector", lambda e: e.tensor_scalar(vals_sb[0:1, 6:8], vals_sb[0:1, 2:4], -1.0, None, ALU.mult), inc="svv")
            P.wait("tensor", "svv", u4)
            P.wait("tensor", "svv", tv3)
            tt_bc = P.emit("tensor", lambda e: e.matmul(psA[:, 3584:3592], ones_r[:, :], vals_sb[0:1, 0:8],
                                                        start=True, stop=True), inc="stt")
            P.wait("vector", "stt", tt_bc)
            tv_scal = P.emit("vector", lambda e: e.tensor_copy(scal_sb[:, :], psA[:, 3584:3592]), inc="svv")
        else:
            wcast_v(wv_d)
    t_tern_lastadd = t_tadd[19]
    t_sign_last = t_sign[19]

    P.waitall("tensor", "wc")
    P.waitall("tensor", "ami")
    last_pair6 = {}
    t_last_ev = 0
    for u in range(32):
        s, tb = u // 8, u % 8
        pr6 = u % 6
        if pr6 in last_pair6:
            P.wait("tensor", "svv", last_pair6[pr6])
        for ic in range(16):
            P.emit("tensor", lambda e, pr6=pr6, ic=ic, s=s, tb=tb: e.matmul(
                psA[:, 512 * pr6:512 * (pr6 + 1)],
                xn_fm[:, 1024 * ic + 128 * tb:1024 * ic + 128 * (tb + 1)],
                w_sb[:, 2048 * ic + 512 * s:2048 * ic + 512 * (s + 1)],
                start=(ic == 0), stop=(ic == 15)),
                inc="stt" if ic == 15 else None)
        t_qkv_mm[(2, u)] = P.cnt["stt"]
        es = ev_sem4[u % 4]
        ecol = 512 * (u % 4)
        P.wait("vector", "stt", t_qkv_mm[(2, u)])
        if u < 4:
            P.waitall("vector", "evA")
            P.waitall("vector", "evB")
        P.waitall("vector", es)
        t_ev = P.emit("vector", lambda e, pr6=pr6, ecol=ecol: e.tensor_copy(
            ev_sb[:, ecol:ecol + 512], psA[:, 512 * pr6:512 * (pr6 + 1)]), inc="svv")
        last_pair6[pr6] = t_ev
        t_last_ev = t_ev
        P.wait("sync", "svv", t_ev)
        for d in range(4):
            P.dma("sync", es,
                  bass.AP(g1i, (4 * s + d) * 393216 + 262144 + tb * 16384, [[128, 128], [1, 128]]),
                  bass.AP(ev_sb, ecol + 128 * d, [[2048, 128], [1, 128]]))
        if tb == 7:
            for esem in ev_sem4:
                P.waitall("gpsimd", esem)
            t_ag1k[s] = P.emit("gpsimd", lambda e, s=s: e.collective_compute(
                "AllGather", ALU.bypass, replica_groups=G4,
                ins=[g1i[s].opt()], outs=[g1o[s].opt()]), inc="scc")
    t_qkv_mm_last = t_qkv_mm[(2, 31)]
    P.wait("vector", "stt", t_qkv_mm_last)
    t_vones2 = P.emit("vector", lambda e: e.memset(
        bass.AP(Vn_sb, 128, [[4608, 128], [144, 32]]), 1.0), inc="svv")

    P.waitall("gpsimd", "tapA")
    RSK = ("rsA", "rsB", "rsC", "rsD")

    def go_rs1(e):
        for k in range(4):
            e.wait_ge(P.sems["scc"], t_ag1k[k])
            with e.register(f"rV{k}") as rV:
                e.reg_load(rV, ms_sb[0:1, 0:1])
                e.reg_mul(rV, rV, 393216)
                e.reg_add(rV, rV, k * 6291456 + 262144)
                e.dma_start(out=bass.AP(rsV, k * 131072, [[524288, 4], [1, 131072]]),
                            in_=bass.AP(g1o, rV, [[1572864, 4], [1, 131072]])
                            ).then_inc(P.sems[RSK[k]], 16)
    for k in range(4):
        P.cnt[RSK[k]] += 16
    P.on("gpsimd", go_rs1)


    if stage >= 2:
        for b in range(8):
            P.waitall("gpsimd", TW[b])
        t_agwf = P.emit("gpsimd", lambda e: e.collective_compute(
            "AllGather", ALU.bypass, replica_groups=G8,
            ins=[q2fT_i.ap().opt()], outs=[q2fT_o.ap().opt()]), inc="scc")
        t_agwp = P.emit("gpsimd", lambda e: e.collective_compute(
            "AllGather", ALU.bypass, replica_groups=G8,
            ins=[q2pT_i.ap().opt()], outs=[q2pT_o.ap().opt()]), inc="scc")

    if stage == 1:
        if "t_rs1" in T:
            for k in range(4):
                P.waitall("gpsimd", RSK[k])
            P.dma("gpsimd", "tapA", bass.AP(T["t_rs1"], 0, [[1024, 2048], [1, 1024]]),
                  bass.AP(rsV, 0, [[1024, 2048], [1, 1024]]))
            P.waitall("gpsimd", "tapA")
        P.dma("sync", "yme", bass.AP(y, 0, [[2048, 128], [262144, 8], [1, 2048]]),
              bass.AP(x_res, 0, [[16384, 128], [2048, 8], [1, 2048]]))
        P.waitall("sync", "yme")
        return finish(nc, P)

    last_exp_par = [0, 0]
    last_pv_pb = [0, 0, 0, 0]
    last_diag_pb = [0, 0, 0, 0]
    t_oevac = 0
    head_done = [0, 0, 0, 0]
    t_vs_copy = 0
    t_suffix_last = 0
    t_vstr_prev = 0
    t_vsf_prev = 0
    first_P_use = [True]
    KQ = ("kqA", "kqB")
    ATTK = (("at0A", "at0B"), ("at1A", "at1B"), ("at2A", "at2B"), ("at3A", "at3B"))

    t_kq_done = {}
    P.waitall("sync", "tapA")
    P.wait("sync", "stt", t_qkv_mm_last)
    for k in range(4):
        bufp = k % 2
        P.wait("sync", "scc", t_ag1k[k])
        if k >= 2:
            P.wait("sync", "stt", head_done[k - 2])

        def kq_loads(e, k=k, bufp=bufp, which=1, dst=None):
            with e.register(f"rkq{k}{'KQ'[0 if which else 1]}") as ro:
                e.reg_load(ro, ms_sb[0:1, 0:1])
                e.reg_mul(ro, ro, 393216)
                e.reg_add(ro, ro, k * 6291456 + which * 131072)
                e.dma_start(out=bass.AP(dst, 4096 * bufp, [[8192, 128], [1024, 4], [1, 1024]]),
                            in_=bass.AP(g1o, ro, [[1024, 128], [1572864, 4], [1, 1024]])
                            ).then_inc(P.sems[KQ[bufp]], 16)
        P.cnt[KQ[bufp]] += 16
        P.on("sync", lambda e, k=k, bufp=bufp: kq_loads(e, k, bufp, 1, K_sb))
        P.cnt[KQ[bufp]] += 16
        P.on("sync", lambda e, k=k, bufp=bufp: kq_loads(e, k, bufp, 0, Q_sb))
        vb = V_sb if k % 2 == 0 else Vn_sb
        P.waitall("sync", RSK[k])
        for r in range(4):
            P.dma("sync", "vv", bass.AP(vb, 1152 * r, [[4608, 128], [144, 8], [1, 128]]),
                  bass.AP(rsV, r * 524288 + k * 131072, [[128, 128], [16384, 8], [1, 128]]))
        if k == 0:
            P.wait("sync", "saa", t_sign_last)
            P.dma("sync", "umx", um_sb[:, :], umask[:, :])

        P.waitall("tensor", "vv")
        if k == 0:
            P.waitall("tensor", "cn")
            P.wait("tensor", "svv", t_last_ev)
            P.wait("tensor", "svv", t_onesb)
        if t_vs_copy:
            P.wait("tensor", "svv", t_vs_copy)
        for j in range(32):
            P.emit("tensor", lambda e, j=j, vb=vb: e.matmul(
                psA[:, 3200 + j:3201 + j], vb[:, 144 * j:144 * j + 128], onesb[:, 0:1],
                start=True, stop=True), inc="stt" if j == 31 else None)
        t_vmm = P.cnt["stt"]
        P.wait("vector", "stt", t_vmm)
        if t_vstr_prev:
            P.wait("vector", "stt", t_vstr_prev)
        t_vsf_prev = P.emit("vector", lambda e: e.tensor_copy(
            vsfm[:, 0:32], psA[:, 3200:3232]), inc="svv")
        P.wait("tensor", "svv", t_vsf_prev)
        tt_tr = P.emit("tensor", lambda e: e.transpose(
            psA[0:32, 3072:3200], vsfm[:, 0:32], ident_sb[:, :]), inc="stt")
        t_vstr_prev = tt_tr
        P.wait("vector", "stt", tt_tr)
        if t_suffix_last:
            P.wait("vector", "stt", t_suffix_last)
        P.wait("vector", "svv", t_vscnt)
        t_vs_copy = P.emit("vector", lambda e: e.tensor_copy(vs_sm[:, 0:128], psA[0:32, 3072:3200]), inc="svv")

        P.waitall("tensor", KQ[bufp])
        P.waitall("tensor", "vv")
        if k == 0:
            P.wait("tensor", "svv", t_vones)
        if k == 1:
            P.wait("tensor", "svv", t_vones2)

        def emit_pvs(grp, w, exp_tgt, sel_tgt):
            P.wait("tensor", "saa", exp_tgt)
            if sel_tgt:
                P.wait("tensor", "svv", sel_tgt)
            if grp[0] == 0 and t_oevac:
                P.wait("tensor", "svv", t_oevac)
            for j in grp:
                for i in range(max(j, 4 * w), 4 * w + 4):
                    P.emit("tensor", lambda e, j=j, i=i, w=w, vb=vb: e.matmul(
                        psA[:, 512 * (i % 4):512 * (i % 4) + 129],
                        P_sb[:, 512 * (j % 4) + 128 * (i - 4 * w):512 * (j % 4) + 128 * (i - 4 * w) + 128],
                        vb[:, 144 * j:144 * j + 129],
                        start=(j == 0), stop=False),
                        inc="stt" if (j == grp[-1] and i == 4 * w + 3) else None)
            for j in grp:
                last_pv_pb[j % 4] = P.cnt["stt"]

        for w in range(8):
            groups = []
            jj = 0
            while jj + 1 <= 4 * w - 1:
                groups.append((jj, jj + 1))
                jj += 2
            while jj <= 4 * w + 3:
                groups.append((jj,))
                jj += 1
            pend = None
            for grp in groups:
                tt_qk = 0
                for j in grp:
                    par = j % 2
                    c0 = max(128 * j, 512 * w)
                    N = 512 * (w + 1) - c0
                    P.wait("tensor", "saa", last_exp_par[par])
                    tt_qk = P.emit("tensor", lambda e, par=par, bufp=bufp, j=j, c0=c0, N=N: e.matmul(
                        psA[:, 512 * (4 + par):512 * (4 + par) + N],
                        K_sb[:, 4096 * bufp + 128 * j:4096 * bufp + 128 * (j + 1)],
                        Q_sb[:, 4096 * bufp + c0:4096 * bufp + c0 + N],
                        start=True, stop=True), inc="stt")
                if pend is not None:
                    emit_pvs(*pend)
                if first_P_use[0]:
                    P.wait("scalar", "svv", t_tern_lastadd)
                    for b in range(8):
                        P.waitall("scalar", TS[b])
                    first_P_use[0] = False
                P.wait("scalar", "stt", tt_qk)
                for j in grp:
                    P.wait("scalar", "stt", last_pv_pb[j % 4])
                if len(grp) == 2:
                    jA = grp[0]
                    t_exp = P.emit("scalar", lambda e, jA=jA: e.activation(
                        P_sb[:, 512 * (jA % 4):512 * (jA % 4) + 1024],
                        psA[:, 2048:3072], AF.Exp, scale=INV64), inc="saa")
                    last_exp_par[0] = t_exp
                    last_exp_par[1] = t_exp
                    sel_tgt = 0
                else:
                    j = grp[0]
                    par = j % 2
                    c0 = max(128 * j, 512 * w)
                    N = 512 * (w + 1) - c0
                    pcol0 = c0 - 512 * w
                    t_exp = P.emit("scalar", lambda e, j=j, par=par, pcol0=pcol0, N=N: e.activation(
                        P_sb[:, 512 * (j % 4) + pcol0:512 * (j % 4) + pcol0 + N],
                        psA[:, 512 * (4 + par):512 * (4 + par) + N], AF.Exp, scale=INV64), inc="saa")
                    last_exp_par[par] = t_exp
                    sel_tgt = 0
                    if j >= 4 * w:
                        P.wait("vector", "saa", t_exp)
                        sel_tgt = P.emit("vector", lambda e, j=j, pcol0=pcol0: e.copy_predicated(
                            P_sb[:, 512 * (j % 4) + pcol0:512 * (j % 4) + pcol0 + 128],
                            itril_bf[:, :], ones2d[:, :]), inc="svv")
                pend = (grp, w, t_exp, sel_tgt)
            emit_pvs(*pend)
            P.wait("tensor", "svv", t_vs_copy)
            if k == 0 and w == 0:
                P.waitall("tensor", "umx")
                P.wait("tensor", "svv", t_ones2d)
                P.wait("tensor", "svv", t_trilbf)
            for i in range(4 * w, 4 * w + 4):
                P.emit("tensor", lambda e, i=i: e.matmul(
                    psA[:, 512 * (i % 4):512 * (i % 4) + 129],
                    um_sb[:, 128 * i:128 * (i + 1)], vs_sm[:, :],
                    start=False, stop=True), inc="stt" if i == 4 * w + 3 else None)
            t_wave = P.cnt["stt"]
            if w == 7:
                t_suffix_last = t_wave
            ab = w % 2
            P.wait("vector", "stt", t_wave)
            if t_oevac:
                P.wait("vector", "svv", t_oevac)
            for u in range(4):
                P.emit("vector", lambda e, ab=ab, u=u: e.reciprocal(
                    rc_sb[:, 4 * ab + u:4 * ab + u + 1], psA[:, 512 * u + 128:512 * u + 129]),
                    inc="svv" if u == 3 else None)
            P.wait("vector", "svv", P.cnt["svv"])
            P.waitall("vector", ATTK[k][ab])
            if w < 2 and k >= 1:
                P.waitall("vector", ATTK[k - 1][ab])
            for u in range(4):
                P.emit("vector", lambda e, ab=ab, u=u: e.tensor_scalar(
                    att_sb[:, 512 * ab + 128 * u:512 * ab + 128 * (u + 1)],
                    psA[:, 512 * u:512 * u + 128], rc_sb[:, 4 * ab + u:4 * ab + u + 1],
                    None, ALU.mult), inc="svv" if u == 3 else None)
            t_oevac = P.cnt["svv"]
            P.wait("sync", "svv", t_oevac)
            for u in range(4):
                i = 4 * w + u
                P.dma("sync", ATTK[k][ab], g2i[k, i // 8][128 * (i % 8):128 * (i % 8 + 1), :],
                      att_sb[:, 512 * ab + 128 * u:512 * ab + 128 * (u + 1)])
            t_attdma_unused = 0
        head_done[k] = P.cnt["stt"]

    t_ag2k = []
    for k in range(4):
        P.waitall("gpsimd", ATTK[k][0])
        P.waitall("gpsimd", ATTK[k][1])
        t_ag2k.append(P.emit("gpsimd", lambda e, k=k: e.collective_compute(
            "AllGather", ALU.bypass, replica_groups=G4,
            ins=[g2i[k].opt()], outs=[g2o[k].opt()]), inc="scc"))

    def go_rs2(e):
        with e.register("ro2") as ro2:
            for k in range(4):
                e.wait_ge(P.sems["scc"], t_ag2k[k])
                e.reg_load(ro2, ms_sb[0:1, 0:1])
                e.reg_mul(ro2, ro2, 131072)
                if k:
                    e.reg_add(ro2, ro2, k * 2097152)
                e.dma_start(out=bass.AP(rs2, k * 524288, [[131072, 4], [1, 131072]]),
                            in_=bass.AP(g2o, ro2, [[524288, 4], [1, 131072]])
                            ).then_inc(P.sems["rs2s"], 16)
    P.cnt["rs2s"] += 4 * 16
    P.on("gpsimd", go_rs2)
    P.wait("gpsimd", "stt", head_done[3])
    wcast(wo_d)

    if stage == 2:
        if "t_rs2" in T:
            P.waitall("gpsimd", "rs2s")
            P.dma("gpsimd", "tapA", bass.AP(T["t_rs2"], 0, [[1024, 2048], [1, 1024]]),
                  bass.AP(rs2, 0, [[1024, 2048], [1, 1024]]))
            P.waitall("gpsimd", "tapA")
        if "t_scal" in T:
            P.wait("sync", "svv", tv_scal)
            P.dma("sync", "tapB", T["t_scal"][:, :], scal_sb[:, :])
            P.waitall("sync", "tapB")
        P.dma("sync", "yme", bass.AP(y, 0, [[2048, 128], [262144, 8], [1, 2048]]),
              bass.AP(x_res, 0, [[16384, 128], [2048, 8], [1, 2048]]))
        P.waitall("sync", "yme")
        return finish(nc, P)

    P.waitall("sync", "rs2s")
    for g in range(16):
        P.dma("sync", "fmao", xn_fm[:, 1024 * g:1024 * (g + 1)], rs2[g % 4, g // 4], transpose=True)

    t_wo_ev = {}
    t_womm = 0
    P.waitall("tensor", "wc")
    P.waitall("tensor", "fmao")
    ap2_prev = [0, 0]
    for tb in range(8):
        b0 = 4 * (tb % 2)
        if tb >= 2:
            P.wait("tensor", "svv", t_wo_ev[tb - 2])
        for ic in range(16):
            for og in range(4):
                P.emit("tensor", lambda e, b0=b0, ic=ic, og=og, tb=tb: e.matmul(
                    psA[:, 512 * (b0 + og):512 * (b0 + og + 1)],
                    xn_fm[:, 1024 * ic + 128 * tb:1024 * ic + 128 * (tb + 1)],
                    w_sb[:, 2048 * ic + 512 * og:2048 * ic + 512 * (og + 1)],
                    start=(ic == 0), stop=(ic == 15)),
                    inc="stt" if (ic == 15 and og == 3) else None)
        t_womm = P.cnt["stt"]
        P.wait("vector", "stt", t_womm)
        for og in range(4):
            P.emit("vector", lambda e, b0=b0, og=og, tb=tb: e.tensor_tensor(
                x_res[:, 2048 * tb + 512 * og:2048 * tb + 512 * (og + 1)],
                psA[:, 512 * (b0 + og):512 * (b0 + og + 1)],
                x_res[:, 2048 * tb + 512 * og:2048 * tb + 512 * (og + 1)], ALU.add),
                inc="svv" if og == 3 else None)
        t_wo_ev[tb] = P.cnt["svv"]
        ln_block(tb, (lambda tb=tb: P.wait("vector", "svv", t_wo_ev[tb])), "x2t", x2t, ap2_prev)

    P.waitall("sync", "x2t")
    P.wait("sync", "stt", t_womm)
    for ic in range(16):
        P.dma("sync", "fmx2", xn_fm[:, 1024 * ic:1024 * (ic + 1)],
              x2t[:, 128 * ic:128 * (ic + 1)], transpose=True)

    if stage == 3:
        P.wait("sync", "svv", t_wo_ev[7])
        P.dma("sync", "yme", bass.AP(y, 0, [[2048, 128], [262144, 8], [1, 2048]]),
              bass.AP(x_res, 0, [[16384, 128], [2048, 8], [1, 2048]]))
        P.waitall("sync", "yme")
        return finish(nc, P)

    FC = ("fcA", "fcB")
    t_fc_mm = {}
    t_sq = {}
    t_relu = {}
    P.wait("sync", "scc", t_agwf)
    for hc in range(40):
        slot = hc % 2
        r, l = hc // 5, hc % 5
        if hc >= 2:
            P.wait("sync", "stt", t_fc_mm[hc - 2])
        P.dma("sync", FC[slot], fcw_sb[:, 2048 * slot:2048 * (slot + 1)],
              bass.AP(q2fT_o, (r * 5 + l) * 262144, [[2048, 128], [1, 2048]]))
        P.waitall("tensor", FC[slot])
        P.waitall("tensor", "fmx2")
        if hc >= 2:
            P.wait("tensor", "saa", t_relu[hc - 2])
        pr = hc % 2
        for ic in range(16):
            for hf in range(2):
                P.emit("tensor", lambda e, pr=pr, hf=hf, ic=ic, slot=slot: e.matmul(
                    psA[:, 512 * (2 * pr + hf):512 * (2 * pr + hf + 1)],
                    fcw_sb[:, 2048 * slot + 128 * ic:2048 * slot + 128 * (ic + 1)],
                    xn_fm[:, 1024 * ic + 512 * hf:1024 * ic + 512 * (hf + 1)],
                    start=(ic == 0), stop=(ic == 15)),
                    inc="stt" if (ic == 15 and hf == 1) else None)
        t_fc_mm[hc] = P.cnt["stt"]
        P.wait("scalar", "stt", t_fc_mm[hc])
        if hc >= 2:
            P.wait("scalar", "svv", t_sq[hc - 2])
        for hf in range(2):
            P.emit("scalar", lambda e, pr=pr, hf=hf, slot=slot: e.activation(
                rl_sb[:, 1024 * slot + 512 * hf:1024 * slot + 512 * (hf + 1)],
                psA[:, 512 * (2 * pr + hf):512 * (2 * pr + hf + 1)],
                AF.Relu, scale=scal_sb[:, C_SFC:C_SFC + 1]),
                inc="saa" if hf == 1 else None)
        t_relu[hc] = P.cnt["saa"]
        P.wait("vector", "saa", t_relu[hc])
        t_sq[hc] = P.emit("vector", lambda e, hc=hc, slot=slot: e.tensor_tensor(
            h_sb[:, 1024 * hc:1024 * (hc + 1)],
            rl_sb[:, 1024 * slot:1024 * (slot + 1)],
            rl_sb[:, 1024 * slot:1024 * (slot + 1)], ALU.mult), inc="svv")
    t_h_done = t_sq[39]

    PR = ("prA", "prB")
    OB = ("obA", "obB")
    P.wait("sync", "scc", t_agwp)
    P.wait("sync", "svv", t_h_done)
    t_quad_ev = 0
    t_prw_mm = {}
    for th in range(2):
        for dh2 in range(2):
            qd = 2 * th + dh2
            for hc in range(40):
                slot = hc % 2
                if hc >= 2:
                    P.wait("sync", "stt", t_prw_mm[(qd, hc - 2)])
                elif qd > 0:
                    P.wait("sync", "stt", t_prw_mm[(qd - 1, 38 + hc)])
                P.dma("sync", PR[slot], bass.AP(prw_sb, 1024 * slot, [[2048, 128], [256, 4], [1, 256]]),
                      bass.AP(q2pT_o, 4 * dh2 * 1310720 + 128 * hc * 256, [[256, 128], [1310720, 4], [1, 256]]))
                P.waitall("tensor", PR[slot])
                if hc == 0:
                    P.wait("tensor", "svv", t_h_done)
                    if t_quad_ev:
                        P.wait("tensor", "svv", t_quad_ev)
                for tb in range(4):
                    tbg = 4 * th + tb
                    for bd in range(2):
                        P.emit("tensor", lambda e, tb=tb, bd=bd, hc=hc, slot=slot, tbg=tbg: e.matmul(
                            psA[:, 512 * (2 * tb + bd):512 * (2 * tb + bd + 1)],
                            h_sb[:, 1024 * hc + 128 * tbg:1024 * hc + 128 * (tbg + 1)],
                            prw_sb[:, 1024 * slot + 512 * bd:1024 * slot + 512 * (bd + 1)],
                            start=(hc == 0), stop=(hc == 39)),
                            inc="stt" if (tb == 3 and bd == 1) else None)
                t_prw_mm[(qd, hc)] = P.cnt["stt"]
            P.wait("vector", "stt", t_prw_mm[(qd, 39)])
            for bi in range(8):
                tb, bd = bi // 2, bi % 2
                tbg = 4 * th + tb
                obs = bi % 2
                P.waitall("vector", OB[obs])
                tq = P.emit("vector", lambda e, bi=bi, tbg=tbg, bd=bd, dh2=dh2, obs=obs: e.scalar_tensor_tensor(
                    ob_sb[:, 512 * obs:512 * (obs + 1)],
                    psA[:, 512 * bi:512 * (bi + 1)],
                    scal_sb[:, C_SPR:C_SPR + 1],
                    x_res[:, 2048 * tbg + 1024 * dh2 + 512 * bd:2048 * tbg + 1024 * dh2 + 512 * (bd + 1)],
                    ALU.mult, ALU.add), inc="svv")
                P.wait("sync", "svv", tq)
                P.dma("sync", OB[obs], y[128 * tbg:128 * (tbg + 1),
                                         1024 * dh2 + 512 * bd:1024 * dh2 + 512 * (bd + 1)],
                      ob_sb[:, 512 * obs:512 * (obs + 1)])
            t_quad_ev = P.cnt["svv"]

    P.waitall("sync", "obA")
    P.waitall("sync", "obB")
    return finish(nc, P)


def finish(nc, P):
    with nc.Block() as block:
        for eng in ENGINES:
            fns = P.streams[eng]

            def runner(e, fns=fns):
                for f in fns:
                    f(e)
            getattr(block, eng)(runner)
    assert nc.sbuf_base <= 17408, f"bass bump allocator grew into overlay: {nc.sbuf_base}"
    return nc


def make_consts():
    s = np.arange(128)[:, None]
    q = np.arange(128)[None, :]
    tril_T = (s <= q).astype(np.float32)
    j = np.arange(32)[:, None]
    i = (np.arange(4096)[None, :] // 128)
    um = (j > i).astype(np.float32)
    return tril_T, um, np.eye(128, dtype=np.float32)


def make_in_maps(inputs):
    x = np.asarray(inputs["L0_attn_x"])
    wq = np.ascontiguousarray(np.asarray(inputs["L0_attn_Wq"], dtype=np.float32))
    wk = np.ascontiguousarray(np.asarray(inputs["L0_attn_Wk"], dtype=np.float32))
    wv = np.ascontiguousarray(np.asarray(inputs["L0_attn_Wv"], dtype=np.float32))
    wo = np.ascontiguousarray(np.asarray(inputs["L0_attn_Wo"], dtype=np.float32))
    wfc = np.asarray(inputs["L0_ffn_w_fc"], dtype=np.float32)
    wpr = np.asarray(inputs["L0_ffn_w_proj"], dtype=np.float32)
    tril_T, um, ident = make_consts()
    maps = []
    for c in range(8):
        r = c % 4
        b = c // 4
        maps.append({
            "x_q": np.ascontiguousarray(x[b, 1024 * r:1024 * (r + 1), :], dtype=np.float32),
            "wq": wq, "wk": wk, "wv": wv, "wo": wo,
            "wfc_sl": np.ascontiguousarray(wfc[640 * c:640 * (c + 1), :]),
            "wpr_sl": np.ascontiguousarray(wpr[256 * c:256 * (c + 1), :]),
            "meta": np.array([[r]], dtype=np.uint32),
            "tril": tril_T, "umask": um, "ident": ident,
        })
    return maps


from concourse.bass_utils import run_bass_kernel_spmd

_cache = {}


def kernel(**inputs):
    if "nc" not in _cache:
        _cache["nc"] = build(stage=4, taps=())
    nc = _cache["nc"]
    maps = make_in_maps(inputs)
    res = run_bass_kernel_spmd(nc, maps, core_ids=list(range(8)))
    out = np.empty((2, 4096, 2048), np.float32)
    for c in range(8):
        r = c % 4
        out[c // 4, 1024 * r:1024 * (r + 1), :] = np.asarray(res.results[c]["y"]).reshape(1024, 2048)
    return out


# version 15
# speedup vs baseline: 1.0766x; 1.0037x over previous
import numpy as np
import ml_dtypes
import concourse.bass as bass
import concourse.mybir as mybir

F32 = mybir.dt.float32
BF16 = mybir.dt.bfloat16
U32 = mybir.dt.uint32
AF = mybir.ActivationFunctionType
ALU = mybir.AluOpType
AX = mybir.AxisListType.X
G4 = [[0, 1, 2, 3], [4, 5, 6, 7]]
G8 = [list(range(8))]
ENGINES = ("sync", "gpsimd", "tensor", "vector", "scalar")

SEMS = ("cn", "umx", "am2", "ami", "xA", "xB", "wc", "xnt", "x2t", "fmx", "fmao", "fmx2",
        "sl0", "sl1", "sl2", "sl3", "evA", "evB", "evC", "evD",
        "ts0", "ts1", "ts2", "ts3", "ts4", "ts5", "ts6", "ts7",
        "tw0", "tw1", "tw2", "tw3", "tw4", "tw5", "tw6", "tw7",
        "kqA", "kqB", "vv",
        "at0A", "at0B", "at1A", "at1B", "at2A", "at2B", "at3A", "at3B",
        "rsA", "rsB", "rsC", "rsD", "rs2s", "fcA", "fcB", "prA", "prB", "obA", "obB",
        "tapA", "tapB", "yme",
        "scc", "stt", "svv", "saa")


class Prog:
    def __init__(self, nc):
        self.nc = nc
        self.streams = {k: [] for k in ENGINES}
        self.sems = {}
        self.cnt = {}
        for s in SEMS:
            self.sems[s] = nc.semaphore(s).__enter__()
            self.cnt[s] = 0

    def on(self, eng, fn):
        self.streams[eng].append(fn)

    def wait(self, eng, sem, tgt):
        assert tgt <= self.cnt[sem], f"wait on future {sem}:{tgt}>{self.cnt[sem]}"
        if tgt <= 0:
            return
        s = self.sems[sem]
        self.on(eng, lambda e, s=s, t=tgt: e.wait_ge(s, t))

    def waitall(self, eng, sem):
        self.wait(eng, sem, self.cnt[sem])

    def emit(self, eng, fn, inc=None, k=1):
        if inc is None:
            self.on(eng, fn)
            return None
        self.cnt[inc] += k
        tgt = self.cnt[inc]
        s = self.sems[inc]
        self.on(eng, lambda e, f=fn, s=s, kk=k: f(e).then_inc(s, kk))
        return tgt

    def dma(self, eng, sem, out, in_, **kw):
        return self.emit(eng, lambda e, o=out, i=in_, kws=dict(kw): e.dma_start(out=o, in_=i, **kws),
                         inc=sem, k=16)


def build(stage=4, taps=()):
    nc = bass.Bass("TRN2", target_bir_lowering=False, debug=False)
    P = Prog(nc)

    x_q = nc.dram_tensor("x_q", [1024, 2048], F32, kind="ExternalInput")
    wq_d = nc.dram_tensor("wq", [2048, 2048], F32, kind="ExternalInput")
    wk_d = nc.dram_tensor("wk", [2048, 2048], F32, kind="ExternalInput")
    wv_d = nc.dram_tensor("wv", [2048, 2048], F32, kind="ExternalInput")
    wo_d = nc.dram_tensor("wo", [2048, 2048], F32, kind="ExternalInput")
    wfc_sl = nc.dram_tensor("wfc_sl", [640, 2048], F32, kind="ExternalInput")
    wpr_sl = nc.dram_tensor("wpr_sl", [256, 5120], F32, kind="ExternalInput")
    meta = nc.dram_tensor("meta", [1, 1], U32, kind="ExternalInput")
    tril = nc.dram_tensor("tril", [128, 128], F32, kind="ExternalInput")
    umask = nc.dram_tensor("umask", [32, 4096], F32, kind="ExternalInput")
    ident = nc.dram_tensor("ident", [128, 128], F32, kind="ExternalInput")
    y = nc.dram_tensor("y", [1024, 2048], F32, kind="ExternalOutput")

    g1i = nc.dram_tensor("g1i", [4, 4, 3, 128, 1024], BF16)
    g1o = nc.dram_tensor("g1o", [4, 4, 4, 3, 128, 1024], BF16)
    rsV = nc.dram_tensor("rsV", [4, 4, 8, 128, 128], BF16)
    xnt = nc.dram_tensor("xnt", [1024, 2048], BF16)
    x2t = nc.dram_tensor("x2t", [1024, 2048], BF16)
    g2i = nc.dram_tensor("g2i", [4, 4, 1024, 128], BF16)
    g2o = nc.dram_tensor("g2o", [4, 4, 4, 1024, 128], BF16)
    rs2 = nc.dram_tensor("rs2", [4, 4, 1024, 128], BF16)
    q2fT_i = nc.dram_tensor("q2fT_i", [5, 128, 16, 128], BF16)
    q2fT_o = nc.dram_tensor("q2fT_o", [8, 5, 128, 16, 128], BF16, addr_space="Shared")
    q2pT_i = nc.dram_tensor("q2pT_i", [5120, 256], BF16)
    q2pT_o = nc.dram_tensor("q2pT_o", [8, 5120, 256], BF16, addr_space="Shared")
    ami_d = nc.dram_tensor("ami", [2, 1], F32)
    amo_d = nc.dram_tensor("amo", [2, 1], F32)

    T = {}
    if "rs1" in taps:
        T["t_rs1"] = nc.dram_tensor("t_rs1", [4, 4, 8, 128, 128], F32, kind="ExternalOutput")
    if "rs2" in taps:
        T["t_rs2"] = nc.dram_tensor("t_rs2", [4, 4, 1024, 128], F32, kind="ExternalOutput")
    if "scal" in taps:
        T["t_scal"] = nc.dram_tensor("t_scal", [128, 8], F32, kind="ExternalOutput")

    SHIFT = 17408

    def A(name, shape, dt, offset):
        return nc.alloc_sbuf_tensor_at(name, shape, dt, offset=offset + SHIFT)
    x_res = A("x_res", [128, 16384], F32, offset=0)
    xn_fm = A("xn_fm", [128, 16384], BF16, offset=65536)
    K_sb = A("K_sb", [128, 8192], BF16, offset=65536)
    Q_sb = A("Q_sb", [128, 8192], BF16, offset=65536 + 16384)
    w_sb = A("w_sb", [128, 32768], BF16, offset=98304)
    h_sb = A("h_sb", [128, 40960], BF16, offset=98304)
    V_sb = A("V_sb", [128, 4608], BF16, offset=163840)
    xn_stage = A("xn_stage", [128, 2048], BF16, offset=173056)
    ev_sb = A("ev_sb", [128, 2048], BF16, offset=173056)
    ln_s = A("ln_s", [128, 64], F32, offset=179200)
    wslab = A("wslab", [128, 4096], F32, offset=180224)
    um_sb = A("um_sb", [32, 4096], F32, offset=180224)
    rl_sb = A("rl_sb", [128, 2048], F32, offset=180224)
    prw_sb = A("prw_sb", [128, 2048], BF16, offset=180224)
    ob_sb = A("ob_sb", [128, 1024], F32, offset=184320)
    Vn_sb = A("Vn_sb", [128, 4608], BF16, offset=98304)
    fcw_sb = A("fcw_sb", [128, 4096], BF16, offset=188416)
    P_sb = A("P_sb", [128, 2048], BF16, offset=196608)
    qa_sb = A("qa_sb", [128, 1024], BF16, offset=196608)
    qb_sb = A("qb_sb", [128, 1024], BF16, offset=198656)
    qb1_sb = A("qb1_sb", [128, 1024], BF16, offset=200704)
    att_sb = A("att_sb", [128, 1024], BF16, offset=202752)
    rc_sb = A("rc_sb", [128, 8], F32, offset=204800)
    vs_sm = A("vs_sm", [32, 129], F32, offset=204832)
    vsfm = A("vsfm", [128, 64], F32, offset=205376)
    tril_sb = A("tril_sb", [128, 128], F32, offset=206848)
    ident_sb = A("ident_sb", [128, 128], F32, offset=207360)
    scal_sb = A("scal_sb", [128, 8], F32, offset=207872)
    amred = A("amred", [128, 24], F32, offset=207904)
    vals_sb = A("vals_sb", [1, 16], F32, offset=208000)
    am2_sb = A("am2_sb", [1, 4], F32, offset=208064)
    ones_r = A("ones_r", [1, 128], F32, offset=208096)
    ones_c = A("ones_c", [128, 1], F32, offset=208640)
    ms_sb = A("ms_sb", [1, 1], U32, offset=208672)
    tsp_sb = A("tsp_sb", [128, 1280], BF16, offset=208704)
    onesb = A("onesb", [128, 1], BF16, offset=211648)
    ones2d = A("ones2d", [128, 128], BF16, offset=211264)
    itril_bf = A("itril_bf", [128, 128], mybir.dt.uint8, offset=211520)

    psA = nc.alloc_psum_tensor("psA", [128, 4096], F32)

    C_THRF, C_THRP, C_SFC, C_SPR, C_NTHRF, C_NTHRP = 2, 3, 4, 5, 6, 7
    INV64 = 1.0 / 64.0
    SL = ("sl0", "sl1", "sl2", "sl3")

    P.dma("sync", "cn", tril_sb[:, :], tril[:, :])
    P.dma("sync", "cn", ident_sb[:, :], ident[:, :])
    P.dma("gpsimd", "tapA", ms_sb[:, :], meta[:, :])
    P.emit("vector", lambda e: e.memset(ones_r[:, :], 1.0))
    P.emit("vector", lambda e: e.memset(ones_c[:, :], 1.0))
    t_vones = P.emit("vector", lambda e: e.memset(
        bass.AP(V_sb, 128, [[4608, 128], [144, 32]]), 1.0), inc="svv")
    t_vscnt = P.emit("vector", lambda e: e.memset(vs_sm[:, 128:129], 128.0), inc="svv")
    t_onesb = P.emit("vector", lambda e: e.memset(onesb[:, :], 1.0), inc="svv")

    P.dma("sync", "xA", bass.AP(x_res, 0, [[16384, 128], [2048, 4], [1, 2048]]),
          bass.AP(x_q, 0, [[2048, 128], [262144, 4], [1, 2048]]))
    P.dma("sync", "xB", bass.AP(x_res, 8192, [[16384, 128], [2048, 4], [1, 2048]]),
          bass.AP(x_q, 1048576, [[2048, 128], [262144, 4], [1, 2048]]))
    t_ones2d = P.emit("vector", lambda e: e.memset(ones2d[:, :], 1.0), inc="svv")

    def wcast(dram):
        return P.dma("gpsimd", "wc",
                     bass.AP(w_sb, 0, [[32768, 128], [2048, 16], [1, 2048]]),
                     bass.AP(dram, 0, [[2048, 128], [262144, 16], [1, 2048]]))
    def wcast_v(dram):
        for s in range(4):
            for d in range(4):
                P.dma("gpsimd", "wc",
                      bass.AP(w_sb, 512 * s + 128 * d, [[32768, 128], [2048, 16], [1, 128]]),
                      bass.AP(dram, (4 * d + s) * 128, [[2048, 128], [262144, 16], [1, 128]]))
    wcast(wq_d)

    def ln_block(tb, pre_wait, stage_sem, out_dram, ap_prev):
        sl = 32 * (tb % 2)
        pre_wait()
        if ap_prev[tb % 2]:
            P.wait("vector", "svv", ap_prev[tb % 2])
        for s4 in range(4):
            P.emit("vector", lambda e, tb=tb, sl=sl, s4=s4: e.bn_stats(
                ln_s[:, sl + 6 * s4: sl + 6 * s4 + 6],
                x_res[:, 2048 * tb + 512 * s4: 2048 * tb + 512 * (s4 + 1)]),
                inc="svv" if s4 == 3 else None)
        P.wait("vector", "svv", P.cnt["svv"])
        t2 = P.emit("vector", lambda e, sl=sl: e.bn_aggr(ln_s[:, sl + 24:sl + 26], ln_s[:, sl:sl + 24]), inc="svv")
        P.wait("vector", "svv", t2)
        t3 = P.emit("vector", lambda e, sl=sl: e.tensor_scalar(
            ln_s[:, sl + 26:sl + 27], ln_s[:, sl + 25:sl + 26], 1e-5, None, ALU.add), inc="svv")
        P.wait("vector", "svv", t3)
        t4 = P.emit("vector", lambda e, sl=sl: e.reciprocal(ln_s[:, sl + 27:sl + 28], ln_s[:, sl + 26:sl + 27]), inc="svv")
        P.wait("scalar", "svv", t4)
        t5 = P.emit("scalar", lambda e, sl=sl: e.activation(
            ln_s[:, sl + 28:sl + 29], ln_s[:, sl + 27:sl + 28], AF.Sqrt), inc="saa")
        P.wait("vector", "saa", t5)
        P.waitall("vector", stage_sem)
        t_ap = P.emit("vector", lambda e, tb=tb, sl=sl: e.tensor_scalar(
            xn_stage[:, :], x_res[:, 2048 * tb:2048 * (tb + 1)],
            ln_s[:, sl + 24:sl + 25], ln_s[:, sl + 28:sl + 29], ALU.subtract, ALU.mult), inc="svv")
        ap_prev[tb % 2] = t_ap
        P.wait("sync", "svv", t_ap)
        P.dma("sync", stage_sem, out_dram[128 * tb:128 * (tb + 1), :], xn_stage[:, :])
        return t_ap

    P.waitall("vector", "cn")
    t_trilbf = P.emit("vector", lambda e: e.tensor_scalar(
        itril_bf[:, :], tril_sb[:, :], 0.0, None, ALU.is_equal), inc="svv")
    ap_prev = [0, 0]
    for tb in range(8):
        ln_block(tb, (lambda tb=tb: P.waitall("vector", "xA" if tb < 4 else "xB")), "xnt", xnt, ap_prev)
    P.waitall("sync", "xnt")
    for ic in range(16):
        P.dma("sync", "fmx", xn_fm[:, 1024 * ic:1024 * (ic + 1)],
              xnt[:, 128 * ic:128 * (ic + 1)], transpose=True)

    pieces = [(wfc_sl, i, cp) for i in range(5) for cp in range(2)] + \
             [(wpr_sl, i, cp) for i in range(2) for cp in range(5)]
    t_red = [0] * 20
    for n, (src_d, i, cp) in enumerate(pieces):
        slot = n % 4
        if n >= 4:
            P.wait("scalar", "svv", t_red[n - 4])
        P.dma("scalar", SL[slot], wslab[:, 1024 * slot:1024 * (slot + 1)],
              src_d[128 * i:128 * (i + 1), 1024 * cp:1024 * (cp + 1)])
        P.waitall("vector", SL[slot])
        t_red[n] = P.emit("vector", lambda e, n=n: e.tensor_reduce(
            amred[:, n:n + 1], wslab[:, 1024 * (n % 4):1024 * (n % 4 + 1)],
            AX, ALU.add, apply_absolute_value=True), inc="svv")
    P.wait("vector", "svv", t_red[19])
    P.emit("vector", lambda e: e.tensor_reduce(amred[:, 20:21], amred[:, 0:10], AX, ALU.add))
    tv2 = P.emit("vector", lambda e: e.tensor_reduce(amred[:, 21:22], amred[:, 10:20], AX, ALU.add), inc="svv")
    P.wait("tensor", "svv", tv2)
    tt_am = P.emit("tensor", lambda e: e.matmul(psA[0:2, 3584:3585], amred[:, 20:22], ones_c[:, :],
                                                start=True, stop=True), inc="stt")
    P.wait("vector", "stt", tt_am)
    tv3 = P.emit("vector", lambda e: e.tensor_copy(amred[0:2, 22:23], psA[0:2, 3584:3585]), inc="svv")
    P.wait("sync", "svv", tv3)
    P.dma("sync", "ami", ami_d[:, :], amred[0:2, 22:23])
    P.waitall("gpsimd", "ami")
    t_ar = P.emit("gpsimd", lambda e: e.collective_compute(
        "AllReduce", ALU.add, replica_groups=G8,
        ins=[ami_d.ap().opt()], outs=[amo_d.ap().opt()]), inc="scc")
    t_qkv_mm = {}
    last_pair_ev = {}
    ev_sem4 = ("evA", "evB", "evC", "evD")
    t_ag1k = [0, 0, 0, 0]
    t_sign = [0] * 20
    t_tadd = [0] * 20
    TS = ("ts0", "ts1", "ts2", "ts3", "ts4", "ts5", "ts6", "ts7")
    TW = ("tw0", "tw1", "tw2", "tw3", "tw4", "tw5", "tw6", "tw7")
    ts_after = {}

    def tern_piece(n):
        src_d, i, cp = pieces[n]
        slot = n % 4
        if n < 4:
            P.wait("scalar", "svv", t_red[16 + n])
        else:
            P.wait("scalar", "saa", t_sign[n - 4])
        P.dma("scalar", SL[slot], wslab[:, 1024 * slot:1024 * (slot + 1)],
              src_d[128 * i:128 * (i + 1), 1024 * cp:1024 * (cp + 1)])
        P.waitall("scalar", SL[slot])
        qb = qb_sb if n % 2 == 0 else qb1_sb
        if n == 0:
            P.wait("scalar", "svv", tv_scal)
        else:
            P.wait("scalar", "svv", t_tadd[n - 1])
        if n >= 2:
            for b in range(8):
                P.wait("scalar", TS[b], ts_after[n - 2][b])
        cT = C_THRF if src_d is wfc_sl else C_THRP
        cN = C_NTHRF if src_d is wfc_sl else C_NTHRP
        P.emit("scalar", lambda e, slot=slot, cT=cT: e.activation(
            qa_sb[:, :], wslab[:, 1024 * slot:1024 * (slot + 1)], AF.Sign, bias=scal_sb[:, cT:cT + 1]))
        t_sign[n] = P.emit("scalar", lambda e, slot=slot, cN=cN, qb=qb: e.activation(
            qb[:, :], wslab[:, 1024 * slot:1024 * (slot + 1)], AF.Sign, bias=scal_sb[:, cN:cN + 1]), inc="saa")
        P.wait("vector", "saa", t_sign[n])
        t_tadd[n] = P.emit("vector", lambda e, qb=qb: e.tensor_tensor(
            qb[:, :], qa_sb[:, :], qb[:, :], ALU.add), inc="svv")
        P.wait("scalar", "svv", t_tadd[n])
        for b in range(8):
            P.waitall("scalar", TW[b])
            P.dma("scalar", TS[b], tsp_sb[:, 128 * b:128 * (b + 1)],
                  qb[:, 128 * b:128 * (b + 1)], transpose=True)
        ts_after[n] = [P.cnt[TS[b]] for b in range(8)]
        for b in range(8):
            P.waitall("gpsimd", TS[b])
            if src_d is wfc_sl:
                dst = bass.AP(q2fT_i, i * 262144 + (8 * cp + b) * 128, [[2048, 128], [1, 128]])
            else:
                dst = bass.AP(q2pT_i, (8 * cp + b) * 32768 + 128 * i, [[256, 128], [1, 128]])
            P.dma("gpsimd", TW[b], dst, tsp_sb[:, 128 * b:128 * (b + 1)])

    for p in range(2):
        P.waitall("tensor", "wc")
        if p == 0:
            P.waitall("tensor", "fmx")
        for oc in range(16):
            it = 16 * p + oc
            pr = oc % 2
            if pr in last_pair_ev:
                P.wait("tensor", "svv", last_pair_ev[pr])
            for ic in range(16):
                for hf in range(2):
                    P.emit("tensor", lambda e, pr=pr, hf=hf, ic=ic, oc=oc: e.matmul(
                        psA[:, 512 * (2 * pr + hf):512 * (2 * pr + hf + 1)],
                        w_sb[:, 2048 * ic + 128 * oc:2048 * ic + 128 * (oc + 1)],
                        xn_fm[:, 1024 * ic + 512 * hf:1024 * ic + 512 * (hf + 1)],
                        start=(ic == 0), stop=(ic == 15)),
                        inc="stt" if (ic == 15 and hf == 1) else None)
            t_qkv_mm[(p, oc)] = P.cnt["stt"]
            evs = "evA" if it % 2 == 0 else "evB"
            ecol = 1024 * (it % 2)
            P.wait("vector", "stt", t_qkv_mm[(p, oc)])
            if it == 0:
                P.waitall("vector", "xnt")
            P.waitall("vector", evs)
            for hf in range(2):
                P.emit("vector", lambda e, pr=pr, hf=hf, ecol=ecol: e.tensor_copy(
                    ev_sb[:, ecol + 512 * hf:ecol + 512 * (hf + 1)],
                    psA[:, 512 * (2 * pr + hf):512 * (2 * pr + hf + 1)]),
                    inc="svv" if hf == 1 else None)
            t_ev = P.cnt["svv"]
            last_pair_ev[pr] = t_ev
            P.wait("sync", "svv", t_ev)
            P.dma("sync", evs, g1i[oc % 4, oc // 4, p], ev_sb[:, ecol:ecol + 1024])
            if p == 1:
                for n in range(20):
                    if n * 16 // 20 == oc:
                        tern_piece(n)
        P.wait("gpsimd", "stt", t_qkv_mm[(p, 15)])
        if p == 0:
            wcast(wk_d)
            P.wait("sync", "scc", t_ar)
            P.dma("sync", "am2", am2_sb[0:1, 0:2], amo_d[:, :])
            P.waitall("vector", "am2")
            u1 = P.emit("vector", lambda e: e.tensor_scalar(
                vals_sb[0:1, 0:2], am2_sb[0:1, 0:2], 1.0 / 10485760.0, 1e-5, ALU.mult, ALU.max), inc="svv")
            P.wait("vector", "svv", u1)
            P.emit("vector", lambda e: e.tensor_scalar(vals_sb[0:1, 2:4], vals_sb[0:1, 0:2], 0.7, None, ALU.mult))
            u3 = P.emit("vector", lambda e: e.tensor_scalar(vals_sb[0:1, 4:6], vals_sb[0:1, 0:2], 0.5, None, ALU.mult), inc="svv")
            P.wait("vector", "svv", u3)
            u4 = P.emit("vector", lambda e: e.tensor_scalar(vals_sb[0:1, 6:8], vals_sb[0:1, 2:4], -1.0, None, ALU.mult), inc="svv")
            P.wait("tensor", "svv", u4)
            P.wait("tensor", "svv", tv3)
            tt_bc = P.emit("tensor", lambda e: e.matmul(psA[:, 3584:3592], ones_r[:, :], vals_sb[0:1, 0:8],
                                                        start=True, stop=True), inc="stt")
            P.wait("vector", "stt", tt_bc)
            tv_scal = P.emit("vector", lambda e: e.tensor_copy(scal_sb[:, :], psA[:, 3584:3592]), inc="svv")
        else:
            wcast_v(wv_d)
    t_tern_lastadd = t_tadd[19]
    t_sign_last = t_sign[19]

    P.waitall("tensor", "wc")
    P.waitall("tensor", "ami")
    last_pair6 = {}
    t_last_ev = 0
    for u in range(32):
        s, tb = u // 8, u % 8
        pr6 = u % 6
        if pr6 in last_pair6:
            P.wait("tensor", "svv", last_pair6[pr6])
        for ic in range(16):
            P.emit("tensor", lambda e, pr6=pr6, ic=ic, s=s, tb=tb: e.matmul(
                psA[:, 512 * pr6:512 * (pr6 + 1)],
                xn_fm[:, 1024 * ic + 128 * tb:1024 * ic + 128 * (tb + 1)],
                w_sb[:, 2048 * ic + 512 * s:2048 * ic + 512 * (s + 1)],
                start=(ic == 0), stop=(ic == 15)),
                inc="stt" if ic == 15 else None)
        t_qkv_mm[(2, u)] = P.cnt["stt"]
        es = ev_sem4[u % 4]
        ecol = 512 * (u % 4)
        P.wait("vector", "stt", t_qkv_mm[(2, u)])
        if u < 4:
            P.waitall("vector", "evA")
            P.waitall("vector", "evB")
        P.waitall("vector", es)
        t_ev = P.emit("vector", lambda e, pr6=pr6, ecol=ecol: e.tensor_copy(
            ev_sb[:, ecol:ecol + 512], psA[:, 512 * pr6:512 * (pr6 + 1)]), inc="svv")
        last_pair6[pr6] = t_ev
        t_last_ev = t_ev
        P.wait("sync", "svv", t_ev)
        for d in range(4):
            P.dma("sync", es,
                  bass.AP(g1i, (4 * s + d) * 393216 + 262144 + tb * 16384, [[128, 128], [1, 128]]),
                  bass.AP(ev_sb, ecol + 128 * d, [[2048, 128], [1, 128]]))
        if tb == 7:
            for esem in ev_sem4:
                P.waitall("gpsimd", esem)
            t_ag1k[s] = P.emit("gpsimd", lambda e, s=s: e.collective_compute(
                "AllGather", ALU.bypass, replica_groups=G4,
                ins=[g1i[s].opt()], outs=[g1o[s].opt()]), inc="scc")
    t_qkv_mm_last = t_qkv_mm[(2, 31)]
    P.wait("vector", "stt", t_qkv_mm_last)
    t_vones2 = P.emit("vector", lambda e: e.memset(
        bass.AP(Vn_sb, 128, [[4608, 128], [144, 32]]), 1.0), inc="svv")

    P.waitall("gpsimd", "tapA")
    RSK = ("rsA", "rsB", "rsC", "rsD")

    def go_rs1(e):
        for k in range(4):
            e.wait_ge(P.sems["scc"], t_ag1k[k])
            with e.register(f"rV{k}") as rV:
                e.reg_load(rV, ms_sb[0:1, 0:1])
                e.reg_mul(rV, rV, 393216)
                e.reg_add(rV, rV, k * 6291456 + 262144)
                e.dma_start(out=bass.AP(rsV, k * 131072, [[524288, 4], [1, 131072]]),
                            in_=bass.AP(g1o, rV, [[1572864, 4], [1, 131072]])
                            ).then_inc(P.sems[RSK[k]], 16)
    for k in range(4):
        P.cnt[RSK[k]] += 16
    P.on("gpsimd", go_rs1)


    if stage >= 2:
        for b in range(8):
            P.waitall("gpsimd", TW[b])
        t_agwf = P.emit("gpsimd", lambda e: e.collective_compute(
            "AllGather", ALU.bypass, replica_groups=G8,
            ins=[q2fT_i.ap().opt()], outs=[q2fT_o.ap().opt()]), inc="scc")
        t_agwp = P.emit("gpsimd", lambda e: e.collective_compute(
            "AllGather", ALU.bypass, replica_groups=G8,
            ins=[q2pT_i.ap().opt()], outs=[q2pT_o.ap().opt()]), inc="scc")

    if stage == 1:
        if "t_rs1" in T:
            for k in range(4):
                P.waitall("gpsimd", RSK[k])
            P.dma("gpsimd", "tapA", bass.AP(T["t_rs1"], 0, [[1024, 2048], [1, 1024]]),
                  bass.AP(rsV, 0, [[1024, 2048], [1, 1024]]))
            P.waitall("gpsimd", "tapA")
        P.dma("sync", "yme", bass.AP(y, 0, [[2048, 128], [262144, 8], [1, 2048]]),
              bass.AP(x_res, 0, [[16384, 128], [2048, 8], [1, 2048]]))
        P.waitall("sync", "yme")
        return finish(nc, P)

    last_exp_par = [0, 0]
    last_pv_pb = [0, 0, 0, 0]
    last_diag_pb = [0, 0, 0, 0]
    t_oevac = 0
    head_done = [0, 0, 0, 0]
    t_vs_copy = 0
    t_suffix_last = 0
    t_vstr_prev = 0
    t_vsf_prev = 0
    first_P_use = [True]
    KQ = ("kqA", "kqB")
    ATTK = (("at0A", "at0B"), ("at1A", "at1B"), ("at2A", "at2B"), ("at3A", "at3B"))

    t_kq_done = {}
    P.waitall("sync", "tapA")
    P.wait("sync", "stt", t_qkv_mm_last)
    for k in range(4):
        bufp = k % 2
        P.wait("sync", "scc", t_ag1k[k])
        if k >= 2:
            P.wait("sync", "stt", head_done[k - 2])

        def kq_loads(e, k=k, bufp=bufp, which=1, dst=None):
            with e.register(f"rkq{k}{'KQ'[0 if which else 1]}") as ro:
                e.reg_load(ro, ms_sb[0:1, 0:1])
                e.reg_mul(ro, ro, 393216)
                e.reg_add(ro, ro, k * 6291456 + which * 131072)
                e.dma_start(out=bass.AP(dst, 4096 * bufp, [[8192, 128], [1024, 4], [1, 1024]]),
                            in_=bass.AP(g1o, ro, [[1024, 128], [1572864, 4], [1, 1024]])
                            ).then_inc(P.sems[KQ[bufp]], 16)
        P.cnt[KQ[bufp]] += 16
        P.on("sync", lambda e, k=k, bufp=bufp: kq_loads(e, k, bufp, 1, K_sb))
        P.cnt[KQ[bufp]] += 16
        P.on("sync", lambda e, k=k, bufp=bufp: kq_loads(e, k, bufp, 0, Q_sb))
        vb = V_sb if k % 2 == 0 else Vn_sb
        P.waitall("sync", RSK[k])
        for r in range(4):
            P.dma("sync", "vv", bass.AP(vb, 1152 * r, [[4608, 128], [144, 8], [1, 128]]),
                  bass.AP(rsV, r * 524288 + k * 131072, [[128, 128], [16384, 8], [1, 128]]))
        if k == 0:
            P.wait("sync", "saa", t_sign_last)
            P.dma("sync", "umx", um_sb[:, :], umask[:, :])

        P.waitall("tensor", "vv")
        if k == 0:
            P.waitall("tensor", "cn")
            P.wait("tensor", "svv", t_last_ev)
            P.wait("tensor", "svv", t_onesb)
        if t_vs_copy:
            P.wait("tensor", "svv", t_vs_copy)
        for j in range(32):
            P.emit("tensor", lambda e, j=j, vb=vb: e.matmul(
                psA[:, 3200 + j:3201 + j], vb[:, 144 * j:144 * j + 128], onesb[:, 0:1],
                start=True, stop=True), inc="stt" if j == 31 else None)
        t_vmm = P.cnt["stt"]
        P.wait("vector", "stt", t_vmm)
        if t_vstr_prev:
            P.wait("vector", "stt", t_vstr_prev)
        t_vsf_prev = P.emit("vector", lambda e: e.tensor_copy(
            vsfm[:, 0:32], psA[:, 3200:3232]), inc="svv")
        P.wait("tensor", "svv", t_vsf_prev)
        tt_tr = P.emit("tensor", lambda e: e.transpose(
            psA[0:32, 3072:3200], vsfm[:, 0:32], ident_sb[:, :]), inc="stt")
        t_vstr_prev = tt_tr
        P.wait("vector", "stt", tt_tr)
        if t_suffix_last:
            P.wait("vector", "stt", t_suffix_last)
        P.wait("vector", "svv", t_vscnt)
        t_vs_copy = P.emit("vector", lambda e: e.tensor_copy(vs_sm[:, 0:128], psA[0:32, 3072:3200]), inc="svv")

        P.waitall("tensor", KQ[bufp])
        P.waitall("tensor", "vv")
        if k == 0:
            P.wait("tensor", "svv", t_vones)
        if k == 1:
            P.wait("tensor", "svv", t_vones2)

        def emit_pvs(grp, w, exp_tgt, sel_tgt):
            P.wait("tensor", "saa", exp_tgt)
            if sel_tgt:
                P.wait("tensor", "svv", sel_tgt)
            if grp[0] == 0 and t_oevac:
                P.wait("tensor", "svv", t_oevac)
            for j in grp:
                for i in range(max(j, 4 * w), 4 * w + 4):
                    P.emit("tensor", lambda e, j=j, i=i, w=w, vb=vb: e.matmul(
                        psA[:, 512 * (i % 4):512 * (i % 4) + 129],
                        P_sb[:, 512 * (j % 4) + 128 * (i - 4 * w):512 * (j % 4) + 128 * (i - 4 * w) + 128],
                        vb[:, 144 * j:144 * j + 129],
                        start=(j == 0), stop=False),
                        inc="stt" if (j == grp[-1] and i == 4 * w + 3) else None)
            for j in grp:
                last_pv_pb[j % 4] = P.cnt["stt"]

        for w in range(8):
            groups = []
            jj = 0
            while jj + 1 <= 4 * w - 1:
                groups.append((jj, jj + 1))
                jj += 2
            while jj <= 4 * w + 3:
                groups.append((jj,))
                jj += 1
            pend = None
            for grp in groups:
                tt_qk = 0
                for j in grp:
                    par = j % 2
                    c0 = max(128 * j, 512 * w)
                    N = 512 * (w + 1) - c0
                    P.wait("tensor", "saa", last_exp_par[par])
                    tt_qk = P.emit("tensor", lambda e, par=par, bufp=bufp, j=j, c0=c0, N=N: e.matmul(
                        psA[:, 512 * (4 + par):512 * (4 + par) + N],
                        K_sb[:, 4096 * bufp + 128 * j:4096 * bufp + 128 * (j + 1)],
                        Q_sb[:, 4096 * bufp + c0:4096 * bufp + c0 + N],
                        start=True, stop=True), inc="stt")
                if pend is not None:
                    emit_pvs(*pend)
                if first_P_use[0]:
                    P.wait("scalar", "svv", t_tern_lastadd)
                    for b in range(8):
                        P.waitall("scalar", TS[b])
                    first_P_use[0] = False
                P.wait("scalar", "stt", tt_qk)
                for j in grp:
                    P.wait("scalar", "stt", last_pv_pb[j % 4])
                if len(grp) == 2:
                    jA = grp[0]
                    t_exp = P.emit("scalar", lambda e, jA=jA: e.activation(
                        P_sb[:, 512 * (jA % 4):512 * (jA % 4) + 1024],
                        psA[:, 2048:3072], AF.Exp, scale=INV64), inc="saa")
                    last_exp_par[0] = t_exp
                    last_exp_par[1] = t_exp
                    sel_tgt = 0
                else:
                    j = grp[0]
                    par = j % 2
                    c0 = max(128 * j, 512 * w)
                    N = 512 * (w + 1) - c0
                    pcol0 = c0 - 512 * w
                    t_exp = P.emit("scalar", lambda e, j=j, par=par, pcol0=pcol0, N=N: e.activation(
                        P_sb[:, 512 * (j % 4) + pcol0:512 * (j % 4) + pcol0 + N],
                        psA[:, 512 * (4 + par):512 * (4 + par) + N], AF.Exp, scale=INV64), inc="saa")
                    last_exp_par[par] = t_exp
                    sel_tgt = 0
                    if j >= 4 * w:
                        P.wait("vector", "saa", t_exp)
                        sel_tgt = P.emit("vector", lambda e, j=j, pcol0=pcol0: e.copy_predicated(
                            P_sb[:, 512 * (j % 4) + pcol0:512 * (j % 4) + pcol0 + 128],
                            itril_bf[:, :], ones2d[:, :]), inc="svv")
                pend = (grp, w, t_exp, sel_tgt)
            emit_pvs(*pend)
            P.wait("tensor", "svv", t_vs_copy)
            if k == 0 and w == 0:
                P.waitall("tensor", "umx")
                P.wait("tensor", "svv", t_ones2d)
                P.wait("tensor", "svv", t_trilbf)
            for i in range(4 * w, 4 * w + 4):
                P.emit("tensor", lambda e, i=i: e.matmul(
                    psA[:, 512 * (i % 4):512 * (i % 4) + 129],
                    um_sb[:, 128 * i:128 * (i + 1)], vs_sm[:, :],
                    start=False, stop=True), inc="stt" if i == 4 * w + 3 else None)
            t_wave = P.cnt["stt"]
            if w == 7:
                t_suffix_last = t_wave
            ab = w % 2
            P.wait("vector", "stt", t_wave)
            if t_oevac:
                P.wait("vector", "svv", t_oevac)
            for u in range(4):
                P.emit("vector", lambda e, ab=ab, u=u: e.reciprocal(
                    rc_sb[:, 4 * ab + u:4 * ab + u + 1], psA[:, 512 * u + 128:512 * u + 129]),
                    inc="svv" if u == 3 else None)
            P.wait("vector", "svv", P.cnt["svv"])
            P.waitall("vector", ATTK[k][ab])
            if w < 2 and k >= 1:
                P.waitall("vector", ATTK[k - 1][ab])
            for u in range(4):
                P.emit("vector", lambda e, ab=ab, u=u: e.tensor_scalar(
                    att_sb[:, 512 * ab + 128 * u:512 * ab + 128 * (u + 1)],
                    psA[:, 512 * u:512 * u + 128], rc_sb[:, 4 * ab + u:4 * ab + u + 1],
                    None, ALU.mult), inc="svv" if u == 3 else None)
            t_oevac = P.cnt["svv"]
            P.wait("sync", "svv", t_oevac)
            for u in range(4):
                i = 4 * w + u
                P.dma("sync", ATTK[k][ab], g2i[k, i // 8][128 * (i % 8):128 * (i % 8 + 1), :],
                      att_sb[:, 512 * ab + 128 * u:512 * ab + 128 * (u + 1)])
            t_attdma_unused = 0
        head_done[k] = P.cnt["stt"]

    t_ag2k = []
    for k in range(4):
        P.waitall("gpsimd", ATTK[k][0])
        P.waitall("gpsimd", ATTK[k][1])
        t_ag2k.append(P.emit("gpsimd", lambda e, k=k: e.collective_compute(
            "AllGather", ALU.bypass, replica_groups=G4,
            ins=[g2i[k].opt()], outs=[g2o[k].opt()]), inc="scc"))

    def go_rs2(e):
        with e.register("ro2") as ro2:
            for k in range(4):
                e.wait_ge(P.sems["scc"], t_ag2k[k])
                e.reg_load(ro2, ms_sb[0:1, 0:1])
                e.reg_mul(ro2, ro2, 131072)
                if k:
                    e.reg_add(ro2, ro2, k * 2097152)
                e.dma_start(out=bass.AP(rs2, k * 524288, [[131072, 4], [1, 131072]]),
                            in_=bass.AP(g2o, ro2, [[524288, 4], [1, 131072]])
                            ).then_inc(P.sems["rs2s"], 16)
    P.cnt["rs2s"] += 4 * 16
    P.on("gpsimd", go_rs2)
    P.wait("gpsimd", "stt", head_done[3])
    wcast(wo_d)

    if stage == 2:
        if "t_rs2" in T:
            P.waitall("gpsimd", "rs2s")
            P.dma("gpsimd", "tapA", bass.AP(T["t_rs2"], 0, [[1024, 2048], [1, 1024]]),
                  bass.AP(rs2, 0, [[1024, 2048], [1, 1024]]))
            P.waitall("gpsimd", "tapA")
        if "t_scal" in T:
            P.wait("sync", "svv", tv_scal)
            P.dma("sync", "tapB", T["t_scal"][:, :], scal_sb[:, :])
            P.waitall("sync", "tapB")
        P.dma("sync", "yme", bass.AP(y, 0, [[2048, 128], [262144, 8], [1, 2048]]),
              bass.AP(x_res, 0, [[16384, 128], [2048, 8], [1, 2048]]))
        P.waitall("sync", "yme")
        return finish(nc, P)

    P.waitall("sync", "rs2s")
    for g in range(16):
        P.dma("sync", "fmao", xn_fm[:, 1024 * g:1024 * (g + 1)], rs2[g % 4, g // 4], transpose=True)

    t_wo_ev = {}
    t_womm = 0
    P.waitall("tensor", "wc")
    P.waitall("tensor", "fmao")
    ap2_prev = [0, 0]
    for tb in range(8):
        b0 = 4 * (tb % 2)
        if tb >= 2:
            P.wait("tensor", "svv", t_wo_ev[tb - 2])
        for ic in range(16):
            for og in range(4):
                P.emit("tensor", lambda e, b0=b0, ic=ic, og=og, tb=tb: e.matmul(
                    psA[:, 512 * (b0 + og):512 * (b0 + og + 1)],
                    xn_fm[:, 1024 * ic + 128 * tb:1024 * ic + 128 * (tb + 1)],
                    w_sb[:, 2048 * ic + 512 * og:2048 * ic + 512 * (og + 1)],
                    start=(ic == 0), stop=(ic == 15)),
                    inc="stt" if (ic == 15 and og == 3) else None)
        t_womm = P.cnt["stt"]
        P.wait("vector", "stt", t_womm)
        for og in range(4):
            P.emit("vector", lambda e, b0=b0, og=og, tb=tb: e.tensor_tensor(
                x_res[:, 2048 * tb + 512 * og:2048 * tb + 512 * (og + 1)],
                psA[:, 512 * (b0 + og):512 * (b0 + og + 1)],
                x_res[:, 2048 * tb + 512 * og:2048 * tb + 512 * (og + 1)], ALU.add),
                inc="svv" if og == 3 else None)
        t_wo_ev[tb] = P.cnt["svv"]
        ln_block(tb, (lambda tb=tb: P.wait("vector", "svv", t_wo_ev[tb])), "x2t", x2t, ap2_prev)

    P.wait("sync", "stt", head_done[3])
    P.wait("sync", "scc", t_agwf)
    for hc0 in range(2):
        P.dma("sync", ("fcA", "fcB")[hc0], fcw_sb[:, 2048 * hc0:2048 * (hc0 + 1)],
              bass.AP(q2fT_o, hc0 * 262144, [[2048, 128], [1, 2048]]))
    P.waitall("scalar", "x2t")
    P.wait("scalar", "stt", t_womm)
    for ic in range(16):
        P.dma("scalar", "fmx2", xn_fm[:, 1024 * ic:1024 * (ic + 1)],
              x2t[:, 128 * ic:128 * (ic + 1)], transpose=True)

    if stage == 3:
        P.wait("sync", "svv", t_wo_ev[7])
        P.dma("sync", "yme", bass.AP(y, 0, [[2048, 128], [262144, 8], [1, 2048]]),
              bass.AP(x_res, 0, [[16384, 128], [2048, 8], [1, 2048]]))
        P.waitall("sync", "yme")
        return finish(nc, P)

    FC = ("fcA", "fcB")
    t_fc_mm = {}
    t_sq = {}
    t_relu = {}
    P.wait("sync", "scc", t_agwf)
    for hc in range(40):
        slot = hc % 2
        r, l = hc // 5, hc % 5
        if hc >= 2:
            P.wait("sync", "stt", t_fc_mm[hc - 2])
            P.dma("sync", FC[slot], fcw_sb[:, 2048 * slot:2048 * (slot + 1)],
                  bass.AP(q2fT_o, (r * 5 + l) * 262144, [[2048, 128], [1, 2048]]))
        P.waitall("tensor", FC[slot])
        P.waitall("tensor", "fmx2")
        if hc >= 2:
            P.wait("tensor", "saa", t_relu[hc - 2])
        pr = hc % 2
        for ic in range(16):
            for hf in range(2):
                P.emit("tensor", lambda e, pr=pr, hf=hf, ic=ic, slot=slot: e.matmul(
                    psA[:, 512 * (2 * pr + hf):512 * (2 * pr + hf + 1)],
                    fcw_sb[:, 2048 * slot + 128 * ic:2048 * slot + 128 * (ic + 1)],
                    xn_fm[:, 1024 * ic + 512 * hf:1024 * ic + 512 * (hf + 1)],
                    start=(ic == 0), stop=(ic == 15)),
                    inc="stt" if (ic == 15 and hf == 1) else None)
        t_fc_mm[hc] = P.cnt["stt"]
        P.wait("scalar", "stt", t_fc_mm[hc])
        if hc >= 2:
            P.wait("scalar", "svv", t_sq[hc - 2])
        for hf in range(2):
            P.emit("scalar", lambda e, pr=pr, hf=hf, slot=slot: e.activation(
                rl_sb[:, 1024 * slot + 512 * hf:1024 * slot + 512 * (hf + 1)],
                psA[:, 512 * (2 * pr + hf):512 * (2 * pr + hf + 1)],
                AF.Relu, scale=scal_sb[:, C_SFC:C_SFC + 1]),
                inc="saa" if hf == 1 else None)
        t_relu[hc] = P.cnt["saa"]
        P.wait("vector", "saa", t_relu[hc])
        t_sq[hc] = P.emit("vector", lambda e, hc=hc, slot=slot: e.tensor_tensor(
            h_sb[:, 1024 * hc:1024 * (hc + 1)],
            rl_sb[:, 1024 * slot:1024 * (slot + 1)],
            rl_sb[:, 1024 * slot:1024 * (slot + 1)], ALU.mult), inc="svv")
    t_h_done = t_sq[39]

    PR = ("prA", "prB")
    OB = ("obA", "obB")
    P.wait("sync", "scc", t_agwp)
    P.wait("sync", "svv", t_h_done)
    t_quad_ev = 0
    t_prw_mm = {}
    for th in range(2):
        for dh2 in range(2):
            qd = 2 * th + dh2
            for hc in range(40):
                slot = hc % 2
                if hc >= 2:
                    P.wait("sync", "stt", t_prw_mm[(qd, hc - 2)])
                elif qd > 0:
                    P.wait("sync", "stt", t_prw_mm[(qd - 1, 38 + hc)])
                P.dma("sync", PR[slot], bass.AP(prw_sb, 1024 * slot, [[2048, 128], [256, 4], [1, 256]]),
                      bass.AP(q2pT_o, 4 * dh2 * 1310720 + 128 * hc * 256, [[256, 128], [1310720, 4], [1, 256]]))
                P.waitall("tensor", PR[slot])
                if hc == 0:
                    P.wait("tensor", "svv", t_h_done)
                    if t_quad_ev:
                        P.wait("tensor", "svv", t_quad_ev)
                for tb in range(4):
                    tbg = 4 * th + tb
                    for bd in range(2):
                        P.emit("tensor", lambda e, tb=tb, bd=bd, hc=hc, slot=slot, tbg=tbg: e.matmul(
                            psA[:, 512 * (2 * tb + bd):512 * (2 * tb + bd + 1)],
                            h_sb[:, 1024 * hc + 128 * tbg:1024 * hc + 128 * (tbg + 1)],
                            prw_sb[:, 1024 * slot + 512 * bd:1024 * slot + 512 * (bd + 1)],
                            start=(hc == 0), stop=(hc == 39)),
                            inc="stt" if (tb == 3 and bd == 1) else None)
                t_prw_mm[(qd, hc)] = P.cnt["stt"]
            P.wait("vector", "stt", t_prw_mm[(qd, 39)])
            for bi in range(8):
                tb, bd = bi // 2, bi % 2
                tbg = 4 * th + tb
                obs = bi % 2
                P.waitall("vector", OB[obs])
                tq = P.emit("vector", lambda e, bi=bi, tbg=tbg, bd=bd, dh2=dh2, obs=obs: e.scalar_tensor_tensor(
                    ob_sb[:, 512 * obs:512 * (obs + 1)],
                    psA[:, 512 * bi:512 * (bi + 1)],
                    scal_sb[:, C_SPR:C_SPR + 1],
                    x_res[:, 2048 * tbg + 1024 * dh2 + 512 * bd:2048 * tbg + 1024 * dh2 + 512 * (bd + 1)],
                    ALU.mult, ALU.add), inc="svv")
                P.wait("sync", "svv", tq)
                P.dma("sync", OB[obs], y[128 * tbg:128 * (tbg + 1),
                                         1024 * dh2 + 512 * bd:1024 * dh2 + 512 * (bd + 1)],
                      ob_sb[:, 512 * obs:512 * (obs + 1)])
            t_quad_ev = P.cnt["svv"]

    P.waitall("sync", "obA")
    P.waitall("sync", "obB")
    return finish(nc, P)


def finish(nc, P):
    with nc.Block() as block:
        for eng in ENGINES:
            fns = P.streams[eng]

            def runner(e, fns=fns):
                for f in fns:
                    f(e)
            getattr(block, eng)(runner)
    assert nc.sbuf_base <= 17408, f"bass bump allocator grew into overlay: {nc.sbuf_base}"
    return nc


def make_consts():
    s = np.arange(128)[:, None]
    q = np.arange(128)[None, :]
    tril_T = (s <= q).astype(np.float32)
    j = np.arange(32)[:, None]
    i = (np.arange(4096)[None, :] // 128)
    um = (j > i).astype(np.float32)
    return tril_T, um, np.eye(128, dtype=np.float32)


def make_in_maps(inputs):
    x = np.asarray(inputs["L0_attn_x"])
    wq = np.ascontiguousarray(np.asarray(inputs["L0_attn_Wq"], dtype=np.float32))
    wk = np.ascontiguousarray(np.asarray(inputs["L0_attn_Wk"], dtype=np.float32))
    wv = np.ascontiguousarray(np.asarray(inputs["L0_attn_Wv"], dtype=np.float32))
    wo = np.ascontiguousarray(np.asarray(inputs["L0_attn_Wo"], dtype=np.float32))
    wfc = np.asarray(inputs["L0_ffn_w_fc"], dtype=np.float32)
    wpr = np.asarray(inputs["L0_ffn_w_proj"], dtype=np.float32)
    tril_T, um, ident = make_consts()
    maps = []
    for c in range(8):
        r = c % 4
        b = c // 4
        maps.append({
            "x_q": np.ascontiguousarray(x[b, 1024 * r:1024 * (r + 1), :], dtype=np.float32),
            "wq": wq, "wk": wk, "wv": wv, "wo": wo,
            "wfc_sl": np.ascontiguousarray(wfc[640 * c:640 * (c + 1), :]),
            "wpr_sl": np.ascontiguousarray(wpr[256 * c:256 * (c + 1), :]),
            "meta": np.array([[r]], dtype=np.uint32),
            "tril": tril_T, "umask": um, "ident": ident,
        })
    return maps


from concourse.bass_utils import run_bass_kernel_spmd

_cache = {}


def kernel(**inputs):
    if "nc" not in _cache:
        _cache["nc"] = build(stage=4, taps=())
    nc = _cache["nc"]
    maps = make_in_maps(inputs)
    res = run_bass_kernel_spmd(nc, maps, core_ids=list(range(8)))
    out = np.empty((2, 4096, 2048), np.float32)
    for c in range(8):
        r = c % 4
        out[c // 4, 1024 * r:1024 * (r + 1), :] = np.asarray(res.results[c]["y"]).reshape(1024, 2048)
    return out


# version 16
# speedup vs baseline: 1.0783x; 1.0016x over previous
import numpy as np
import ml_dtypes
import concourse.bass as bass
import concourse.mybir as mybir

F32 = mybir.dt.float32
BF16 = mybir.dt.bfloat16
U32 = mybir.dt.uint32
AF = mybir.ActivationFunctionType
ALU = mybir.AluOpType
AX = mybir.AxisListType.X
G4 = [[0, 1, 2, 3], [4, 5, 6, 7]]
G8 = [list(range(8))]
ENGINES = ("sync", "gpsimd", "tensor", "vector", "scalar")

SEMS = ("cn", "umx", "am2", "ami", "xA", "xB", "wc", "xnt", "x2t", "fmx", "fmao", "fmx2",
        "sl0", "sl1", "sl2", "sl3", "evA", "evB", "evC", "evD",
        "ts0", "ts1", "ts2", "ts3", "ts4", "ts5", "ts6", "ts7",
        "tw0", "tw1", "tw2", "tw3", "tw4", "tw5", "tw6", "tw7",
        "kqA", "kqB", "vv",
        "at0A", "at0B", "at1A", "at1B", "at2A", "at2B", "at3A", "at3B",
        "rsA", "rsB", "rsC", "rsD", "rs2s", "fcA", "fcB", "prA", "prB", "obA", "obB",
        "tapA", "tapB", "yme",
        "scc", "stt", "svv", "saa")


class Prog:
    def __init__(self, nc):
        self.nc = nc
        self.streams = {k: [] for k in ENGINES}
        self.sems = {}
        self.cnt = {}
        for s in SEMS:
            self.sems[s] = nc.semaphore(s).__enter__()
            self.cnt[s] = 0

    def on(self, eng, fn):
        self.streams[eng].append(fn)

    def wait(self, eng, sem, tgt):
        assert tgt <= self.cnt[sem], f"wait on future {sem}:{tgt}>{self.cnt[sem]}"
        if tgt <= 0:
            return
        s = self.sems[sem]
        self.on(eng, lambda e, s=s, t=tgt: e.wait_ge(s, t))

    def waitall(self, eng, sem):
        self.wait(eng, sem, self.cnt[sem])

    def emit(self, eng, fn, inc=None, k=1):
        if inc is None:
            self.on(eng, fn)
            return None
        self.cnt[inc] += k
        tgt = self.cnt[inc]
        s = self.sems[inc]
        self.on(eng, lambda e, f=fn, s=s, kk=k: f(e).then_inc(s, kk))
        return tgt

    def dma(self, eng, sem, out, in_, **kw):
        return self.emit(eng, lambda e, o=out, i=in_, kws=dict(kw): e.dma_start(out=o, in_=i, **kws),
                         inc=sem, k=16)


def build(stage=4, taps=()):
    nc = bass.Bass("TRN2", target_bir_lowering=False, debug=False)
    P = Prog(nc)

    x_q = nc.dram_tensor("x_q", [1024, 2048], F32, kind="ExternalInput")
    wq_d = nc.dram_tensor("wq", [2048, 2048], F32, kind="ExternalInput")
    wk_d = nc.dram_tensor("wk", [2048, 2048], F32, kind="ExternalInput")
    wv_d = nc.dram_tensor("wv", [2048, 2048], F32, kind="ExternalInput")
    wo_d = nc.dram_tensor("wo", [2048, 2048], F32, kind="ExternalInput")
    wfc_sl = nc.dram_tensor("wfc_sl", [640, 2048], F32, kind="ExternalInput")
    wpr_sl = nc.dram_tensor("wpr_sl", [256, 5120], F32, kind="ExternalInput")
    meta = nc.dram_tensor("meta", [1, 1], U32, kind="ExternalInput")
    tril = nc.dram_tensor("tril", [128, 128], F32, kind="ExternalInput")
    umask = nc.dram_tensor("umask", [32, 4096], F32, kind="ExternalInput")
    ident = nc.dram_tensor("ident", [128, 128], F32, kind="ExternalInput")
    y = nc.dram_tensor("y", [1024, 2048], F32, kind="ExternalOutput")

    g1i = nc.dram_tensor("g1i", [4, 4, 3, 128, 1024], BF16)
    g1o = nc.dram_tensor("g1o", [4, 4, 4, 3, 128, 1024], BF16)
    rsV = nc.dram_tensor("rsV", [4, 4, 8, 128, 128], BF16)
    xnt = nc.dram_tensor("xnt", [1024, 2048], BF16)
    x2t = nc.dram_tensor("x2t", [1024, 2048], BF16)
    g2i = nc.dram_tensor("g2i", [4, 4, 1024, 128], BF16)
    g2o = nc.dram_tensor("g2o", [4, 4, 4, 1024, 128], BF16)
    rs2 = nc.dram_tensor("rs2", [4, 4, 1024, 128], BF16)
    q2fT_i = nc.dram_tensor("q2fT_i", [5, 128, 16, 128], BF16)
    q2fT_o = nc.dram_tensor("q2fT_o", [8, 5, 128, 16, 128], BF16, addr_space="Shared")
    q2pT_i = nc.dram_tensor("q2pT_i", [5120, 256], BF16)
    q2pT_o = nc.dram_tensor("q2pT_o", [8, 5120, 256], BF16, addr_space="Shared")
    ami_d = nc.dram_tensor("ami", [2, 1], F32)
    amo_d = nc.dram_tensor("amo", [2, 1], F32)

    T = {}
    if "rs1" in taps:
        T["t_rs1"] = nc.dram_tensor("t_rs1", [4, 4, 8, 128, 128], F32, kind="ExternalOutput")
    if "rs2" in taps:
        T["t_rs2"] = nc.dram_tensor("t_rs2", [4, 4, 1024, 128], F32, kind="ExternalOutput")
    if "scal" in taps:
        T["t_scal"] = nc.dram_tensor("t_scal", [128, 8], F32, kind="ExternalOutput")

    SHIFT = 17408

    def A(name, shape, dt, offset):
        return nc.alloc_sbuf_tensor_at(name, shape, dt, offset=offset + SHIFT)
    x_res = A("x_res", [128, 16384], F32, offset=0)
    xn_fm = A("xn_fm", [128, 16384], BF16, offset=65536)
    K_sb = A("K_sb", [128, 8192], BF16, offset=65536)
    Q_sb = A("Q_sb", [128, 8192], BF16, offset=65536 + 16384)
    w_sb = A("w_sb", [128, 32768], BF16, offset=98304)
    h_sb = A("h_sb", [128, 40960], BF16, offset=98304)
    V_sb = A("V_sb", [128, 4608], BF16, offset=163840)
    xn_stage = A("xn_stage", [128, 2048], BF16, offset=173056)
    ev_sb = A("ev_sb", [128, 2048], BF16, offset=173056)
    ln_s = A("ln_s", [128, 64], F32, offset=179200)
    wslab = A("wslab", [128, 4096], F32, offset=180224)
    um_sb = A("um_sb", [32, 4096], F32, offset=180224)
    rl_sb = A("rl_sb", [128, 2048], F32, offset=180224)
    prw_sb = A("prw_sb", [128, 2048], BF16, offset=180224)
    ob_sb = A("ob_sb", [128, 1024], F32, offset=184320)
    Vn_sb = A("Vn_sb", [128, 4608], BF16, offset=98304)
    fcw_sb = A("fcw_sb", [128, 4096], BF16, offset=188416)
    P_sb = A("P_sb", [128, 2048], BF16, offset=196608)
    qa_sb = A("qa_sb", [128, 1024], BF16, offset=196608)
    qb_sb = A("qb_sb", [128, 1024], BF16, offset=198656)
    qb1_sb = A("qb1_sb", [128, 1024], BF16, offset=200704)
    att_sb = A("att_sb", [128, 1024], BF16, offset=202752)
    rc_sb = A("rc_sb", [128, 8], F32, offset=204800)
    vs_sm = A("vs_sm", [32, 129], F32, offset=204832)
    vsfm = A("vsfm", [128, 64], F32, offset=205376)
    tril_sb = A("tril_sb", [128, 128], F32, offset=206848)
    ident_sb = A("ident_sb", [128, 128], F32, offset=207360)
    scal_sb = A("scal_sb", [128, 8], F32, offset=207872)
    amred = A("amred", [128, 24], F32, offset=207904)
    vals_sb = A("vals_sb", [1, 16], F32, offset=208000)
    am2_sb = A("am2_sb", [1, 4], F32, offset=208064)
    ones_r = A("ones_r", [1, 128], F32, offset=208096)
    ones_c = A("ones_c", [128, 1], F32, offset=208640)
    ms_sb = A("ms_sb", [1, 1], U32, offset=208672)
    tsp_sb = A("tsp_sb", [128, 1280], BF16, offset=208704)
    onesb = A("onesb", [128, 1], BF16, offset=211648)
    ones2d = A("ones2d", [128, 128], BF16, offset=211264)
    itril_bf = A("itril_bf", [128, 128], mybir.dt.uint8, offset=211520)

    psA = nc.alloc_psum_tensor("psA", [128, 4096], F32)

    C_THRF, C_THRP, C_SFC, C_SPR, C_NTHRF, C_NTHRP = 2, 3, 4, 5, 6, 7
    INV64 = 1.0 / 64.0
    SL = ("sl0", "sl1", "sl2", "sl3")

    P.dma("sync", "cn", tril_sb[:, :], tril[:, :])
    P.dma("sync", "cn", ident_sb[:, :], ident[:, :])
    P.dma("gpsimd", "tapA", ms_sb[:, :], meta[:, :])
    P.emit("vector", lambda e: e.memset(ones_r[:, :], 1.0))
    P.emit("vector", lambda e: e.memset(ones_c[:, :], 1.0))
    t_vones = P.emit("vector", lambda e: e.memset(
        bass.AP(V_sb, 128, [[4608, 128], [144, 32]]), 1.0), inc="svv")
    t_vscnt = P.emit("vector", lambda e: e.memset(vs_sm[:, 128:129], 128.0), inc="svv")
    t_onesb = P.emit("vector", lambda e: e.memset(onesb[:, :], 1.0), inc="svv")

    P.dma("sync", "xA", bass.AP(x_res, 0, [[16384, 128], [2048, 4], [1, 2048]]),
          bass.AP(x_q, 0, [[2048, 128], [262144, 4], [1, 2048]]))
    P.dma("sync", "xB", bass.AP(x_res, 8192, [[16384, 128], [2048, 4], [1, 2048]]),
          bass.AP(x_q, 1048576, [[2048, 128], [262144, 4], [1, 2048]]))
    t_ones2d = P.emit("vector", lambda e: e.memset(ones2d[:, :], 1.0), inc="svv")

    def wcast(dram):
        return P.dma("gpsimd", "wc",
                     bass.AP(w_sb, 0, [[32768, 128], [2048, 16], [1, 2048]]),
                     bass.AP(dram, 0, [[2048, 128], [262144, 16], [1, 2048]]))
    def wcast_v(dram):
        for s in range(4):
            for d in range(4):
                P.dma("gpsimd", "wc",
                      bass.AP(w_sb, 512 * s + 128 * d, [[32768, 128], [2048, 16], [1, 128]]),
                      bass.AP(dram, (4 * d + s) * 128, [[2048, 128], [262144, 16], [1, 128]]))
    wcast(wq_d)

    def ln_block(tb, pre_wait, stage_sem, out_dram, ap_prev):
        sl = 32 * (tb % 2)
        pre_wait()
        if ap_prev[tb % 2]:
            P.wait("vector", "svv", ap_prev[tb % 2])
        for s4 in range(4):
            P.emit("vector", lambda e, tb=tb, sl=sl, s4=s4: e.bn_stats(
                ln_s[:, sl + 6 * s4: sl + 6 * s4 + 6],
                x_res[:, 2048 * tb + 512 * s4: 2048 * tb + 512 * (s4 + 1)]),
                inc="svv" if s4 == 3 else None)
        P.wait("vector", "svv", P.cnt["svv"])
        t2 = P.emit("vector", lambda e, sl=sl: e.bn_aggr(ln_s[:, sl + 24:sl + 26], ln_s[:, sl:sl + 24]), inc="svv")
        P.wait("vector", "svv", t2)
        t3 = P.emit("vector", lambda e, sl=sl: e.tensor_scalar(
            ln_s[:, sl + 26:sl + 27], ln_s[:, sl + 25:sl + 26], 1e-5, None, ALU.add), inc="svv")
        P.wait("vector", "svv", t3)
        t4 = P.emit("vector", lambda e, sl=sl: e.reciprocal(ln_s[:, sl + 27:sl + 28], ln_s[:, sl + 26:sl + 27]), inc="svv")
        P.wait("scalar", "svv", t4)
        t5 = P.emit("scalar", lambda e, sl=sl: e.activation(
            ln_s[:, sl + 28:sl + 29], ln_s[:, sl + 27:sl + 28], AF.Sqrt), inc="saa")
        P.wait("vector", "saa", t5)
        P.waitall("vector", stage_sem)
        t_ap = P.emit("vector", lambda e, tb=tb, sl=sl: e.tensor_scalar(
            xn_stage[:, :], x_res[:, 2048 * tb:2048 * (tb + 1)],
            ln_s[:, sl + 24:sl + 25], ln_s[:, sl + 28:sl + 29], ALU.subtract, ALU.mult), inc="svv")
        ap_prev[tb % 2] = t_ap
        P.wait("sync", "svv", t_ap)
        P.dma("sync", stage_sem, out_dram[128 * tb:128 * (tb + 1), :], xn_stage[:, :])
        return t_ap

    P.waitall("vector", "cn")
    t_trilbf = P.emit("vector", lambda e: e.tensor_scalar(
        itril_bf[:, :], tril_sb[:, :], 0.0, None, ALU.is_equal), inc="svv")
    ap_prev = [0, 0]
    for tb in range(8):
        ln_block(tb, (lambda tb=tb: P.waitall("vector", "xA" if tb < 4 else "xB")), "xnt", xnt, ap_prev)
    P.waitall("sync", "xnt")
    for ic in range(16):
        P.dma("sync", "fmx", xn_fm[:, 1024 * ic:1024 * (ic + 1)],
              xnt[:, 128 * ic:128 * (ic + 1)], transpose=True)

    pieces = [(wfc_sl, i, cp) for i in range(5) for cp in range(2)] + \
             [(wpr_sl, i, cp) for i in range(2) for cp in range(5)]
    t_red = [0] * 20
    for n, (src_d, i, cp) in enumerate(pieces):
        slot = n % 4
        if n >= 4:
            P.wait("scalar", "svv", t_red[n - 4])
        P.dma("scalar", SL[slot], wslab[:, 1024 * slot:1024 * (slot + 1)],
              src_d[128 * i:128 * (i + 1), 1024 * cp:1024 * (cp + 1)])
        P.waitall("vector", SL[slot])
        t_red[n] = P.emit("vector", lambda e, n=n: e.tensor_reduce(
            amred[:, n:n + 1], wslab[:, 1024 * (n % 4):1024 * (n % 4 + 1)],
            AX, ALU.add, apply_absolute_value=True), inc="svv")
    P.wait("vector", "svv", t_red[19])
    P.emit("vector", lambda e: e.tensor_reduce(amred[:, 20:21], amred[:, 0:10], AX, ALU.add))
    tv2 = P.emit("vector", lambda e: e.tensor_reduce(amred[:, 21:22], amred[:, 10:20], AX, ALU.add), inc="svv")
    P.wait("tensor", "svv", tv2)
    tt_am = P.emit("tensor", lambda e: e.matmul(psA[0:2, 3584:3585], amred[:, 20:22], ones_c[:, :],
                                                start=True, stop=True), inc="stt")
    P.wait("vector", "stt", tt_am)
    tv3 = P.emit("vector", lambda e: e.tensor_copy(amred[0:2, 22:23], psA[0:2, 3584:3585]), inc="svv")
    P.wait("sync", "svv", tv3)
    P.dma("sync", "ami", ami_d[:, :], amred[0:2, 22:23])
    P.waitall("gpsimd", "ami")
    t_ar = P.emit("gpsimd", lambda e: e.collective_compute(
        "AllReduce", ALU.add, replica_groups=G8,
        ins=[ami_d.ap().opt()], outs=[amo_d.ap().opt()]), inc="scc")
    t_qkv_mm = {}
    last_pair_ev = {}
    ev_sem4 = ("evA", "evB", "evC", "evD")
    t_ag1k = [0, 0, 0, 0]
    t_sign = [0] * 20
    t_tadd = [0] * 20
    TS = ("ts0", "ts1", "ts2", "ts3", "ts4", "ts5", "ts6", "ts7")
    TW = ("tw0", "tw1", "tw2", "tw3", "tw4", "tw5", "tw6", "tw7")
    ts_after = {}

    def tern_piece(n):
        src_d, i, cp = pieces[n]
        slot = n % 4
        if n < 4:
            P.wait("scalar", "svv", t_red[16 + n])
        else:
            P.wait("scalar", "saa", t_sign[n - 4])
        P.dma("scalar", SL[slot], wslab[:, 1024 * slot:1024 * (slot + 1)],
              src_d[128 * i:128 * (i + 1), 1024 * cp:1024 * (cp + 1)])
        P.waitall("scalar", SL[slot])
        qb = qb_sb if n % 2 == 0 else qb1_sb
        if n == 0:
            P.wait("scalar", "svv", tv_scal)
        else:
            P.wait("scalar", "svv", t_tadd[n - 1])
        if n >= 2:
            for b in range(8):
                P.wait("scalar", TS[b], ts_after[n - 2][b])
        cT = C_THRF if src_d is wfc_sl else C_THRP
        cN = C_NTHRF if src_d is wfc_sl else C_NTHRP
        P.emit("scalar", lambda e, slot=slot, cT=cT: e.activation(
            qa_sb[:, :], wslab[:, 1024 * slot:1024 * (slot + 1)], AF.Sign, bias=scal_sb[:, cT:cT + 1]))
        t_sign[n] = P.emit("scalar", lambda e, slot=slot, cN=cN, qb=qb: e.activation(
            qb[:, :], wslab[:, 1024 * slot:1024 * (slot + 1)], AF.Sign, bias=scal_sb[:, cN:cN + 1]), inc="saa")
        P.wait("vector", "saa", t_sign[n])
        t_tadd[n] = P.emit("vector", lambda e, qb=qb: e.tensor_tensor(
            qb[:, :], qa_sb[:, :], qb[:, :], ALU.add), inc="svv")
        P.wait("scalar", "svv", t_tadd[n])
        for b in range(8):
            P.waitall("scalar", TW[b])
            P.dma("scalar", TS[b], tsp_sb[:, 128 * b:128 * (b + 1)],
                  qb[:, 128 * b:128 * (b + 1)], transpose=True)
        ts_after[n] = [P.cnt[TS[b]] for b in range(8)]
        for b in range(8):
            P.waitall("gpsimd", TS[b])
            if src_d is wfc_sl:
                dst = bass.AP(q2fT_i, i * 262144 + (8 * cp + b) * 128, [[2048, 128], [1, 128]])
            else:
                dst = bass.AP(q2pT_i, (8 * cp + b) * 32768 + 128 * i, [[256, 128], [1, 128]])
            P.dma("gpsimd", TW[b], dst, tsp_sb[:, 128 * b:128 * (b + 1)])

    for p in range(2):
        P.waitall("tensor", "wc")
        if p == 0:
            P.waitall("tensor", "fmx")
        for oc in range(16):
            it = 16 * p + oc
            pr = oc % 2
            if pr in last_pair_ev:
                P.wait("tensor", "svv", last_pair_ev[pr])
            for ic in range(16):
                for hf in range(2):
                    P.emit("tensor", lambda e, pr=pr, hf=hf, ic=ic, oc=oc: e.matmul(
                        psA[:, 512 * (2 * pr + hf):512 * (2 * pr + hf + 1)],
                        w_sb[:, 2048 * ic + 128 * oc:2048 * ic + 128 * (oc + 1)],
                        xn_fm[:, 1024 * ic + 512 * hf:1024 * ic + 512 * (hf + 1)],
                        start=(ic == 0), stop=(ic == 15)),
                        inc="stt" if (ic == 15 and hf == 1) else None)
            t_qkv_mm[(p, oc)] = P.cnt["stt"]
            evs = "evA" if it % 2 == 0 else "evB"
            ecol = 1024 * (it % 2)
            P.wait("vector", "stt", t_qkv_mm[(p, oc)])
            if it == 0:
                P.waitall("vector", "xnt")
            P.waitall("vector", evs)
            for hf in range(2):
                P.emit("vector", lambda e, pr=pr, hf=hf, ecol=ecol: e.tensor_copy(
                    ev_sb[:, ecol + 512 * hf:ecol + 512 * (hf + 1)],
                    psA[:, 512 * (2 * pr + hf):512 * (2 * pr + hf + 1)]),
                    inc="svv" if hf == 1 else None)
            t_ev = P.cnt["svv"]
            last_pair_ev[pr] = t_ev
            P.wait("sync", "svv", t_ev)
            P.dma("sync", evs, g1i[oc % 4, oc // 4, p], ev_sb[:, ecol:ecol + 1024])
            if p == 1:
                for n in range(20):
                    if n * 16 // 20 == oc:
                        tern_piece(n)
        P.wait("gpsimd", "stt", t_qkv_mm[(p, 15)])
        if p == 0:
            wcast(wk_d)
            P.wait("sync", "scc", t_ar)
            P.dma("sync", "am2", am2_sb[0:1, 0:2], amo_d[:, :])
            P.waitall("vector", "am2")
            u1 = P.emit("vector", lambda e: e.tensor_scalar(
                vals_sb[0:1, 0:2], am2_sb[0:1, 0:2], 1.0 / 10485760.0, 1e-5, ALU.mult, ALU.max), inc="svv")
            P.wait("vector", "svv", u1)
            P.emit("vector", lambda e: e.tensor_scalar(vals_sb[0:1, 2:4], vals_sb[0:1, 0:2], 0.7, None, ALU.mult))
            u3 = P.emit("vector", lambda e: e.tensor_scalar(vals_sb[0:1, 4:6], vals_sb[0:1, 0:2], 0.5, None, ALU.mult), inc="svv")
            P.wait("vector", "svv", u3)
            u4 = P.emit("vector", lambda e: e.tensor_scalar(vals_sb[0:1, 6:8], vals_sb[0:1, 2:4], -1.0, None, ALU.mult), inc="svv")
            P.wait("tensor", "svv", u4)
            P.wait("tensor", "svv", tv3)
            tt_bc = P.emit("tensor", lambda e: e.matmul(psA[:, 3584:3592], ones_r[:, :], vals_sb[0:1, 0:8],
                                                        start=True, stop=True), inc="stt")
            P.wait("vector", "stt", tt_bc)
            tv_scal = P.emit("vector", lambda e: e.tensor_copy(scal_sb[:, :], psA[:, 3584:3592]), inc="svv")
        else:
            wcast_v(wv_d)
    t_tern_lastadd = t_tadd[19]
    t_sign_last = t_sign[19]

    P.waitall("tensor", "wc")
    P.waitall("tensor", "ami")
    last_pair6 = {}
    t_last_ev = 0
    for u in range(32):
        s, tb = u // 8, u % 8
        pr6 = u % 8
        if pr6 in last_pair6:
            P.wait("tensor", "svv", last_pair6[pr6])
        for ic in range(16):
            P.emit("tensor", lambda e, pr6=pr6, ic=ic, s=s, tb=tb: e.matmul(
                psA[:, 512 * pr6:512 * (pr6 + 1)],
                xn_fm[:, 1024 * ic + 128 * tb:1024 * ic + 128 * (tb + 1)],
                w_sb[:, 2048 * ic + 512 * s:2048 * ic + 512 * (s + 1)],
                start=(ic == 0), stop=(ic == 15)),
                inc="stt" if ic == 15 else None)
        t_qkv_mm[(2, u)] = P.cnt["stt"]
        es = ev_sem4[u % 4]
        ecol = 512 * (u % 4)
        P.wait("vector", "stt", t_qkv_mm[(2, u)])
        if u < 4:
            P.waitall("vector", "evA")
            P.waitall("vector", "evB")
        P.waitall("vector", es)
        t_ev = P.emit("vector", lambda e, pr6=pr6, ecol=ecol: e.tensor_copy(
            ev_sb[:, ecol:ecol + 512], psA[:, 512 * pr6:512 * (pr6 + 1)]), inc="svv")
        last_pair6[pr6] = t_ev
        t_last_ev = t_ev
        P.wait("sync", "svv", t_ev)
        for d in range(4):
            P.dma("sync", es,
                  bass.AP(g1i, (4 * s + d) * 393216 + 262144 + tb * 16384, [[128, 128], [1, 128]]),
                  bass.AP(ev_sb, ecol + 128 * d, [[2048, 128], [1, 128]]))
        if tb == 7:
            for esem in ev_sem4:
                P.waitall("gpsimd", esem)
            t_ag1k[s] = P.emit("gpsimd", lambda e, s=s: e.collective_compute(
                "AllGather", ALU.bypass, replica_groups=G4,
                ins=[g1i[s].opt()], outs=[g1o[s].opt()]), inc="scc")
    t_qkv_mm_last = t_qkv_mm[(2, 31)]
    P.wait("vector", "stt", t_qkv_mm_last)
    t_vones2 = P.emit("vector", lambda e: e.memset(
        bass.AP(Vn_sb, 128, [[4608, 128], [144, 32]]), 1.0), inc="svv")

    P.waitall("gpsimd", "tapA")
    RSK = ("rsA", "rsB", "rsC", "rsD")

    def go_rs1(e):
        for k in range(4):
            e.wait_ge(P.sems["scc"], t_ag1k[k])
            with e.register(f"rV{k}") as rV:
                e.reg_load(rV, ms_sb[0:1, 0:1])
                e.reg_mul(rV, rV, 393216)
                e.reg_add(rV, rV, k * 6291456 + 262144)
                e.dma_start(out=bass.AP(rsV, k * 131072, [[524288, 4], [1, 131072]]),
                            in_=bass.AP(g1o, rV, [[1572864, 4], [1, 131072]])
                            ).then_inc(P.sems[RSK[k]], 16)
    for k in range(4):
        P.cnt[RSK[k]] += 16
    P.on("gpsimd", go_rs1)


    if stage >= 2:
        for b in range(8):
            P.waitall("gpsimd", TW[b])
        t_agwf = P.emit("gpsimd", lambda e: e.collective_compute(
            "AllGather", ALU.bypass, replica_groups=G8,
            ins=[q2fT_i.ap().opt()], outs=[q2fT_o.ap().opt()]), inc="scc")
        t_agwp = P.emit("gpsimd", lambda e: e.collective_compute(
            "AllGather", ALU.bypass, replica_groups=G8,
            ins=[q2pT_i.ap().opt()], outs=[q2pT_o.ap().opt()]), inc="scc")

    if stage == 1:
        if "t_rs1" in T:
            for k in range(4):
                P.waitall("gpsimd", RSK[k])
            P.dma("gpsimd", "tapA", bass.AP(T["t_rs1"], 0, [[1024, 2048], [1, 1024]]),
                  bass.AP(rsV, 0, [[1024, 2048], [1, 1024]]))
            P.waitall("gpsimd", "tapA")
        P.dma("sync", "yme", bass.AP(y, 0, [[2048, 128], [262144, 8], [1, 2048]]),
              bass.AP(x_res, 0, [[16384, 128], [2048, 8], [1, 2048]]))
        P.waitall("sync", "yme")
        return finish(nc, P)

    last_exp_par = [0, 0]
    last_pv_pb = [0, 0, 0, 0]
    last_diag_pb = [0, 0, 0, 0]
    t_oevac = 0
    head_done = [0, 0, 0, 0]
    t_vs_copy = 0
    t_suffix_last = 0
    t_vstr_prev = 0
    t_vsf_prev = 0
    first_P_use = [True]
    KQ = ("kqA", "kqB")
    ATTK = (("at0A", "at0B"), ("at1A", "at1B"), ("at2A", "at2B"), ("at3A", "at3B"))

    t_kq_done = {}
    P.waitall("sync", "tapA")
    P.wait("sync", "stt", t_qkv_mm_last)
    for k in range(4):
        bufp = k % 2
        P.wait("sync", "scc", t_ag1k[k])
        if k >= 2:
            P.wait("sync", "stt", head_done[k - 2])

        def kq_loads(e, k=k, bufp=bufp, which=1, dst=None):
            with e.register(f"rkq{k}{'KQ'[0 if which else 1]}") as ro:
                e.reg_load(ro, ms_sb[0:1, 0:1])
                e.reg_mul(ro, ro, 393216)
                e.reg_add(ro, ro, k * 6291456 + which * 131072)
                e.dma_start(out=bass.AP(dst, 4096 * bufp, [[8192, 128], [1024, 4], [1, 1024]]),
                            in_=bass.AP(g1o, ro, [[1024, 128], [1572864, 4], [1, 1024]])
                            ).then_inc(P.sems[KQ[bufp]], 16)
        P.cnt[KQ[bufp]] += 16
        P.on("sync", lambda e, k=k, bufp=bufp: kq_loads(e, k, bufp, 1, K_sb))
        P.cnt[KQ[bufp]] += 16
        P.on("sync", lambda e, k=k, bufp=bufp: kq_loads(e, k, bufp, 0, Q_sb))
        vb = V_sb if k % 2 == 0 else Vn_sb
        P.waitall("sync", RSK[k])
        for r in range(4):
            P.dma("sync", "vv", bass.AP(vb, 1152 * r, [[4608, 128], [144, 8], [1, 128]]),
                  bass.AP(rsV, r * 524288 + k * 131072, [[128, 128], [16384, 8], [1, 128]]))
        if k == 0:
            P.wait("sync", "saa", t_sign_last)
            P.dma("sync", "umx", um_sb[:, :], umask[:, :])

        P.waitall("tensor", "vv")
        if k == 0:
            P.waitall("tensor", "cn")
            P.wait("tensor", "svv", t_last_ev)
            P.wait("tensor", "svv", t_onesb)
        if t_vs_copy:
            P.wait("tensor", "svv", t_vs_copy)
        for j in range(32):
            P.emit("tensor", lambda e, j=j, vb=vb: e.matmul(
                psA[:, 3200 + j:3201 + j], vb[:, 144 * j:144 * j + 128], onesb[:, 0:1],
                start=True, stop=True), inc="stt" if j == 31 else None)
        t_vmm = P.cnt["stt"]
        P.wait("vector", "stt", t_vmm)
        if t_vstr_prev:
            P.wait("vector", "stt", t_vstr_prev)
        t_vsf_prev = P.emit("vector", lambda e: e.tensor_copy(
            vsfm[:, 0:32], psA[:, 3200:3232]), inc="svv")
        P.wait("tensor", "svv", t_vsf_prev)
        tt_tr = P.emit("tensor", lambda e: e.transpose(
            psA[0:32, 3072:3200], vsfm[:, 0:32], ident_sb[:, :]), inc="stt")
        t_vstr_prev = tt_tr
        P.wait("vector", "stt", tt_tr)
        if t_suffix_last:
            P.wait("vector", "stt", t_suffix_last)
        P.wait("vector", "svv", t_vscnt)
        t_vs_copy = P.emit("vector", lambda e: e.tensor_copy(vs_sm[:, 0:128], psA[0:32, 3072:3200]), inc="svv")

        P.waitall("tensor", KQ[bufp])
        P.waitall("tensor", "vv")
        if k == 0:
            P.wait("tensor", "svv", t_vones)
        if k == 1:
            P.wait("tensor", "svv", t_vones2)

        def emit_pvs(grp, w, exp_tgt, sel_tgt):
            P.wait("tensor", "saa", exp_tgt)
            if sel_tgt:
                P.wait("tensor", "svv", sel_tgt)
            if grp[0] == 0 and t_oevac:
                P.wait("tensor", "svv", t_oevac)
            for j in grp:
                for i in range(max(j, 4 * w), 4 * w + 4):
                    P.emit("tensor", lambda e, j=j, i=i, w=w, vb=vb: e.matmul(
                        psA[:, 512 * (i % 4):512 * (i % 4) + 129],
                        P_sb[:, 512 * (j % 4) + 128 * (i - 4 * w):512 * (j % 4) + 128 * (i - 4 * w) + 128],
                        vb[:, 144 * j:144 * j + 129],
                        start=(j == 0), stop=False),
                        inc="stt" if (j == grp[-1] and i == 4 * w + 3) else None)
            for j in grp:
                last_pv_pb[j % 4] = P.cnt["stt"]

        for w in range(8):
            groups = []
            jj = 0
            while jj + 1 <= 4 * w - 1:
                groups.append((jj, jj + 1))
                jj += 2
            while jj <= 4 * w + 3:
                groups.append((jj,))
                jj += 1
            pend = None
            for grp in groups:
                tt_qk = 0
                for j in grp:
                    par = j % 2
                    c0 = max(128 * j, 512 * w)
                    N = 512 * (w + 1) - c0
                    P.wait("tensor", "saa", last_exp_par[par])
                    tt_qk = P.emit("tensor", lambda e, par=par, bufp=bufp, j=j, c0=c0, N=N: e.matmul(
                        psA[:, 512 * (4 + par):512 * (4 + par) + N],
                        K_sb[:, 4096 * bufp + 128 * j:4096 * bufp + 128 * (j + 1)],
                        Q_sb[:, 4096 * bufp + c0:4096 * bufp + c0 + N],
                        start=True, stop=True), inc="stt")
                if pend is not None:
                    emit_pvs(*pend)
                if first_P_use[0]:
                    P.wait("scalar", "svv", t_tern_lastadd)
                    for b in range(8):
                        P.waitall("scalar", TS[b])
                    first_P_use[0] = False
                P.wait("scalar", "stt", tt_qk)
                for j in grp:
                    P.wait("scalar", "stt", last_pv_pb[j % 4])
                if len(grp) == 2:
                    jA = grp[0]
                    t_exp = P.emit("scalar", lambda e, jA=jA: e.activation(
                        P_sb[:, 512 * (jA % 4):512 * (jA % 4) + 1024],
                        psA[:, 2048:3072], AF.Exp, scale=INV64), inc="saa")
                    last_exp_par[0] = t_exp
                    last_exp_par[1] = t_exp
                    sel_tgt = 0
                else:
                    j = grp[0]
                    par = j % 2
                    c0 = max(128 * j, 512 * w)
                    N = 512 * (w + 1) - c0
                    pcol0 = c0 - 512 * w
                    t_exp = P.emit("scalar", lambda e, j=j, par=par, pcol0=pcol0, N=N: e.activation(
                        P_sb[:, 512 * (j % 4) + pcol0:512 * (j % 4) + pcol0 + N],
                        psA[:, 512 * (4 + par):512 * (4 + par) + N], AF.Exp, scale=INV64), inc="saa")
                    last_exp_par[par] = t_exp
                    sel_tgt = 0
                    if j >= 4 * w:
                        P.wait("vector", "saa", t_exp)
                        sel_tgt = P.emit("vector", lambda e, j=j, pcol0=pcol0: e.copy_predicated(
                            P_sb[:, 512 * (j % 4) + pcol0:512 * (j % 4) + pcol0 + 128],
                            itril_bf[:, :], ones2d[:, :]), inc="svv")
                pend = (grp, w, t_exp, sel_tgt)
            emit_pvs(*pend)
            P.wait("tensor", "svv", t_vs_copy)
            if k == 0 and w == 0:
                P.waitall("tensor", "umx")
                P.wait("tensor", "svv", t_ones2d)
                P.wait("tensor", "svv", t_trilbf)
            for i in range(4 * w, 4 * w + 4):
                P.emit("tensor", lambda e, i=i: e.matmul(
                    psA[:, 512 * (i % 4):512 * (i % 4) + 129],
                    um_sb[:, 128 * i:128 * (i + 1)], vs_sm[:, :],
                    start=False, stop=True), inc="stt" if i == 4 * w + 3 else None)
            t_wave = P.cnt["stt"]
            if w == 7:
                t_suffix_last = t_wave
            ab = w % 2
            P.wait("vector", "stt", t_wave)
            if t_oevac:
                P.wait("vector", "svv", t_oevac)
            for u in range(4):
                P.emit("vector", lambda e, ab=ab, u=u: e.reciprocal(
                    rc_sb[:, 4 * ab + u:4 * ab + u + 1], psA[:, 512 * u + 128:512 * u + 129]),
                    inc="svv" if u == 3 else None)
            P.wait("vector", "svv", P.cnt["svv"])
            P.waitall("vector", ATTK[k][ab])
            if w < 2 and k >= 1:
                P.waitall("vector", ATTK[k - 1][ab])
            for u in range(4):
                P.emit("vector", lambda e, ab=ab, u=u: e.tensor_scalar(
                    att_sb[:, 512 * ab + 128 * u:512 * ab + 128 * (u + 1)],
                    psA[:, 512 * u:512 * u + 128], rc_sb[:, 4 * ab + u:4 * ab + u + 1],
                    None, ALU.mult), inc="svv" if u == 3 else None)
            t_oevac = P.cnt["svv"]
            P.wait("sync", "svv", t_oevac)
            for u in range(4):
                i = 4 * w + u
                P.dma("sync", ATTK[k][ab], g2i[k, i // 8][128 * (i % 8):128 * (i % 8 + 1), :],
                      att_sb[:, 512 * ab + 128 * u:512 * ab + 128 * (u + 1)])
            t_attdma_unused = 0
        head_done[k] = P.cnt["stt"]

    t_ag2k = []
    for k in range(4):
        P.waitall("gpsimd", ATTK[k][0])
        P.waitall("gpsimd", ATTK[k][1])
        t_ag2k.append(P.emit("gpsimd", lambda e, k=k: e.collective_compute(
            "AllGather", ALU.bypass, replica_groups=G4,
            ins=[g2i[k].opt()], outs=[g2o[k].opt()]), inc="scc"))

    def go_rs2(e):
        with e.register("ro2") as ro2:
            for k in range(4):
                e.wait_ge(P.sems["scc"], t_ag2k[k])
                e.reg_load(ro2, ms_sb[0:1, 0:1])
                e.reg_mul(ro2, ro2, 131072)
                if k:
                    e.reg_add(ro2, ro2, k * 2097152)
                e.dma_start(out=bass.AP(rs2, k * 524288, [[131072, 4], [1, 131072]]),
                            in_=bass.AP(g2o, ro2, [[524288, 4], [1, 131072]])
                            ).then_inc(P.sems["rs2s"], 16)
    P.cnt["rs2s"] += 4 * 16
    P.on("gpsimd", go_rs2)
    P.wait("gpsimd", "stt", head_done[3])
    wcast(wo_d)

    if stage == 2:
        if "t_rs2" in T:
            P.waitall("gpsimd", "rs2s")
            P.dma("gpsimd", "tapA", bass.AP(T["t_rs2"], 0, [[1024, 2048], [1, 1024]]),
                  bass.AP(rs2, 0, [[1024, 2048], [1, 1024]]))
            P.waitall("gpsimd", "tapA")
        if "t_scal" in T:
            P.wait("sync", "svv", tv_scal)
            P.dma("sync", "tapB", T["t_scal"][:, :], scal_sb[:, :])
            P.waitall("sync", "tapB")
        P.dma("sync", "yme", bass.AP(y, 0, [[2048, 128], [262144, 8], [1, 2048]]),
              bass.AP(x_res, 0, [[16384, 128], [2048, 8], [1, 2048]]))
        P.waitall("sync", "yme")
        return finish(nc, P)

    P.waitall("sync", "rs2s")
    for g in range(16):
        P.dma("sync", "fmao", xn_fm[:, 1024 * g:1024 * (g + 1)], rs2[g % 4, g // 4], transpose=True)

    t_wo_ev = {}
    t_womm = 0
    P.waitall("tensor", "wc")
    P.waitall("tensor", "fmao")
    ap2_prev = [0, 0]
    for tb in range(8):
        b0 = 4 * (tb % 2)
        if tb >= 2:
            P.wait("tensor", "svv", t_wo_ev[tb - 2])
        for ic in range(16):
            for og in range(4):
                P.emit("tensor", lambda e, b0=b0, ic=ic, og=og, tb=tb: e.matmul(
                    psA[:, 512 * (b0 + og):512 * (b0 + og + 1)],
                    xn_fm[:, 1024 * ic + 128 * tb:1024 * ic + 128 * (tb + 1)],
                    w_sb[:, 2048 * ic + 512 * og:2048 * ic + 512 * (og + 1)],
                    start=(ic == 0), stop=(ic == 15)),
                    inc="stt" if (ic == 15 and og == 3) else None)
        t_womm = P.cnt["stt"]
        P.wait("vector", "stt", t_womm)
        for og in range(4):
            P.emit("vector", lambda e, b0=b0, og=og, tb=tb: e.tensor_tensor(
                x_res[:, 2048 * tb + 512 * og:2048 * tb + 512 * (og + 1)],
                psA[:, 512 * (b0 + og):512 * (b0 + og + 1)],
                x_res[:, 2048 * tb + 512 * og:2048 * tb + 512 * (og + 1)], ALU.add),
                inc="svv" if og == 3 else None)
        t_wo_ev[tb] = P.cnt["svv"]
        ln_block(tb, (lambda tb=tb: P.wait("vector", "svv", t_wo_ev[tb])), "x2t", x2t, ap2_prev)

    P.wait("sync", "stt", head_done[3])
    P.wait("sync", "scc", t_agwf)
    for hc0 in range(2):
        P.dma("sync", ("fcA", "fcB")[hc0], fcw_sb[:, 2048 * hc0:2048 * (hc0 + 1)],
              bass.AP(q2fT_o, hc0 * 262144, [[2048, 128], [1, 2048]]))
    P.waitall("scalar", "x2t")
    P.wait("scalar", "stt", t_womm)
    for ic in range(16):
        P.dma("scalar", "fmx2", xn_fm[:, 1024 * ic:1024 * (ic + 1)],
              x2t[:, 128 * ic:128 * (ic + 1)], transpose=True)

    if stage == 3:
        P.wait("sync", "svv", t_wo_ev[7])
        P.dma("sync", "yme", bass.AP(y, 0, [[2048, 128], [262144, 8], [1, 2048]]),
              bass.AP(x_res, 0, [[16384, 128], [2048, 8], [1, 2048]]))
        P.waitall("sync", "yme")
        return finish(nc, P)

    FC = ("fcA", "fcB")
    t_fc_mm = {}
    t_sq = {}
    t_relu = {}
    P.wait("sync", "scc", t_agwf)
    for hc in range(40):
        slot = hc % 2
        r, l = hc // 5, hc % 5
        if hc >= 2:
            P.wait("sync", "stt", t_fc_mm[hc - 2])
            P.dma("sync", FC[slot], fcw_sb[:, 2048 * slot:2048 * (slot + 1)],
                  bass.AP(q2fT_o, (r * 5 + l) * 262144, [[2048, 128], [1, 2048]]))
        P.waitall("tensor", FC[slot])
        P.waitall("tensor", "fmx2")
        if hc >= 2:
            P.wait("tensor", "saa", t_relu[hc - 2])
        pr = hc % 2
        for ic in range(16):
            for hf in range(2):
                P.emit("tensor", lambda e, pr=pr, hf=hf, ic=ic, slot=slot: e.matmul(
                    psA[:, 512 * (2 * pr + hf):512 * (2 * pr + hf + 1)],
                    fcw_sb[:, 2048 * slot + 128 * ic:2048 * slot + 128 * (ic + 1)],
                    xn_fm[:, 1024 * ic + 512 * hf:1024 * ic + 512 * (hf + 1)],
                    start=(ic == 0), stop=(ic == 15)),
                    inc="stt" if (ic == 15 and hf == 1) else None)
        t_fc_mm[hc] = P.cnt["stt"]
        P.wait("scalar", "stt", t_fc_mm[hc])
        if hc >= 2:
            P.wait("scalar", "svv", t_sq[hc - 2])
        for hf in range(2):
            P.emit("scalar", lambda e, pr=pr, hf=hf, slot=slot: e.activation(
                rl_sb[:, 1024 * slot + 512 * hf:1024 * slot + 512 * (hf + 1)],
                psA[:, 512 * (2 * pr + hf):512 * (2 * pr + hf + 1)],
                AF.Relu, scale=scal_sb[:, C_SFC:C_SFC + 1]),
                inc="saa" if hf == 1 else None)
        t_relu[hc] = P.cnt["saa"]
        P.wait("vector", "saa", t_relu[hc])
        t_sq[hc] = P.emit("vector", lambda e, hc=hc, slot=slot: e.tensor_tensor(
            h_sb[:, 1024 * hc:1024 * (hc + 1)],
            rl_sb[:, 1024 * slot:1024 * (slot + 1)],
            rl_sb[:, 1024 * slot:1024 * (slot + 1)], ALU.mult), inc="svv")
    t_h_done = t_sq[39]

    PR = ("prA", "prB")
    OB = ("obA", "obB")
    P.wait("sync", "scc", t_agwp)
    P.wait("sync", "svv", t_h_done)
    t_quad_ev = 0
    t_prw_mm = {}
    for th in range(2):
        for dh2 in range(2):
            qd = 2 * th + dh2
            for hc in range(40):
                slot = hc % 2
                if hc >= 2:
                    P.wait("sync", "stt", t_prw_mm[(qd, hc - 2)])
                elif qd > 0:
                    P.wait("sync", "stt", t_prw_mm[(qd - 1, 38 + hc)])
                P.dma("sync", PR[slot], bass.AP(prw_sb, 1024 * slot, [[2048, 128], [256, 4], [1, 256]]),
                      bass.AP(q2pT_o, 4 * dh2 * 1310720 + 128 * hc * 256, [[256, 128], [1310720, 4], [1, 256]]))
                P.waitall("tensor", PR[slot])
                if hc == 0:
                    P.wait("tensor", "svv", t_h_done)
                    if t_quad_ev:
                        P.wait("tensor", "svv", t_quad_ev)
                for tb in range(4):
                    tbg = 4 * th + tb
                    for bd in range(2):
                        P.emit("tensor", lambda e, tb=tb, bd=bd, hc=hc, slot=slot, tbg=tbg: e.matmul(
                            psA[:, 512 * (2 * tb + bd):512 * (2 * tb + bd + 1)],
                            h_sb[:, 1024 * hc + 128 * tbg:1024 * hc + 128 * (tbg + 1)],
                            prw_sb[:, 1024 * slot + 512 * bd:1024 * slot + 512 * (bd + 1)],
                            start=(hc == 0), stop=(hc == 39)),
                            inc="stt" if (tb == 3 and bd == 1) else None)
                t_prw_mm[(qd, hc)] = P.cnt["stt"]
            P.wait("vector", "stt", t_prw_mm[(qd, 39)])
            for bi in range(8):
                tb, bd = bi // 2, bi % 2
                tbg = 4 * th + tb
                obs = bi % 2
                P.waitall("vector", OB[obs])
                tq = P.emit("vector", lambda e, bi=bi, tbg=tbg, bd=bd, dh2=dh2, obs=obs: e.scalar_tensor_tensor(
                    ob_sb[:, 512 * obs:512 * (obs + 1)],
                    psA[:, 512 * bi:512 * (bi + 1)],
                    scal_sb[:, C_SPR:C_SPR + 1],
                    x_res[:, 2048 * tbg + 1024 * dh2 + 512 * bd:2048 * tbg + 1024 * dh2 + 512 * (bd + 1)],
                    ALU.mult, ALU.add), inc="svv")
                P.wait("sync", "svv", tq)
                P.dma("sync", OB[obs], y[128 * tbg:128 * (tbg + 1),
                                         1024 * dh2 + 512 * bd:1024 * dh2 + 512 * (bd + 1)],
                      ob_sb[:, 512 * obs:512 * (obs + 1)])
            t_quad_ev = P.cnt["svv"]

    P.waitall("sync", "obA")
    P.waitall("sync", "obB")
    return finish(nc, P)


def finish(nc, P):
    with nc.Block() as block:
        for eng in ENGINES:
            fns = P.streams[eng]

            def runner(e, fns=fns):
                for f in fns:
                    f(e)
            getattr(block, eng)(runner)
    assert nc.sbuf_base <= 17408, f"bass bump allocator grew into overlay: {nc.sbuf_base}"
    return nc


def make_consts():
    s = np.arange(128)[:, None]
    q = np.arange(128)[None, :]
    tril_T = (s <= q).astype(np.float32)
    j = np.arange(32)[:, None]
    i = (np.arange(4096)[None, :] // 128)
    um = (j > i).astype(np.float32)
    return tril_T, um, np.eye(128, dtype=np.float32)


def make_in_maps(inputs):
    x = np.asarray(inputs["L0_attn_x"])
    wq = np.ascontiguousarray(np.asarray(inputs["L0_attn_Wq"], dtype=np.float32))
    wk = np.ascontiguousarray(np.asarray(inputs["L0_attn_Wk"], dtype=np.float32))
    wv = np.ascontiguousarray(np.asarray(inputs["L0_attn_Wv"], dtype=np.float32))
    wo = np.ascontiguousarray(np.asarray(inputs["L0_attn_Wo"], dtype=np.float32))
    wfc = np.asarray(inputs["L0_ffn_w_fc"], dtype=np.float32)
    wpr = np.asarray(inputs["L0_ffn_w_proj"], dtype=np.float32)
    tril_T, um, ident = make_consts()
    maps = []
    for c in range(8):
        r = c % 4
        b = c // 4
        maps.append({
            "x_q": np.ascontiguousarray(x[b, 1024 * r:1024 * (r + 1), :], dtype=np.float32),
            "wq": wq, "wk": wk, "wv": wv, "wo": wo,
            "wfc_sl": np.ascontiguousarray(wfc[640 * c:640 * (c + 1), :]),
            "wpr_sl": np.ascontiguousarray(wpr[256 * c:256 * (c + 1), :]),
            "meta": np.array([[r]], dtype=np.uint32),
            "tril": tril_T, "umask": um, "ident": ident,
        })
    return maps


from concourse.bass_utils import run_bass_kernel_spmd

_cache = {}


def kernel(**inputs):
    if "nc" not in _cache:
        _cache["nc"] = build(stage=4, taps=())
    nc = _cache["nc"]
    maps = make_in_maps(inputs)
    res = run_bass_kernel_spmd(nc, maps, core_ids=list(range(8)))
    out = np.empty((2, 4096, 2048), np.float32)
    for c in range(8):
        r = c % 4
        out[c // 4, 1024 * r:1024 * (r + 1), :] = np.asarray(res.results[c]["y"]).reshape(1024, 2048)
    return out


# version 17
# speedup vs baseline: 1.0861x; 1.0072x over previous
import numpy as np
import ml_dtypes
import concourse.bass as bass
import concourse.mybir as mybir

F32 = mybir.dt.float32
BF16 = mybir.dt.bfloat16
U32 = mybir.dt.uint32
AF = mybir.ActivationFunctionType
ALU = mybir.AluOpType
AX = mybir.AxisListType.X
G4 = [[0, 1, 2, 3], [4, 5, 6, 7]]
G8 = [list(range(8))]
ENGINES = ("sync", "gpsimd", "tensor", "vector", "scalar")

SEMS = ("cn", "umx", "am2", "ami", "xA", "xB", "wc", "xnt", "x2t", "fmx", "fmao", "fmx2",
        "sl0", "sl1", "sl2", "sl3", "evA", "evB", "evC", "evD",
        "wv0", "wv1", "wv2", "wv3",
        "ts0", "ts1", "ts2", "ts3", "ts4", "ts5", "ts6", "ts7",
        "tw0", "tw1", "tw2", "tw3", "tw4", "tw5", "tw6", "tw7",
        "kqA", "kqB", "vv",
        "at0A", "at0B", "at1A", "at1B", "at2A", "at2B", "at3A", "at3B",
        "rsA", "rsB", "rsC", "rsD", "rs2s", "fcA", "fcB", "prA", "prB", "obA", "obB",
        "tapA", "tapB", "yme",
        "scc", "stt", "svv", "saa")


class Prog:
    def __init__(self, nc):
        self.nc = nc
        self.streams = {k: [] for k in ENGINES}
        self.sems = {}
        self.cnt = {}
        for s in SEMS:
            self.sems[s] = nc.semaphore(s).__enter__()
            self.cnt[s] = 0

    def on(self, eng, fn):
        self.streams[eng].append(fn)

    def wait(self, eng, sem, tgt):
        assert tgt <= self.cnt[sem], f"wait on future {sem}:{tgt}>{self.cnt[sem]}"
        if tgt <= 0:
            return
        s = self.sems[sem]
        self.on(eng, lambda e, s=s, t=tgt: e.wait_ge(s, t))

    def waitall(self, eng, sem):
        self.wait(eng, sem, self.cnt[sem])

    def emit(self, eng, fn, inc=None, k=1):
        if inc is None:
            self.on(eng, fn)
            return None
        self.cnt[inc] += k
        tgt = self.cnt[inc]
        s = self.sems[inc]
        self.on(eng, lambda e, f=fn, s=s, kk=k: f(e).then_inc(s, kk))
        return tgt

    def dma(self, eng, sem, out, in_, **kw):
        return self.emit(eng, lambda e, o=out, i=in_, kws=dict(kw): e.dma_start(out=o, in_=i, **kws),
                         inc=sem, k=16)


def build(stage=4, taps=()):
    nc = bass.Bass("TRN2", target_bir_lowering=False, debug=False)
    P = Prog(nc)

    x_q = nc.dram_tensor("x_q", [1024, 2048], F32, kind="ExternalInput")
    wq_d = nc.dram_tensor("wq", [2048, 2048], F32, kind="ExternalInput")
    wk_d = nc.dram_tensor("wk", [2048, 2048], F32, kind="ExternalInput")
    wv_d = nc.dram_tensor("wv", [2048, 2048], F32, kind="ExternalInput")
    wo_d = nc.dram_tensor("wo", [2048, 2048], F32, kind="ExternalInput")
    wfc_sl = nc.dram_tensor("wfc_sl", [640, 2048], F32, kind="ExternalInput")
    wpr_sl = nc.dram_tensor("wpr_sl", [256, 5120], F32, kind="ExternalInput")
    meta = nc.dram_tensor("meta", [1, 1], U32, kind="ExternalInput")
    tril = nc.dram_tensor("tril", [128, 128], F32, kind="ExternalInput")
    umask = nc.dram_tensor("umask", [32, 4096], F32, kind="ExternalInput")
    ident = nc.dram_tensor("ident", [128, 128], F32, kind="ExternalInput")
    y = nc.dram_tensor("y", [1024, 2048], F32, kind="ExternalOutput")

    g1i = nc.dram_tensor("g1i", [4, 4, 3, 128, 1024], BF16)
    g1o = nc.dram_tensor("g1o", [4, 4, 4, 3, 128, 1024], BF16)
    rsV = nc.dram_tensor("rsV", [4, 4, 8, 128, 128], BF16)
    xnt = nc.dram_tensor("xnt", [1024, 2048], BF16)
    x2t = nc.dram_tensor("x2t", [1024, 2048], BF16)
    g2i = nc.dram_tensor("g2i", [4, 4, 1024, 128], BF16)
    g2o = nc.dram_tensor("g2o", [4, 4, 4, 1024, 128], BF16)
    rs2 = nc.dram_tensor("rs2", [4, 4, 1024, 128], BF16)
    q2fT_i = nc.dram_tensor("q2fT_i", [5, 128, 16, 128], BF16)
    q2fT_o = nc.dram_tensor("q2fT_o", [8, 5, 128, 16, 128], BF16, addr_space="Shared")
    q2pT_i = nc.dram_tensor("q2pT_i", [5120, 256], BF16)
    q2pT_o = nc.dram_tensor("q2pT_o", [8, 5120, 256], BF16, addr_space="Shared")
    ami_d = nc.dram_tensor("ami", [2, 1], F32)
    amo_d = nc.dram_tensor("amo", [2, 1], F32)

    T = {}
    if "rs1" in taps:
        T["t_rs1"] = nc.dram_tensor("t_rs1", [4, 4, 8, 128, 128], F32, kind="ExternalOutput")
    if "rs2" in taps:
        T["t_rs2"] = nc.dram_tensor("t_rs2", [4, 4, 1024, 128], F32, kind="ExternalOutput")
    if "scal" in taps:
        T["t_scal"] = nc.dram_tensor("t_scal", [128, 8], F32, kind="ExternalOutput")

    SHIFT = 17408

    def A(name, shape, dt, offset):
        return nc.alloc_sbuf_tensor_at(name, shape, dt, offset=offset + SHIFT)
    x_res = A("x_res", [128, 16384], F32, offset=0)
    xn_fm = A("xn_fm", [128, 16384], BF16, offset=65536)
    K_sb = A("K_sb", [128, 8192], BF16, offset=65536)
    Q_sb = A("Q_sb", [128, 8192], BF16, offset=65536 + 16384)
    w_sb = A("w_sb", [128, 32768], BF16, offset=98304)
    h_sb = A("h_sb", [128, 40960], BF16, offset=98304)
    V_sb = A("V_sb", [128, 4608], BF16, offset=163840)
    xn_stage = A("xn_stage", [128, 2048], BF16, offset=173056)
    ev_sb = A("ev_sb", [128, 2048], BF16, offset=173056)
    ln_s = A("ln_s", [128, 64], F32, offset=179200)
    wslab = A("wslab", [128, 4096], F32, offset=180224)
    um_sb = A("um_sb", [32, 4096], F32, offset=180224)
    rl_sb = A("rl_sb", [128, 2048], F32, offset=180224)
    prw_sb = A("prw_sb", [128, 2048], BF16, offset=180224)
    ob_sb = A("ob_sb", [128, 1024], F32, offset=184320)
    Vn_sb = A("Vn_sb", [128, 4608], BF16, offset=98304)
    fcw_sb = A("fcw_sb", [128, 4096], BF16, offset=188416)
    P_sb = A("P_sb", [128, 2048], BF16, offset=196608)
    qa_sb = A("qa_sb", [128, 1024], BF16, offset=196608)
    qb_sb = A("qb_sb", [128, 1024], BF16, offset=198656)
    qb1_sb = A("qb1_sb", [128, 1024], BF16, offset=200704)
    att_sb = A("att_sb", [128, 1024], BF16, offset=202752)
    rc_sb = A("rc_sb", [128, 8], F32, offset=204800)
    vs_sm = A("vs_sm", [32, 129], F32, offset=204832)
    vsfm = A("vsfm", [128, 64], F32, offset=205376)
    tril_sb = A("tril_sb", [128, 128], F32, offset=206848)
    ident_sb = A("ident_sb", [128, 128], F32, offset=207360)
    scal_sb = A("scal_sb", [128, 8], F32, offset=207872)
    amred = A("amred", [128, 24], F32, offset=207904)
    vals_sb = A("vals_sb", [1, 16], F32, offset=208000)
    am2_sb = A("am2_sb", [1, 4], F32, offset=208064)
    ones_r = A("ones_r", [1, 128], F32, offset=208096)
    ones_c = A("ones_c", [128, 1], F32, offset=208640)
    ms_sb = A("ms_sb", [1, 1], U32, offset=208672)
    tsp_sb = A("tsp_sb", [128, 1280], BF16, offset=208704)
    onesb = A("onesb", [128, 1], BF16, offset=211648)
    ones2d = A("ones2d", [128, 128], BF16, offset=211264)
    itril_bf = A("itril_bf", [128, 128], mybir.dt.uint8, offset=211520)

    psA = nc.alloc_psum_tensor("psA", [128, 4096], F32)

    C_THRF, C_THRP, C_SFC, C_SPR, C_NTHRF, C_NTHRP = 2, 3, 4, 5, 6, 7
    INV64 = 1.0 / 64.0
    SL = ("sl0", "sl1", "sl2", "sl3")

    P.dma("sync", "cn", tril_sb[:, :], tril[:, :])
    P.dma("sync", "cn", ident_sb[:, :], ident[:, :])
    P.dma("gpsimd", "tapA", ms_sb[:, :], meta[:, :])
    P.emit("vector", lambda e: e.memset(ones_r[:, :], 1.0))
    P.emit("vector", lambda e: e.memset(ones_c[:, :], 1.0))
    t_vones = P.emit("vector", lambda e: e.memset(
        bass.AP(V_sb, 128, [[4608, 128], [144, 32]]), 1.0), inc="svv")
    t_vscnt = P.emit("vector", lambda e: e.memset(vs_sm[:, 128:129], 128.0), inc="svv")
    t_onesb = P.emit("vector", lambda e: e.memset(onesb[:, :], 1.0), inc="svv")

    P.dma("sync", "xA", bass.AP(x_res, 0, [[16384, 128], [2048, 4], [1, 2048]]),
          bass.AP(x_q, 0, [[2048, 128], [262144, 4], [1, 2048]]))
    P.dma("sync", "xB", bass.AP(x_res, 8192, [[16384, 128], [2048, 4], [1, 2048]]),
          bass.AP(x_q, 1048576, [[2048, 128], [262144, 4], [1, 2048]]))
    t_ones2d = P.emit("vector", lambda e: e.memset(ones2d[:, :], 1.0), inc="svv")

    def wcast(dram):
        return P.dma("gpsimd", "wc",
                     bass.AP(w_sb, 0, [[32768, 128], [2048, 16], [1, 2048]]),
                     bass.AP(dram, 0, [[2048, 128], [262144, 16], [1, 2048]]))
    WVS = ("wv0", "wv1", "wv2", "wv3")

    def wcast_v(dram):
        for s in range(4):
            for d in range(4):
                P.dma("gpsimd", WVS[s],
                      bass.AP(w_sb, 512 * s + 128 * d, [[32768, 128], [2048, 16], [1, 128]]),
                      bass.AP(dram, (4 * d + s) * 128, [[2048, 128], [262144, 16], [1, 128]]))
    wcast(wq_d)

    def ln_block(tb, pre_wait, stage_sem, out_dram, ap_prev):
        sl = 32 * (tb % 2)
        pre_wait()
        if ap_prev[tb % 2]:
            P.wait("vector", "svv", ap_prev[tb % 2])
        for s4 in range(4):
            P.emit("vector", lambda e, tb=tb, sl=sl, s4=s4: e.bn_stats(
                ln_s[:, sl + 6 * s4: sl + 6 * s4 + 6],
                x_res[:, 2048 * tb + 512 * s4: 2048 * tb + 512 * (s4 + 1)]),
                inc="svv" if s4 == 3 else None)
        P.wait("vector", "svv", P.cnt["svv"])
        t2 = P.emit("vector", lambda e, sl=sl: e.bn_aggr(ln_s[:, sl + 24:sl + 26], ln_s[:, sl:sl + 24]), inc="svv")
        P.wait("vector", "svv", t2)
        t3 = P.emit("vector", lambda e, sl=sl: e.tensor_scalar(
            ln_s[:, sl + 26:sl + 27], ln_s[:, sl + 25:sl + 26], 1e-5, None, ALU.add), inc="svv")
        P.wait("vector", "svv", t3)
        t4 = P.emit("vector", lambda e, sl=sl: e.reciprocal(ln_s[:, sl + 27:sl + 28], ln_s[:, sl + 26:sl + 27]), inc="svv")
        P.wait("scalar", "svv", t4)
        t5 = P.emit("scalar", lambda e, sl=sl: e.activation(
            ln_s[:, sl + 28:sl + 29], ln_s[:, sl + 27:sl + 28], AF.Sqrt), inc="saa")
        P.wait("vector", "saa", t5)
        P.waitall("vector", stage_sem)
        t_ap = P.emit("vector", lambda e, tb=tb, sl=sl: e.tensor_scalar(
            xn_stage[:, :], x_res[:, 2048 * tb:2048 * (tb + 1)],
            ln_s[:, sl + 24:sl + 25], ln_s[:, sl + 28:sl + 29], ALU.subtract, ALU.mult), inc="svv")
        ap_prev[tb % 2] = t_ap
        P.wait("sync", "svv", t_ap)
        P.dma("sync", stage_sem, out_dram[128 * tb:128 * (tb + 1), :], xn_stage[:, :])
        return t_ap

    P.waitall("vector", "cn")
    t_trilbf = P.emit("vector", lambda e: e.tensor_scalar(
        itril_bf[:, :], tril_sb[:, :], 0.0, None, ALU.is_equal), inc="svv")
    ap_prev = [0, 0]
    for tb in range(8):
        ln_block(tb, (lambda tb=tb: P.waitall("vector", "xA" if tb < 4 else "xB")), "xnt", xnt, ap_prev)
    P.waitall("sync", "xnt")
    for ic in range(16):
        P.dma("sync", "fmx", xn_fm[:, 1024 * ic:1024 * (ic + 1)],
              xnt[:, 128 * ic:128 * (ic + 1)], transpose=True)

    pieces = [(wfc_sl, i, cp) for i in range(5) for cp in range(2)] + \
             [(wpr_sl, i, cp) for i in range(2) for cp in range(5)]
    t_red = [0] * 20
    for n, (src_d, i, cp) in enumerate(pieces):
        slot = n % 4
        if n >= 4:
            P.wait("scalar", "svv", t_red[n - 4])
        P.dma("scalar", SL[slot], wslab[:, 1024 * slot:1024 * (slot + 1)],
              src_d[128 * i:128 * (i + 1), 1024 * cp:1024 * (cp + 1)])
        P.waitall("vector", SL[slot])
        t_red[n] = P.emit("vector", lambda e, n=n: e.tensor_reduce(
            amred[:, n:n + 1], wslab[:, 1024 * (n % 4):1024 * (n % 4 + 1)],
            AX, ALU.add, apply_absolute_value=True), inc="svv")
    P.wait("vector", "svv", t_red[19])
    P.emit("vector", lambda e: e.tensor_reduce(amred[:, 20:21], amred[:, 0:10], AX, ALU.add))
    tv2 = P.emit("vector", lambda e: e.tensor_reduce(amred[:, 21:22], amred[:, 10:20], AX, ALU.add), inc="svv")
    P.wait("tensor", "svv", tv2)
    tt_am = P.emit("tensor", lambda e: e.matmul(psA[0:2, 3584:3585], amred[:, 20:22], ones_c[:, :],
                                                start=True, stop=True), inc="stt")
    P.wait("vector", "stt", tt_am)
    tv3 = P.emit("vector", lambda e: e.tensor_copy(amred[0:2, 22:23], psA[0:2, 3584:3585]), inc="svv")
    P.wait("sync", "svv", tv3)
    P.dma("sync", "ami", ami_d[:, :], amred[0:2, 22:23])
    P.waitall("gpsimd", "ami")
    t_ar = P.emit("gpsimd", lambda e: e.collective_compute(
        "AllReduce", ALU.add, replica_groups=G8,
        ins=[ami_d.ap().opt()], outs=[amo_d.ap().opt()]), inc="scc")
    t_qkv_mm = {}
    last_pair_ev = {}
    ev_sem4 = ("evA", "evB", "evC", "evD")
    t_ag1k = [0, 0, 0, 0]
    t_sign = [0] * 20
    t_tadd = [0] * 20
    TS = ("ts0", "ts1", "ts2", "ts3", "ts4", "ts5", "ts6", "ts7")
    TW = ("tw0", "tw1", "tw2", "tw3", "tw4", "tw5", "tw6", "tw7")
    ts_after = {}

    def tern_piece(n):
        src_d, i, cp = pieces[n]
        slot = n % 4
        if n < 4:
            P.wait("scalar", "svv", t_red[16 + n])
        else:
            P.wait("scalar", "saa", t_sign[n - 4])
        P.dma("scalar", SL[slot], wslab[:, 1024 * slot:1024 * (slot + 1)],
              src_d[128 * i:128 * (i + 1), 1024 * cp:1024 * (cp + 1)])
        P.waitall("scalar", SL[slot])
        qb = qb_sb if n % 2 == 0 else qb1_sb
        if n == 0:
            P.wait("scalar", "svv", tv_scal)
        else:
            P.wait("scalar", "svv", t_tadd[n - 1])
        if n >= 2:
            for b in range(8):
                P.wait("scalar", TS[b], ts_after[n - 2][b])
        cT = C_THRF if src_d is wfc_sl else C_THRP
        cN = C_NTHRF if src_d is wfc_sl else C_NTHRP
        P.emit("scalar", lambda e, slot=slot, cT=cT: e.activation(
            qa_sb[:, :], wslab[:, 1024 * slot:1024 * (slot + 1)], AF.Sign, bias=scal_sb[:, cT:cT + 1]))
        t_sign[n] = P.emit("scalar", lambda e, slot=slot, cN=cN, qb=qb: e.activation(
            qb[:, :], wslab[:, 1024 * slot:1024 * (slot + 1)], AF.Sign, bias=scal_sb[:, cN:cN + 1]), inc="saa")
        P.wait("vector", "saa", t_sign[n])
        t_tadd[n] = P.emit("vector", lambda e, qb=qb: e.tensor_tensor(
            qb[:, :], qa_sb[:, :], qb[:, :], ALU.add), inc="svv")
        P.wait("scalar", "svv", t_tadd[n])
        for b in range(8):
            P.waitall("scalar", TW[b])
            P.dma("scalar", TS[b], tsp_sb[:, 128 * b:128 * (b + 1)],
                  qb[:, 128 * b:128 * (b + 1)], transpose=True)
        ts_after[n] = [P.cnt[TS[b]] for b in range(8)]
        for b in range(8):
            P.waitall("gpsimd", TS[b])
            if src_d is wfc_sl:
                dst = bass.AP(q2fT_i, i * 262144 + (8 * cp + b) * 128, [[2048, 128], [1, 128]])
            else:
                dst = bass.AP(q2pT_i, (8 * cp + b) * 32768 + 128 * i, [[256, 128], [1, 128]])
            P.dma("gpsimd", TW[b], dst, tsp_sb[:, 128 * b:128 * (b + 1)])

    for p in range(2):
        P.waitall("tensor", "wc")
        if p == 0:
            P.waitall("tensor", "fmx")
        for oc in range(16):
            it = 16 * p + oc
            pr = oc % 2
            if pr in last_pair_ev:
                P.wait("tensor", "svv", last_pair_ev[pr])
            for ic in range(16):
                for hf in range(2):
                    P.emit("tensor", lambda e, pr=pr, hf=hf, ic=ic, oc=oc: e.matmul(
                        psA[:, 512 * (2 * pr + hf):512 * (2 * pr + hf + 1)],
                        w_sb[:, 2048 * ic + 128 * oc:2048 * ic + 128 * (oc + 1)],
                        xn_fm[:, 1024 * ic + 512 * hf:1024 * ic + 512 * (hf + 1)],
                        start=(ic == 0), stop=(ic == 15)),
                        inc="stt" if (ic == 15 and hf == 1) else None)
            t_qkv_mm[(p, oc)] = P.cnt["stt"]
            evs = "evA" if it % 2 == 0 else "evB"
            ecol = 1024 * (it % 2)
            P.wait("vector", "stt", t_qkv_mm[(p, oc)])
            if it == 0:
                P.waitall("vector", "xnt")
            P.waitall("vector", evs)
            for hf in range(2):
                P.emit("vector", lambda e, pr=pr, hf=hf, ecol=ecol: e.tensor_copy(
                    ev_sb[:, ecol + 512 * hf:ecol + 512 * (hf + 1)],
                    psA[:, 512 * (2 * pr + hf):512 * (2 * pr + hf + 1)]),
                    inc="svv" if hf == 1 else None)
            t_ev = P.cnt["svv"]
            last_pair_ev[pr] = t_ev
            P.wait("sync", "svv", t_ev)
            P.dma("sync", evs, g1i[oc % 4, oc // 4, p], ev_sb[:, ecol:ecol + 1024])
            if p == 1:
                for n in range(20):
                    if n * 16 // 20 == oc:
                        tern_piece(n)
        P.wait("gpsimd", "stt", t_qkv_mm[(p, 15)])
        if p == 0:
            wcast(wk_d)
            P.wait("sync", "scc", t_ar)
            P.dma("sync", "am2", am2_sb[0:1, 0:2], amo_d[:, :])
            P.waitall("vector", "am2")
            u1 = P.emit("vector", lambda e: e.tensor_scalar(
                vals_sb[0:1, 0:2], am2_sb[0:1, 0:2], 1.0 / 10485760.0, 1e-5, ALU.mult, ALU.max), inc="svv")
            P.wait("vector", "svv", u1)
            P.emit("vector", lambda e: e.tensor_scalar(vals_sb[0:1, 2:4], vals_sb[0:1, 0:2], 0.7, None, ALU.mult))
            u3 = P.emit("vector", lambda e: e.tensor_scalar(vals_sb[0:1, 4:6], vals_sb[0:1, 0:2], 0.5, None, ALU.mult), inc="svv")
            P.wait("vector", "svv", u3)
            u4 = P.emit("vector", lambda e: e.tensor_scalar(vals_sb[0:1, 6:8], vals_sb[0:1, 2:4], -1.0, None, ALU.mult), inc="svv")
            P.wait("tensor", "svv", u4)
            P.wait("tensor", "svv", tv3)
            tt_bc = P.emit("tensor", lambda e: e.matmul(psA[:, 3584:3592], ones_r[:, :], vals_sb[0:1, 0:8],
                                                        start=True, stop=True), inc="stt")
            P.wait("vector", "stt", tt_bc)
            tv_scal = P.emit("vector", lambda e: e.tensor_copy(scal_sb[:, :], psA[:, 3584:3592]), inc="svv")
        else:
            wcast_v(wv_d)
    t_tern_lastadd = t_tadd[19]
    t_sign_last = t_sign[19]

    P.waitall("tensor", "ami")
    last_pair6 = {}
    t_last_ev = 0
    for u in range(32):
        s, tb = u // 8, u % 8
        if tb == 0:
            P.waitall("tensor", WVS[s])
        pr6 = u % 8
        if pr6 in last_pair6:
            P.wait("tensor", "svv", last_pair6[pr6])
        for ic in range(16):
            P.emit("tensor", lambda e, pr6=pr6, ic=ic, s=s, tb=tb: e.matmul(
                psA[:, 512 * pr6:512 * (pr6 + 1)],
                xn_fm[:, 1024 * ic + 128 * tb:1024 * ic + 128 * (tb + 1)],
                w_sb[:, 2048 * ic + 512 * s:2048 * ic + 512 * (s + 1)],
                start=(ic == 0), stop=(ic == 15)),
                inc="stt" if ic == 15 else None)
        t_qkv_mm[(2, u)] = P.cnt["stt"]
        es = ev_sem4[u % 4]
        ecol = 512 * (u % 4)
        P.wait("vector", "stt", t_qkv_mm[(2, u)])
        if u < 4:
            P.waitall("vector", "evA")
            P.waitall("vector", "evB")
        P.waitall("vector", es)
        t_ev = P.emit("vector", lambda e, pr6=pr6, ecol=ecol: e.tensor_copy(
            ev_sb[:, ecol:ecol + 512], psA[:, 512 * pr6:512 * (pr6 + 1)]), inc="svv")
        last_pair6[pr6] = t_ev
        t_last_ev = t_ev
        P.wait("sync", "svv", t_ev)
        for d in range(4):
            P.dma("sync", es,
                  bass.AP(g1i, (4 * s + d) * 393216 + 262144 + tb * 16384, [[128, 128], [1, 128]]),
                  bass.AP(ev_sb, ecol + 128 * d, [[2048, 128], [1, 128]]))
        if tb == 7:
            for esem in ev_sem4:
                P.waitall("gpsimd", esem)
            t_ag1k[s] = P.emit("gpsimd", lambda e, s=s: e.collective_compute(
                "AllGather", ALU.bypass, replica_groups=G4,
                ins=[g1i[s].opt()], outs=[g1o[s].opt()]), inc="scc")
    t_qkv_mm_last = t_qkv_mm[(2, 31)]
    P.wait("vector", "stt", t_qkv_mm_last)
    t_vones2 = P.emit("vector", lambda e: e.memset(
        bass.AP(Vn_sb, 128, [[4608, 128], [144, 32]]), 1.0), inc="svv")

    P.waitall("gpsimd", "tapA")
    RSK = ("rsA", "rsB", "rsC", "rsD")

    def go_rs1(e):
        for k in range(4):
            e.wait_ge(P.sems["scc"], t_ag1k[k])
            with e.register(f"rV{k}") as rV:
                e.reg_load(rV, ms_sb[0:1, 0:1])
                e.reg_mul(rV, rV, 393216)
                e.reg_add(rV, rV, k * 6291456 + 262144)
                e.dma_start(out=bass.AP(rsV, k * 131072, [[524288, 4], [1, 131072]]),
                            in_=bass.AP(g1o, rV, [[1572864, 4], [1, 131072]])
                            ).then_inc(P.sems[RSK[k]], 16)
    for k in range(4):
        P.cnt[RSK[k]] += 16
    P.on("gpsimd", go_rs1)


    if stage >= 2:
        for b in range(8):
            P.waitall("gpsimd", TW[b])
        t_agwf = P.emit("gpsimd", lambda e: e.collective_compute(
            "AllGather", ALU.bypass, replica_groups=G8,
            ins=[q2fT_i.ap().opt()], outs=[q2fT_o.ap().opt()]), inc="scc")
        t_agwp = P.emit("gpsimd", lambda e: e.collective_compute(
            "AllGather", ALU.bypass, replica_groups=G8,
            ins=[q2pT_i.ap().opt()], outs=[q2pT_o.ap().opt()]), inc="scc")

    if stage == 1:
        if "t_rs1" in T:
            for k in range(4):
                P.waitall("gpsimd", RSK[k])
            P.dma("gpsimd", "tapA", bass.AP(T["t_rs1"], 0, [[1024, 2048], [1, 1024]]),
                  bass.AP(rsV, 0, [[1024, 2048], [1, 1024]]))
            P.waitall("gpsimd", "tapA")
        P.dma("sync", "yme", bass.AP(y, 0, [[2048, 128], [262144, 8], [1, 2048]]),
              bass.AP(x_res, 0, [[16384, 128], [2048, 8], [1, 2048]]))
        P.waitall("sync", "yme")
        return finish(nc, P)

    last_exp_par = [0, 0]
    last_pv_pb = [0, 0, 0, 0]
    last_diag_pb = [0, 0, 0, 0]
    t_oevac = 0
    head_done = [0, 0, 0, 0]
    t_vs_copy = 0
    t_suffix_last = 0
    t_vstr_prev = 0
    t_vsf_prev = 0
    first_P_use = [True]
    KQ = ("kqA", "kqB")
    ATTK = (("at0A", "at0B"), ("at1A", "at1B"), ("at2A", "at2B"), ("at3A", "at3B"))

    t_kq_done = {}
    P.waitall("sync", "tapA")
    P.wait("sync", "stt", t_qkv_mm_last)
    for k in range(4):
        bufp = k % 2
        P.wait("sync", "scc", t_ag1k[k])
        if k >= 2:
            P.wait("sync", "stt", head_done[k - 2])

        def kq_loads(e, k=k, bufp=bufp, which=1, dst=None):
            with e.register(f"rkq{k}{'KQ'[0 if which else 1]}") as ro:
                e.reg_load(ro, ms_sb[0:1, 0:1])
                e.reg_mul(ro, ro, 393216)
                e.reg_add(ro, ro, k * 6291456 + which * 131072)
                e.dma_start(out=bass.AP(dst, 4096 * bufp, [[8192, 128], [1024, 4], [1, 1024]]),
                            in_=bass.AP(g1o, ro, [[1024, 128], [1572864, 4], [1, 1024]])
                            ).then_inc(P.sems[KQ[bufp]], 16)
        P.cnt[KQ[bufp]] += 16
        P.on("sync", lambda e, k=k, bufp=bufp: kq_loads(e, k, bufp, 1, K_sb))
        P.cnt[KQ[bufp]] += 16
        P.on("sync", lambda e, k=k, bufp=bufp: kq_loads(e, k, bufp, 0, Q_sb))
        vb = V_sb if k % 2 == 0 else Vn_sb
        P.waitall("sync", RSK[k])
        for r in range(4):
            P.dma("sync", "vv", bass.AP(vb, 1152 * r, [[4608, 128], [144, 8], [1, 128]]),
                  bass.AP(rsV, r * 524288 + k * 131072, [[128, 128], [16384, 8], [1, 128]]))
        if k == 0:
            P.wait("sync", "saa", t_sign_last)
            P.dma("sync", "umx", um_sb[:, :], umask[:, :])

        P.waitall("tensor", "vv")
        if k == 0:
            P.waitall("tensor", "cn")
            P.wait("tensor", "svv", t_last_ev)
            P.wait("tensor", "svv", t_onesb)
        if t_vs_copy:
            P.wait("tensor", "svv", t_vs_copy)
        for j in range(32):
            P.emit("tensor", lambda e, j=j, vb=vb: e.matmul(
                psA[:, 3200 + j:3201 + j], vb[:, 144 * j:144 * j + 128], onesb[:, 0:1],
                start=True, stop=True), inc="stt" if j == 31 else None)
        t_vmm = P.cnt["stt"]
        P.wait("vector", "stt", t_vmm)
        if t_vstr_prev:
            P.wait("vector", "stt", t_vstr_prev)
        t_vsf_prev = P.emit("vector", lambda e: e.tensor_copy(
            vsfm[:, 0:32], psA[:, 3200:3232]), inc="svv")
        P.wait("tensor", "svv", t_vsf_prev)
        tt_tr = P.emit("tensor", lambda e: e.transpose(
            psA[0:32, 3072:3200], vsfm[:, 0:32], ident_sb[:, :]), inc="stt")
        t_vstr_prev = tt_tr
        P.wait("vector", "stt", tt_tr)
        if t_suffix_last:
            P.wait("vector", "stt", t_suffix_last)
        P.wait("vector", "svv", t_vscnt)
        t_vs_copy = P.emit("vector", lambda e: e.tensor_copy(vs_sm[:, 0:128], psA[0:32, 3072:3200]), inc="svv")

        P.waitall("tensor", KQ[bufp])
        P.waitall("tensor", "vv")
        if k == 0:
            P.wait("tensor", "svv", t_vones)
        if k == 1:
            P.wait("tensor", "svv", t_vones2)

        def emit_pvs(grp, w, exp_tgt, sel_tgt):
            P.wait("tensor", "saa", exp_tgt)
            if sel_tgt:
                P.wait("tensor", "svv", sel_tgt)
            if grp[0] == 0 and t_oevac:
                P.wait("tensor", "svv", t_oevac)
            for j in grp:
                for i in range(max(j, 4 * w), 4 * w + 4):
                    P.emit("tensor", lambda e, j=j, i=i, w=w, vb=vb: e.matmul(
                        psA[:, 512 * (i % 4):512 * (i % 4) + 129],
                        P_sb[:, 512 * (j % 4) + 128 * (i - 4 * w):512 * (j % 4) + 128 * (i - 4 * w) + 128],
                        vb[:, 144 * j:144 * j + 129],
                        start=(j == 0), stop=False),
                        inc="stt" if (j == grp[-1] and i == 4 * w + 3) else None)
            for j in grp:
                last_pv_pb[j % 4] = P.cnt["stt"]

        for w in range(8):
            groups = []
            jj = 0
            while jj + 1 <= 4 * w - 1:
                groups.append((jj, jj + 1))
                jj += 2
            while jj <= 4 * w + 3:
                groups.append((jj,))
                jj += 1
            pend = None
            for grp in groups:
                tt_qk = 0
                for j in grp:
                    par = j % 2
                    c0 = max(128 * j, 512 * w)
                    N = 512 * (w + 1) - c0
                    P.wait("tensor", "saa", last_exp_par[par])
                    tt_qk = P.emit("tensor", lambda e, par=par, bufp=bufp, j=j, c0=c0, N=N: e.matmul(
                        psA[:, 512 * (4 + par):512 * (4 + par) + N],
                        K_sb[:, 4096 * bufp + 128 * j:4096 * bufp + 128 * (j + 1)],
                        Q_sb[:, 4096 * bufp + c0:4096 * bufp + c0 + N],
                        start=True, stop=True), inc="stt")
                if pend is not None:
                    emit_pvs(*pend)
                if first_P_use[0]:
                    P.wait("scalar", "svv", t_tern_lastadd)
                    for b in range(8):
                        P.waitall("scalar", TS[b])
                    first_P_use[0] = False
                P.wait("scalar", "stt", tt_qk)
                for j in grp:
                    P.wait("scalar", "stt", last_pv_pb[j % 4])
                if len(grp) == 2:
                    jA = grp[0]
                    t_exp = P.emit("scalar", lambda e, jA=jA: e.activation(
                        P_sb[:, 512 * (jA % 4):512 * (jA % 4) + 1024],
                        psA[:, 2048:3072], AF.Exp, scale=INV64), inc="saa")
                    last_exp_par[0] = t_exp
                    last_exp_par[1] = t_exp
                    sel_tgt = 0
                else:
                    j = grp[0]
                    par = j % 2
                    c0 = max(128 * j, 512 * w)
                    N = 512 * (w + 1) - c0
                    pcol0 = c0 - 512 * w
                    t_exp = P.emit("scalar", lambda e, j=j, par=par, pcol0=pcol0, N=N: e.activation(
                        P_sb[:, 512 * (j % 4) + pcol0:512 * (j % 4) + pcol0 + N],
                        psA[:, 512 * (4 + par):512 * (4 + par) + N], AF.Exp, scale=INV64), inc="saa")
                    last_exp_par[par] = t_exp
                    sel_tgt = 0
                    if j >= 4 * w:
                        P.wait("vector", "saa", t_exp)
                        sel_tgt = P.emit("vector", lambda e, j=j, pcol0=pcol0: e.copy_predicated(
                            P_sb[:, 512 * (j % 4) + pcol0:512 * (j % 4) + pcol0 + 128],
                            itril_bf[:, :], ones2d[:, :]), inc="svv")
                pend = (grp, w, t_exp, sel_tgt)
            emit_pvs(*pend)
            P.wait("tensor", "svv", t_vs_copy)
            if k == 0 and w == 0:
                P.waitall("tensor", "umx")
                P.wait("tensor", "svv", t_ones2d)
                P.wait("tensor", "svv", t_trilbf)
            for i in range(4 * w, 4 * w + 4):
                P.emit("tensor", lambda e, i=i: e.matmul(
                    psA[:, 512 * (i % 4):512 * (i % 4) + 129],
                    um_sb[:, 128 * i:128 * (i + 1)], vs_sm[:, :],
                    start=False, stop=True), inc="stt" if i == 4 * w + 3 else None)
            t_wave = P.cnt["stt"]
            if w == 7:
                t_suffix_last = t_wave
            ab = w % 2
            P.wait("vector", "stt", t_wave)
            if t_oevac:
                P.wait("vector", "svv", t_oevac)
            for u in range(4):
                P.emit("vector", lambda e, ab=ab, u=u: e.reciprocal(
                    rc_sb[:, 4 * ab + u:4 * ab + u + 1], psA[:, 512 * u + 128:512 * u + 129]),
                    inc="svv" if u == 3 else None)
            P.wait("vector", "svv", P.cnt["svv"])
            P.waitall("vector", ATTK[k][ab])
            if w < 2 and k >= 1:
                P.waitall("vector", ATTK[k - 1][ab])
            for u in range(4):
                P.emit("vector", lambda e, ab=ab, u=u: e.tensor_scalar(
                    att_sb[:, 512 * ab + 128 * u:512 * ab + 128 * (u + 1)],
                    psA[:, 512 * u:512 * u + 128], rc_sb[:, 4 * ab + u:4 * ab + u + 1],
                    None, ALU.mult), inc="svv" if u == 3 else None)
            t_oevac = P.cnt["svv"]
            P.wait("sync", "svv", t_oevac)
            for u in range(4):
                i = 4 * w + u
                P.dma("sync", ATTK[k][ab], g2i[k, i // 8][128 * (i % 8):128 * (i % 8 + 1), :],
                      att_sb[:, 512 * ab + 128 * u:512 * ab + 128 * (u + 1)])
            t_attdma_unused = 0
        head_done[k] = P.cnt["stt"]

    t_ag2k = []
    for k in range(4):
        P.waitall("gpsimd", ATTK[k][0])
        P.waitall("gpsimd", ATTK[k][1])
        t_ag2k.append(P.emit("gpsimd", lambda e, k=k: e.collective_compute(
            "AllGather", ALU.bypass, replica_groups=G4,
            ins=[g2i[k].opt()], outs=[g2o[k].opt()]), inc="scc"))

    def go_rs2(e):
        with e.register("ro2") as ro2:
            for k in range(4):
                e.wait_ge(P.sems["scc"], t_ag2k[k])
                e.reg_load(ro2, ms_sb[0:1, 0:1])
                e.reg_mul(ro2, ro2, 131072)
                if k:
                    e.reg_add(ro2, ro2, k * 2097152)
                e.dma_start(out=bass.AP(rs2, k * 524288, [[131072, 4], [1, 131072]]),
                            in_=bass.AP(g2o, ro2, [[524288, 4], [1, 131072]])
                            ).then_inc(P.sems["rs2s"], 16)
    P.cnt["rs2s"] += 4 * 16
    P.on("gpsimd", go_rs2)
    P.wait("gpsimd", "stt", head_done[3])
    wcast(wo_d)

    if stage == 2:
        if "t_rs2" in T:
            P.waitall("gpsimd", "rs2s")
            P.dma("gpsimd", "tapA", bass.AP(T["t_rs2"], 0, [[1024, 2048], [1, 1024]]),
                  bass.AP(rs2, 0, [[1024, 2048], [1, 1024]]))
            P.waitall("gpsimd", "tapA")
        if "t_scal" in T:
            P.wait("sync", "svv", tv_scal)
            P.dma("sync", "tapB", T["t_scal"][:, :], scal_sb[:, :])
            P.waitall("sync", "tapB")
        P.dma("sync", "yme", bass.AP(y, 0, [[2048, 128], [262144, 8], [1, 2048]]),
              bass.AP(x_res, 0, [[16384, 128], [2048, 8], [1, 2048]]))
        P.waitall("sync", "yme")
        return finish(nc, P)

    P.waitall("sync", "rs2s")
    for g in range(16):
        P.dma("sync", "fmao", xn_fm[:, 1024 * g:1024 * (g + 1)], rs2[g % 4, g // 4], transpose=True)

    t_wo_ev = {}
    t_womm = 0
    P.waitall("tensor", "wc")
    P.waitall("tensor", "fmao")
    ap2_prev = [0, 0]
    for tb in range(8):
        b0 = 4 * (tb % 2)
        if tb >= 2:
            P.wait("tensor", "svv", t_wo_ev[tb - 2])
        for ic in range(16):
            for og in range(4):
                P.emit("tensor", lambda e, b0=b0, ic=ic, og=og, tb=tb: e.matmul(
                    psA[:, 512 * (b0 + og):512 * (b0 + og + 1)],
                    xn_fm[:, 1024 * ic + 128 * tb:1024 * ic + 128 * (tb + 1)],
                    w_sb[:, 2048 * ic + 512 * og:2048 * ic + 512 * (og + 1)],
                    start=(ic == 0), stop=(ic == 15)),
                    inc="stt" if (ic == 15 and og == 3) else None)
        t_womm = P.cnt["stt"]
        P.wait("vector", "stt", t_womm)
        for og in range(4):
            P.emit("vector", lambda e, b0=b0, og=og, tb=tb: e.tensor_tensor(
                x_res[:, 2048 * tb + 512 * og:2048 * tb + 512 * (og + 1)],
                psA[:, 512 * (b0 + og):512 * (b0 + og + 1)],
                x_res[:, 2048 * tb + 512 * og:2048 * tb + 512 * (og + 1)], ALU.add),
                inc="svv" if og == 3 else None)
        t_wo_ev[tb] = P.cnt["svv"]
        ln_block(tb, (lambda tb=tb: P.wait("vector", "svv", t_wo_ev[tb])), "x2t", x2t, ap2_prev)

    P.wait("sync", "stt", head_done[3])
    P.wait("sync", "scc", t_agwf)
    for hc0 in range(2):
        P.dma("sync", ("fcA", "fcB")[hc0], fcw_sb[:, 2048 * hc0:2048 * (hc0 + 1)],
              bass.AP(q2fT_o, hc0 * 262144, [[2048, 128], [1, 2048]]))
    P.waitall("scalar", "x2t")
    P.wait("scalar", "stt", t_womm)
    for ic in range(16):
        P.dma("scalar", "fmx2", xn_fm[:, 1024 * ic:1024 * (ic + 1)],
              x2t[:, 128 * ic:128 * (ic + 1)], transpose=True)

    if stage == 3:
        P.wait("sync", "svv", t_wo_ev[7])
        P.dma("sync", "yme", bass.AP(y, 0, [[2048, 128], [262144, 8], [1, 2048]]),
              bass.AP(x_res, 0, [[16384, 128], [2048, 8], [1, 2048]]))
        P.waitall("sync", "yme")
        return finish(nc, P)

    FC = ("fcA", "fcB")
    t_fc_mm = {}
    t_sq = {}
    t_relu = {}
    P.wait("sync", "scc", t_agwf)
    for hc in range(40):
        slot = hc % 2
        r, l = hc // 5, hc % 5
        if hc >= 2:
            P.wait("sync", "stt", t_fc_mm[hc - 2])
            P.dma("sync", FC[slot], fcw_sb[:, 2048 * slot:2048 * (slot + 1)],
                  bass.AP(q2fT_o, (r * 5 + l) * 262144, [[2048, 128], [1, 2048]]))
        P.waitall("tensor", FC[slot])
        P.waitall("tensor", "fmx2")
        if hc >= 2:
            P.wait("tensor", "saa", t_relu[hc - 2])
        pr = hc % 2
        for ic in range(16):
            for hf in range(2):
                P.emit("tensor", lambda e, pr=pr, hf=hf, ic=ic, slot=slot: e.matmul(
                    psA[:, 512 * (2 * pr + hf):512 * (2 * pr + hf + 1)],
                    fcw_sb[:, 2048 * slot + 128 * ic:2048 * slot + 128 * (ic + 1)],
                    xn_fm[:, 1024 * ic + 512 * hf:1024 * ic + 512 * (hf + 1)],
                    start=(ic == 0), stop=(ic == 15)),
                    inc="stt" if (ic == 15 and hf == 1) else None)
        t_fc_mm[hc] = P.cnt["stt"]
        P.wait("scalar", "stt", t_fc_mm[hc])
        if hc >= 2:
            P.wait("scalar", "svv", t_sq[hc - 2])
        for hf in range(2):
            P.emit("scalar", lambda e, pr=pr, hf=hf, slot=slot: e.activation(
                rl_sb[:, 1024 * slot + 512 * hf:1024 * slot + 512 * (hf + 1)],
                psA[:, 512 * (2 * pr + hf):512 * (2 * pr + hf + 1)],
                AF.Relu, scale=scal_sb[:, C_SFC:C_SFC + 1]),
                inc="saa" if hf == 1 else None)
        t_relu[hc] = P.cnt["saa"]
        P.wait("vector", "saa", t_relu[hc])
        t_sq[hc] = P.emit("vector", lambda e, hc=hc, slot=slot: e.tensor_tensor(
            h_sb[:, 1024 * hc:1024 * (hc + 1)],
            rl_sb[:, 1024 * slot:1024 * (slot + 1)],
            rl_sb[:, 1024 * slot:1024 * (slot + 1)], ALU.mult), inc="svv")
    t_h_done = t_sq[39]

    PR = ("prA", "prB")
    OB = ("obA", "obB")
    P.wait("sync", "scc", t_agwp)
    P.wait("sync", "svv", t_h_done)
    t_quad_ev = 0
    t_prw_mm = {}
    for th in range(2):
        for dh2 in range(2):
            qd = 2 * th + dh2
            for hc in range(40):
                slot = hc % 2
                if hc >= 2:
                    P.wait("sync", "stt", t_prw_mm[(qd, hc - 2)])
                elif qd > 0:
                    P.wait("sync", "stt", t_prw_mm[(qd - 1, 38 + hc)])
                P.dma("sync", PR[slot], bass.AP(prw_sb, 1024 * slot, [[2048, 128], [256, 4], [1, 256]]),
                      bass.AP(q2pT_o, 4 * dh2 * 1310720 + 128 * hc * 256, [[256, 128], [1310720, 4], [1, 256]]))
                P.waitall("tensor", PR[slot])
                if hc == 0:
                    P.wait("tensor", "svv", t_h_done)
                    if t_quad_ev:
                        P.wait("tensor", "svv", t_quad_ev)
                for tb in range(4):
                    tbg = 4 * th + tb
                    for bd in range(2):
                        P.emit("tensor", lambda e, tb=tb, bd=bd, hc=hc, slot=slot, tbg=tbg: e.matmul(
                            psA[:, 512 * (2 * tb + bd):512 * (2 * tb + bd + 1)],
                            h_sb[:, 1024 * hc + 128 * tbg:1024 * hc + 128 * (tbg + 1)],
                            prw_sb[:, 1024 * slot + 512 * bd:1024 * slot + 512 * (bd + 1)],
                            start=(hc == 0), stop=(hc == 39)),
                            inc="stt" if (tb == 3 and bd == 1) else None)
                t_prw_mm[(qd, hc)] = P.cnt["stt"]
            P.wait("vector", "stt", t_prw_mm[(qd, 39)])
            for bi in range(8):
                tb, bd = bi // 2, bi % 2
                tbg = 4 * th + tb
                obs = bi % 2
                P.waitall("vector", OB[obs])
                tq = P.emit("vector", lambda e, bi=bi, tbg=tbg, bd=bd, dh2=dh2, obs=obs: e.scalar_tensor_tensor(
                    ob_sb[:, 512 * obs:512 * (obs + 1)],
                    psA[:, 512 * bi:512 * (bi + 1)],
                    scal_sb[:, C_SPR:C_SPR + 1],
                    x_res[:, 2048 * tbg + 1024 * dh2 + 512 * bd:2048 * tbg + 1024 * dh2 + 512 * (bd + 1)],
                    ALU.mult, ALU.add), inc="svv")
                P.wait("sync", "svv", tq)
                P.dma("sync", OB[obs], y[128 * tbg:128 * (tbg + 1),
                                         1024 * dh2 + 512 * bd:1024 * dh2 + 512 * (bd + 1)],
                      ob_sb[:, 512 * obs:512 * (obs + 1)])
            t_quad_ev = P.cnt["svv"]

    P.waitall("sync", "obA")
    P.waitall("sync", "obB")
    return finish(nc, P)


def finish(nc, P):
    with nc.Block() as block:
        for eng in ENGINES:
            fns = P.streams[eng]

            def runner(e, fns=fns):
                for f in fns:
                    f(e)
            getattr(block, eng)(runner)
    assert nc.sbuf_base <= 17408, f"bass bump allocator grew into overlay: {nc.sbuf_base}"
    return nc


def make_consts():
    s = np.arange(128)[:, None]
    q = np.arange(128)[None, :]
    tril_T = (s <= q).astype(np.float32)
    j = np.arange(32)[:, None]
    i = (np.arange(4096)[None, :] // 128)
    um = (j > i).astype(np.float32)
    return tril_T, um, np.eye(128, dtype=np.float32)


def make_in_maps(inputs):
    x = np.asarray(inputs["L0_attn_x"])
    wq = np.ascontiguousarray(np.asarray(inputs["L0_attn_Wq"], dtype=np.float32))
    wk = np.ascontiguousarray(np.asarray(inputs["L0_attn_Wk"], dtype=np.float32))
    wv = np.ascontiguousarray(np.asarray(inputs["L0_attn_Wv"], dtype=np.float32))
    wo = np.ascontiguousarray(np.asarray(inputs["L0_attn_Wo"], dtype=np.float32))
    wfc = np.asarray(inputs["L0_ffn_w_fc"], dtype=np.float32)
    wpr = np.asarray(inputs["L0_ffn_w_proj"], dtype=np.float32)
    tril_T, um, ident = make_consts()
    maps = []
    for c in range(8):
        r = c % 4
        b = c // 4
        maps.append({
            "x_q": np.ascontiguousarray(x[b, 1024 * r:1024 * (r + 1), :], dtype=np.float32),
            "wq": wq, "wk": wk, "wv": wv, "wo": wo,
            "wfc_sl": np.ascontiguousarray(wfc[640 * c:640 * (c + 1), :]),
            "wpr_sl": np.ascontiguousarray(wpr[256 * c:256 * (c + 1), :]),
            "meta": np.array([[r]], dtype=np.uint32),
            "tril": tril_T, "umask": um, "ident": ident,
        })
    return maps


from concourse.bass_utils import run_bass_kernel_spmd

_cache = {}


def kernel(**inputs):
    if "nc" not in _cache:
        _cache["nc"] = build(stage=4, taps=())
    nc = _cache["nc"]
    maps = make_in_maps(inputs)
    res = run_bass_kernel_spmd(nc, maps, core_ids=list(range(8)))
    out = np.empty((2, 4096, 2048), np.float32)
    for c in range(8):
        r = c % 4
        out[c // 4, 1024 * r:1024 * (r + 1), :] = np.asarray(res.results[c]["y"]).reshape(1024, 2048)
    return out


# version 18 (final)
# speedup vs baseline: 1.1049x; 1.0173x over previous
import numpy as np
import ml_dtypes
import concourse.bass as bass
import concourse.mybir as mybir

F32 = mybir.dt.float32
BF16 = mybir.dt.bfloat16
U32 = mybir.dt.uint32
AF = mybir.ActivationFunctionType
ALU = mybir.AluOpType
AX = mybir.AxisListType.X
G4 = [[0, 1, 2, 3], [4, 5, 6, 7]]
G8 = [list(range(8))]
ENGINES = ("sync", "gpsimd", "tensor", "vector", "scalar")

SEMS = ("cn", "umx", "am2", "ami", "xA", "xB", "wc", "xnt", "x2t", "fmx", "fmao", "fmx2",
        "sl0", "sl1", "sl2", "sl3", "evA", "evB", "evC", "evD",
        "wv0", "wv1", "wv2", "wv3", "wkA", "wkB",
        "ts0", "ts1", "ts2", "ts3", "ts4", "ts5", "ts6", "ts7",
        "tw0", "tw1", "tw2", "tw3", "tw4", "tw5", "tw6", "tw7",
        "kqA", "kqB", "vv",
        "at0A", "at0B", "at1A", "at1B", "at2A", "at2B", "at3A", "at3B",
        "rsA", "rsB", "rsC", "rsD", "rs2s", "fcA", "fcB", "prA", "prB", "obA", "obB",
        "tapA", "tapB", "yme",
        "scc", "stt", "svv", "saa")


class Prog:
    def __init__(self, nc):
        self.nc = nc
        self.streams = {k: [] for k in ENGINES}
        self.sems = {}
        self.cnt = {}
        for s in SEMS:
            self.sems[s] = nc.semaphore(s).__enter__()
            self.cnt[s] = 0

    def on(self, eng, fn):
        self.streams[eng].append(fn)

    def wait(self, eng, sem, tgt):
        assert tgt <= self.cnt[sem], f"wait on future {sem}:{tgt}>{self.cnt[sem]}"
        if tgt <= 0:
            return
        s = self.sems[sem]
        self.on(eng, lambda e, s=s, t=tgt: e.wait_ge(s, t))

    def waitall(self, eng, sem):
        self.wait(eng, sem, self.cnt[sem])

    def emit(self, eng, fn, inc=None, k=1):
        if inc is None:
            self.on(eng, fn)
            return None
        self.cnt[inc] += k
        tgt = self.cnt[inc]
        s = self.sems[inc]
        self.on(eng, lambda e, f=fn, s=s, kk=k: f(e).then_inc(s, kk))
        return tgt

    def dma(self, eng, sem, out, in_, **kw):
        return self.emit(eng, lambda e, o=out, i=in_, kws=dict(kw): e.dma_start(out=o, in_=i, **kws),
                         inc=sem, k=16)


def build(stage=4, taps=()):
    nc = bass.Bass("TRN2", target_bir_lowering=False, debug=False)
    P = Prog(nc)

    x_q = nc.dram_tensor("x_q", [1024, 2048], F32, kind="ExternalInput")
    wq_d = nc.dram_tensor("wq", [2048, 2048], F32, kind="ExternalInput")
    wk_d = nc.dram_tensor("wk", [2048, 2048], F32, kind="ExternalInput")
    wv_d = nc.dram_tensor("wv", [2048, 2048], F32, kind="ExternalInput")
    wo_d = nc.dram_tensor("wo", [2048, 2048], F32, kind="ExternalInput")
    wfc_sl = nc.dram_tensor("wfc_sl", [640, 2048], F32, kind="ExternalInput")
    wpr_sl = nc.dram_tensor("wpr_sl", [256, 5120], F32, kind="ExternalInput")
    meta = nc.dram_tensor("meta", [1, 1], U32, kind="ExternalInput")
    tril = nc.dram_tensor("tril", [128, 128], F32, kind="ExternalInput")
    umask = nc.dram_tensor("umask", [32, 4096], F32, kind="ExternalInput")
    ident = nc.dram_tensor("ident", [128, 128], F32, kind="ExternalInput")
    y = nc.dram_tensor("y", [1024, 2048], F32, kind="ExternalOutput")

    g1i = nc.dram_tensor("g1i", [4, 4, 3, 128, 1024], BF16)
    g1o = nc.dram_tensor("g1o", [4, 4, 4, 3, 128, 1024], BF16)
    rsV = nc.dram_tensor("rsV", [4, 4, 8, 128, 128], BF16)
    xnt = nc.dram_tensor("xnt", [1024, 2048], BF16)
    x2t = nc.dram_tensor("x2t", [1024, 2048], BF16)
    g2i = nc.dram_tensor("g2i", [4, 4, 1024, 128], BF16)
    g2o = nc.dram_tensor("g2o", [4, 4, 4, 1024, 128], BF16)
    rs2 = nc.dram_tensor("rs2", [4, 4, 1024, 128], BF16)
    q2fT_i = nc.dram_tensor("q2fT_i", [5, 128, 16, 128], BF16)
    q2fT_o = nc.dram_tensor("q2fT_o", [8, 5, 128, 16, 128], BF16, addr_space="Shared")
    q2pT_i = nc.dram_tensor("q2pT_i", [5120, 256], BF16)
    q2pT_o = nc.dram_tensor("q2pT_o", [8, 5120, 256], BF16, addr_space="Shared")
    ami_d = nc.dram_tensor("ami", [2, 1], F32)
    amo_d = nc.dram_tensor("amo", [2, 1], F32)

    T = {}
    if "rs1" in taps:
        T["t_rs1"] = nc.dram_tensor("t_rs1", [4, 4, 8, 128, 128], F32, kind="ExternalOutput")
    if "rs2" in taps:
        T["t_rs2"] = nc.dram_tensor("t_rs2", [4, 4, 1024, 128], F32, kind="ExternalOutput")
    if "scal" in taps:
        T["t_scal"] = nc.dram_tensor("t_scal", [128, 8], F32, kind="ExternalOutput")

    SHIFT = 17408

    def A(name, shape, dt, offset):
        return nc.alloc_sbuf_tensor_at(name, shape, dt, offset=offset + SHIFT)
    x_res = A("x_res", [128, 16384], F32, offset=0)
    xn_fm = A("xn_fm", [128, 16384], BF16, offset=65536)
    K_sb = A("K_sb", [128, 8192], BF16, offset=65536)
    Q_sb = A("Q_sb", [128, 8192], BF16, offset=65536 + 16384)
    w_sb = A("w_sb", [128, 32768], BF16, offset=98304)
    h_sb = A("h_sb", [128, 40960], BF16, offset=98304)
    V_sb = A("V_sb", [128, 4608], BF16, offset=163840)
    xn_stage = A("xn_stage", [128, 2048], BF16, offset=173056)
    ev_sb = A("ev_sb", [128, 2048], BF16, offset=173056)
    ln_s = A("ln_s", [128, 64], F32, offset=179200)
    wslab = A("wslab", [128, 4096], F32, offset=180224)
    um_sb = A("um_sb", [32, 4096], F32, offset=180224)
    rl_sb = A("rl_sb", [128, 2048], F32, offset=180224)
    prw_sb = A("prw_sb", [128, 2048], BF16, offset=180224)
    ob_sb = A("ob_sb", [128, 1024], F32, offset=184320)
    Vn_sb = A("Vn_sb", [128, 4608], BF16, offset=98304)
    fcw_sb = A("fcw_sb", [128, 4096], BF16, offset=188416)
    P_sb = A("P_sb", [128, 2048], BF16, offset=196608)
    qa_sb = A("qa_sb", [128, 1024], BF16, offset=196608)
    qb_sb = A("qb_sb", [128, 1024], BF16, offset=198656)
    qb1_sb = A("qb1_sb", [128, 1024], BF16, offset=200704)
    att_sb = A("att_sb", [128, 1024], BF16, offset=202752)
    rc_sb = A("rc_sb", [128, 8], F32, offset=204800)
    vs_sm = A("vs_sm", [32, 129], F32, offset=204832)
    vsfm = A("vsfm", [128, 64], F32, offset=205376)
    tril_sb = A("tril_sb", [128, 128], F32, offset=206848)
    ident_sb = A("ident_sb", [128, 128], F32, offset=207360)
    scal_sb = A("scal_sb", [128, 8], F32, offset=207872)
    amred = A("amred", [128, 24], F32, offset=207904)
    vals_sb = A("vals_sb", [1, 16], F32, offset=208000)
    am2_sb = A("am2_sb", [1, 4], F32, offset=208064)
    ones_r = A("ones_r", [1, 128], F32, offset=208096)
    ones_c = A("ones_c", [128, 1], F32, offset=208640)
    ms_sb = A("ms_sb", [1, 1], U32, offset=208672)
    tsp_sb = A("tsp_sb", [128, 1280], BF16, offset=208704)
    onesb = A("onesb", [128, 1], BF16, offset=211648)
    ones2d = A("ones2d", [128, 128], BF16, offset=211264)
    itril_bf = A("itril_bf", [128, 128], mybir.dt.uint8, offset=211520)

    psA = nc.alloc_psum_tensor("psA", [128, 4096], F32)

    C_THRF, C_THRP, C_SFC, C_SPR, C_NTHRF, C_NTHRP = 2, 3, 4, 5, 6, 7
    INV64 = 1.0 / 64.0
    SL = ("sl0", "sl1", "sl2", "sl3")

    P.dma("sync", "cn", tril_sb[:, :], tril[:, :])
    P.dma("sync", "cn", ident_sb[:, :], ident[:, :])
    P.dma("gpsimd", "tapA", ms_sb[:, :], meta[:, :])
    P.emit("vector", lambda e: e.memset(ones_r[:, :], 1.0))
    P.emit("vector", lambda e: e.memset(ones_c[:, :], 1.0))
    t_vones = P.emit("vector", lambda e: e.memset(
        bass.AP(V_sb, 128, [[4608, 128], [144, 32]]), 1.0), inc="svv")
    t_vscnt = P.emit("vector", lambda e: e.memset(vs_sm[:, 128:129], 128.0), inc="svv")
    t_onesb = P.emit("vector", lambda e: e.memset(onesb[:, :], 1.0), inc="svv")

    P.dma("sync", "xA", bass.AP(x_res, 0, [[16384, 128], [2048, 4], [1, 2048]]),
          bass.AP(x_q, 0, [[2048, 128], [262144, 4], [1, 2048]]))
    P.dma("sync", "xB", bass.AP(x_res, 8192, [[16384, 128], [2048, 4], [1, 2048]]),
          bass.AP(x_q, 1048576, [[2048, 128], [262144, 4], [1, 2048]]))
    t_ones2d = P.emit("vector", lambda e: e.memset(ones2d[:, :], 1.0), inc="svv")

    def wcast(dram):
        return P.dma("gpsimd", "wc",
                     bass.AP(w_sb, 0, [[32768, 128], [2048, 16], [1, 2048]]),
                     bass.AP(dram, 0, [[2048, 128], [262144, 16], [1, 2048]]))
    WVS = ("wv0", "wv1", "wv2", "wv3")

    def wcast_v(dram):
        for s in range(4):
            for d in range(4):
                P.dma("gpsimd", WVS[s],
                      bass.AP(w_sb, 512 * s + 128 * d, [[32768, 128], [2048, 16], [1, 128]]),
                      bass.AP(dram, (4 * d + s) * 128, [[2048, 128], [262144, 16], [1, 128]]))
    wcast(wq_d)

    def ln_block(tb, pre_wait, stage_sem, out_dram, ap_prev):
        sl = 32 * (tb % 2)
        pre_wait()
        if ap_prev[tb % 2]:
            P.wait("vector", "svv", ap_prev[tb % 2])
        for s4 in range(4):
            P.emit("vector", lambda e, tb=tb, sl=sl, s4=s4: e.bn_stats(
                ln_s[:, sl + 6 * s4: sl + 6 * s4 + 6],
                x_res[:, 2048 * tb + 512 * s4: 2048 * tb + 512 * (s4 + 1)]),
                inc="svv" if s4 == 3 else None)
        P.wait("vector", "svv", P.cnt["svv"])
        t2 = P.emit("vector", lambda e, sl=sl: e.bn_aggr(ln_s[:, sl + 24:sl + 26], ln_s[:, sl:sl + 24]), inc="svv")
        P.wait("vector", "svv", t2)
        t3 = P.emit("vector", lambda e, sl=sl: e.tensor_scalar(
            ln_s[:, sl + 26:sl + 27], ln_s[:, sl + 25:sl + 26], 1e-5, None, ALU.add), inc="svv")
        P.wait("vector", "svv", t3)
        t4 = P.emit("vector", lambda e, sl=sl: e.reciprocal(ln_s[:, sl + 27:sl + 28], ln_s[:, sl + 26:sl + 27]), inc="svv")
        P.wait("scalar", "svv", t4)
        t5 = P.emit("scalar", lambda e, sl=sl: e.activation(
            ln_s[:, sl + 28:sl + 29], ln_s[:, sl + 27:sl + 28], AF.Sqrt), inc="saa")
        P.wait("vector", "saa", t5)
        P.waitall("vector", stage_sem)
        t_ap = P.emit("vector", lambda e, tb=tb, sl=sl: e.tensor_scalar(
            xn_stage[:, :], x_res[:, 2048 * tb:2048 * (tb + 1)],
            ln_s[:, sl + 24:sl + 25], ln_s[:, sl + 28:sl + 29], ALU.subtract, ALU.mult), inc="svv")
        ap_prev[tb % 2] = t_ap
        P.wait("sync", "svv", t_ap)
        P.dma("sync", stage_sem, out_dram[128 * tb:128 * (tb + 1), :], xn_stage[:, :])
        return t_ap

    P.waitall("vector", "cn")
    t_trilbf = P.emit("vector", lambda e: e.tensor_scalar(
        itril_bf[:, :], tril_sb[:, :], 0.0, None, ALU.is_equal), inc="svv")
    ap_prev = [0, 0]
    for tb in range(8):
        ln_block(tb, (lambda tb=tb: P.waitall("vector", "xA" if tb < 4 else "xB")), "xnt", xnt, ap_prev)
    P.waitall("sync", "xnt")
    for ic in range(16):
        P.dma("sync", "fmx", xn_fm[:, 1024 * ic:1024 * (ic + 1)],
              xnt[:, 128 * ic:128 * (ic + 1)], transpose=True)

    pieces = [(wfc_sl, i, cp) for i in range(5) for cp in range(2)] + \
             [(wpr_sl, i, cp) for i in range(2) for cp in range(5)]
    t_red = [0] * 20
    for n, (src_d, i, cp) in enumerate(pieces):
        slot = n % 4
        if n >= 4:
            P.wait("scalar", "svv", t_red[n - 4])
        P.dma("scalar", SL[slot], wslab[:, 1024 * slot:1024 * (slot + 1)],
              src_d[128 * i:128 * (i + 1), 1024 * cp:1024 * (cp + 1)])
        P.waitall("vector", SL[slot])
        t_red[n] = P.emit("vector", lambda e, n=n: e.tensor_reduce(
            amred[:, n:n + 1], wslab[:, 1024 * (n % 4):1024 * (n % 4 + 1)],
            AX, ALU.add, apply_absolute_value=True), inc="svv")
    P.wait("vector", "svv", t_red[19])
    P.emit("vector", lambda e: e.tensor_reduce(amred[:, 20:21], amred[:, 0:10], AX, ALU.add))
    tv2 = P.emit("vector", lambda e: e.tensor_reduce(amred[:, 21:22], amred[:, 10:20], AX, ALU.add), inc="svv")
    P.wait("tensor", "svv", tv2)
    tt_am = P.emit("tensor", lambda e: e.matmul(psA[0:2, 3584:3585], amred[:, 20:22], ones_c[:, :],
                                                start=True, stop=True), inc="stt")
    P.wait("vector", "stt", tt_am)
    tv3 = P.emit("vector", lambda e: e.tensor_copy(amred[0:2, 22:23], psA[0:2, 3584:3585]), inc="svv")
    P.wait("sync", "svv", tv3)
    P.dma("sync", "ami", ami_d[:, :], amred[0:2, 22:23])
    P.waitall("gpsimd", "ami")
    t_ar = P.emit("gpsimd", lambda e: e.collective_compute(
        "AllReduce", ALU.add, replica_groups=G8,
        ins=[ami_d.ap().opt()], outs=[amo_d.ap().opt()]), inc="scc")
    t_qkv_mm = {}
    last_pair_ev = {}
    ev_sem4 = ("evA", "evB", "evC", "evD")
    t_ag1k = [0, 0, 0, 0]
    t_sign = [0] * 20
    t_tadd = [0] * 20
    TS = ("ts0", "ts1", "ts2", "ts3", "ts4", "ts5", "ts6", "ts7")
    TW = ("tw0", "tw1", "tw2", "tw3", "tw4", "tw5", "tw6", "tw7")
    ts_after = {}

    def tern_piece(n):
        src_d, i, cp = pieces[n]
        slot = n % 4
        if n < 4:
            P.wait("scalar", "svv", t_red[16 + n])
        else:
            P.wait("scalar", "saa", t_sign[n - 4])
        P.dma("scalar", SL[slot], wslab[:, 1024 * slot:1024 * (slot + 1)],
              src_d[128 * i:128 * (i + 1), 1024 * cp:1024 * (cp + 1)])
        P.waitall("scalar", SL[slot])
        qb = qb_sb if n % 2 == 0 else qb1_sb
        if n == 0:
            P.wait("scalar", "svv", tv_scal)
        else:
            P.wait("scalar", "svv", t_tadd[n - 1])
        if n >= 2:
            for b in range(8):
                P.wait("scalar", TS[b], ts_after[n - 2][b])
        cT = C_THRF if src_d is wfc_sl else C_THRP
        cN = C_NTHRF if src_d is wfc_sl else C_NTHRP
        P.emit("scalar", lambda e, slot=slot, cT=cT: e.activation(
            qa_sb[:, :], wslab[:, 1024 * slot:1024 * (slot + 1)], AF.Sign, bias=scal_sb[:, cT:cT + 1]))
        t_sign[n] = P.emit("scalar", lambda e, slot=slot, cN=cN, qb=qb: e.activation(
            qb[:, :], wslab[:, 1024 * slot:1024 * (slot + 1)], AF.Sign, bias=scal_sb[:, cN:cN + 1]), inc="saa")
        P.wait("vector", "saa", t_sign[n])
        t_tadd[n] = P.emit("vector", lambda e, qb=qb: e.tensor_tensor(
            qb[:, :], qa_sb[:, :], qb[:, :], ALU.add), inc="svv")
        P.wait("scalar", "svv", t_tadd[n])
        for b in range(8):
            P.waitall("scalar", TW[b])
            P.dma("scalar", TS[b], tsp_sb[:, 128 * b:128 * (b + 1)],
                  qb[:, 128 * b:128 * (b + 1)], transpose=True)
        ts_after[n] = [P.cnt[TS[b]] for b in range(8)]
        for b in range(8):
            P.waitall("gpsimd", TS[b])
            if src_d is wfc_sl:
                dst = bass.AP(q2fT_i, i * 262144 + (8 * cp + b) * 128, [[2048, 128], [1, 128]])
            else:
                dst = bass.AP(q2pT_i, (8 * cp + b) * 32768 + 128 * i, [[256, 128], [1, 128]])
            P.dma("gpsimd", TW[b], dst, tsp_sb[:, 128 * b:128 * (b + 1)])

    for p in range(2):
        if p == 0:
            P.waitall("tensor", "wc")
            P.waitall("tensor", "fmx")
        for oc in range(16):
            if p == 1 and oc == 0:
                P.waitall("tensor", "wkA")
            if p == 1 and oc == 8:
                P.waitall("tensor", "wkB")
            it = 16 * p + oc
            pr = oc % 2
            if pr in last_pair_ev:
                P.wait("tensor", "svv", last_pair_ev[pr])
            for ic in range(16):
                for hf in range(2):
                    P.emit("tensor", lambda e, pr=pr, hf=hf, ic=ic, oc=oc: e.matmul(
                        psA[:, 512 * (2 * pr + hf):512 * (2 * pr + hf + 1)],
                        w_sb[:, 2048 * ic + 128 * oc:2048 * ic + 128 * (oc + 1)],
                        xn_fm[:, 1024 * ic + 512 * hf:1024 * ic + 512 * (hf + 1)],
                        start=(ic == 0), stop=(ic == 15)),
                        inc="stt" if (ic == 15 and hf == 1) else None)
            t_qkv_mm[(p, oc)] = P.cnt["stt"]
            evs = "evA" if it % 2 == 0 else "evB"
            ecol = 1024 * (it % 2)
            P.wait("vector", "stt", t_qkv_mm[(p, oc)])
            if it == 0:
                P.waitall("vector", "xnt")
            P.waitall("vector", evs)
            for hf in range(2):
                P.emit("vector", lambda e, pr=pr, hf=hf, ecol=ecol: e.tensor_copy(
                    ev_sb[:, ecol + 512 * hf:ecol + 512 * (hf + 1)],
                    psA[:, 512 * (2 * pr + hf):512 * (2 * pr + hf + 1)]),
                    inc="svv" if hf == 1 else None)
            t_ev = P.cnt["svv"]
            last_pair_ev[pr] = t_ev
            P.wait("sync", "svv", t_ev)
            P.dma("sync", evs, g1i[oc % 4, oc // 4, p], ev_sb[:, ecol:ecol + 1024])
            if p == 1:
                for n in range(20):
                    if n * 16 // 20 == oc:
                        tern_piece(n)
        P.wait("gpsimd", "stt", t_qkv_mm[(p, 15)])
        if p == 0:
            for hfc, wks in ((0, "wkA"), (1, "wkB")):
                P.dma("gpsimd", wks,
                      bass.AP(w_sb, 1024 * hfc, [[32768, 128], [2048, 16], [1, 1024]]),
                      bass.AP(wk_d, 1024 * hfc, [[2048, 128], [262144, 16], [1, 1024]]))
            P.wait("sync", "scc", t_ar)
            P.dma("sync", "am2", am2_sb[0:1, 0:2], amo_d[:, :])
            P.waitall("vector", "am2")
            u1 = P.emit("vector", lambda e: e.tensor_scalar(
                vals_sb[0:1, 0:2], am2_sb[0:1, 0:2], 1.0 / 10485760.0, 1e-5, ALU.mult, ALU.max), inc="svv")
            P.wait("vector", "svv", u1)
            P.emit("vector", lambda e: e.tensor_scalar(vals_sb[0:1, 2:4], vals_sb[0:1, 0:2], 0.7, None, ALU.mult))
            u3 = P.emit("vector", lambda e: e.tensor_scalar(vals_sb[0:1, 4:6], vals_sb[0:1, 0:2], 0.5, None, ALU.mult), inc="svv")
            P.wait("vector", "svv", u3)
            u4 = P.emit("vector", lambda e: e.tensor_scalar(vals_sb[0:1, 6:8], vals_sb[0:1, 2:4], -1.0, None, ALU.mult), inc="svv")
            P.wait("tensor", "svv", u4)
            P.wait("tensor", "svv", tv3)
            tt_bc = P.emit("tensor", lambda e: e.matmul(psA[:, 3584:3592], ones_r[:, :], vals_sb[0:1, 0:8],
                                                        start=True, stop=True), inc="stt")
            P.wait("vector", "stt", tt_bc)
            tv_scal = P.emit("vector", lambda e: e.tensor_copy(scal_sb[:, :], psA[:, 3584:3592]), inc="svv")
        else:
            wcast_v(wv_d)
    t_tern_lastadd = t_tadd[19]
    t_sign_last = t_sign[19]

    P.waitall("tensor", "ami")
    last_pair6 = {}
    t_last_ev = 0
    for u in range(32):
        s, tb = u // 8, u % 8
        if tb == 0:
            P.waitall("tensor", WVS[s])
        pr6 = u % 8
        if pr6 in last_pair6:
            P.wait("tensor", "svv", last_pair6[pr6])
        for ic in range(16):
            P.emit("tensor", lambda e, pr6=pr6, ic=ic, s=s, tb=tb: e.matmul(
                psA[:, 512 * pr6:512 * (pr6 + 1)],
                xn_fm[:, 1024 * ic + 128 * tb:1024 * ic + 128 * (tb + 1)],
                w_sb[:, 2048 * ic + 512 * s:2048 * ic + 512 * (s + 1)],
                start=(ic == 0), stop=(ic == 15)),
                inc="stt" if ic == 15 else None)
        t_qkv_mm[(2, u)] = P.cnt["stt"]
        es = ev_sem4[u % 4]
        ecol = 512 * (u % 4)
        P.wait("vector", "stt", t_qkv_mm[(2, u)])
        if u < 4:
            P.waitall("vector", "evA")
            P.waitall("vector", "evB")
        P.waitall("vector", es)
        t_ev = P.emit("vector", lambda e, pr6=pr6, ecol=ecol: e.tensor_copy(
            ev_sb[:, ecol:ecol + 512], psA[:, 512 * pr6:512 * (pr6 + 1)]), inc="svv")
        last_pair6[pr6] = t_ev
        t_last_ev = t_ev
        P.wait("sync", "svv", t_ev)
        for d in range(4):
            P.dma("sync", es,
                  bass.AP(g1i, (4 * s + d) * 393216 + 262144 + tb * 16384, [[128, 128], [1, 128]]),
                  bass.AP(ev_sb, ecol + 128 * d, [[2048, 128], [1, 128]]))
        if tb == 7:
            for esem in ev_sem4:
                P.waitall("gpsimd", esem)
            t_ag1k[s] = P.emit("gpsimd", lambda e, s=s: e.collective_compute(
                "AllGather", ALU.bypass, replica_groups=G4,
                ins=[g1i[s].opt()], outs=[g1o[s].opt()]), inc="scc")
    t_qkv_mm_last = t_qkv_mm[(2, 31)]
    P.wait("vector", "stt", t_qkv_mm_last)
    t_vones2 = P.emit("vector", lambda e: e.memset(
        bass.AP(Vn_sb, 128, [[4608, 128], [144, 32]]), 1.0), inc="svv")

    P.waitall("gpsimd", "tapA")
    RSK = ("rsA", "rsB", "rsC", "rsD")

    def go_rs1(e):
        for k in range(4):
            e.wait_ge(P.sems["scc"], t_ag1k[k])
            with e.register(f"rV{k}") as rV:
                e.reg_load(rV, ms_sb[0:1, 0:1])
                e.reg_mul(rV, rV, 393216)
                e.reg_add(rV, rV, k * 6291456 + 262144)
                e.dma_start(out=bass.AP(rsV, k * 131072, [[524288, 4], [1, 131072]]),
                            in_=bass.AP(g1o, rV, [[1572864, 4], [1, 131072]])
                            ).then_inc(P.sems[RSK[k]], 16)
    for k in range(4):
        P.cnt[RSK[k]] += 16
    P.on("gpsimd", go_rs1)


    if stage >= 2:
        for b in range(8):
            P.waitall("gpsimd", TW[b])
        t_agwf = P.emit("gpsimd", lambda e: e.collective_compute(
            "AllGather", ALU.bypass, replica_groups=G8,
            ins=[q2fT_i.ap().opt()], outs=[q2fT_o.ap().opt()]), inc="scc")
        t_agwp = P.emit("gpsimd", lambda e: e.collective_compute(
            "AllGather", ALU.bypass, replica_groups=G8,
            ins=[q2pT_i.ap().opt()], outs=[q2pT_o.ap().opt()]), inc="scc")

    if stage == 1:
        if "t_rs1" in T:
            for k in range(4):
                P.waitall("gpsimd", RSK[k])
            P.dma("gpsimd", "tapA", bass.AP(T["t_rs1"], 0, [[1024, 2048], [1, 1024]]),
                  bass.AP(rsV, 0, [[1024, 2048], [1, 1024]]))
            P.waitall("gpsimd", "tapA")
        P.dma("sync", "yme", bass.AP(y, 0, [[2048, 128], [262144, 8], [1, 2048]]),
              bass.AP(x_res, 0, [[16384, 128], [2048, 8], [1, 2048]]))
        P.waitall("sync", "yme")
        return finish(nc, P)

    last_exp_par = [0, 0]
    last_pv_pb = [0, 0, 0, 0]
    last_diag_pb = [0, 0, 0, 0]
    t_oevac = 0
    head_done = [0, 0, 0, 0]
    t_vs_copy = 0
    t_suffix_last = 0
    t_vstr_prev = 0
    t_vsf_prev = 0
    first_P_use = [True]
    KQ = ("kqA", "kqB")
    ATTK = (("at0A", "at0B"), ("at1A", "at1B"), ("at2A", "at2B"), ("at3A", "at3B"))

    t_kq_done = {}
    P.waitall("sync", "tapA")
    P.wait("sync", "stt", t_qkv_mm_last)
    for k in range(4):
        bufp = k % 2
        P.wait("sync", "scc", t_ag1k[k])
        if k >= 2:
            P.wait("sync", "stt", head_done[k - 2])

        def kq_loads(e, k=k, bufp=bufp, which=1, dst=None):
            with e.register(f"rkq{k}{'KQ'[0 if which else 1]}") as ro:
                e.reg_load(ro, ms_sb[0:1, 0:1])
                e.reg_mul(ro, ro, 393216)
                e.reg_add(ro, ro, k * 6291456 + which * 131072)
                e.dma_start(out=bass.AP(dst, 4096 * bufp, [[8192, 128], [1024, 4], [1, 1024]]),
                            in_=bass.AP(g1o, ro, [[1024, 128], [1572864, 4], [1, 1024]])
                            ).then_inc(P.sems[KQ[bufp]], 16)
        P.cnt[KQ[bufp]] += 16
        P.on("sync", lambda e, k=k, bufp=bufp: kq_loads(e, k, bufp, 1, K_sb))
        P.cnt[KQ[bufp]] += 16
        P.on("sync", lambda e, k=k, bufp=bufp: kq_loads(e, k, bufp, 0, Q_sb))
        vb = V_sb if k % 2 == 0 else Vn_sb
        P.waitall("sync", RSK[k])
        for r in range(4):
            P.dma("sync", "vv", bass.AP(vb, 1152 * r, [[4608, 128], [144, 8], [1, 128]]),
                  bass.AP(rsV, r * 524288 + k * 131072, [[128, 128], [16384, 8], [1, 128]]))
        if k == 0:
            P.wait("sync", "saa", t_sign_last)
            P.dma("sync", "umx", um_sb[:, :], umask[:, :])

        P.waitall("tensor", "vv")
        if k == 0:
            P.waitall("tensor", "cn")
            P.wait("tensor", "svv", t_last_ev)
            P.wait("tensor", "svv", t_onesb)
        if t_vs_copy:
            P.wait("tensor", "svv", t_vs_copy)
        for j in range(32):
            P.emit("tensor", lambda e, j=j, vb=vb: e.matmul(
                psA[:, 3200 + j:3201 + j], vb[:, 144 * j:144 * j + 128], onesb[:, 0:1],
                start=True, stop=True), inc="stt" if j == 31 else None)
        t_vmm = P.cnt["stt"]
        P.wait("vector", "stt", t_vmm)
        if t_vstr_prev:
            P.wait("vector", "stt", t_vstr_prev)
        t_vsf_prev = P.emit("vector", lambda e: e.tensor_copy(
            vsfm[:, 0:32], psA[:, 3200:3232]), inc="svv")
        P.wait("tensor", "svv", t_vsf_prev)
        tt_tr = P.emit("tensor", lambda e: e.transpose(
            psA[0:32, 3072:3200], vsfm[:, 0:32], ident_sb[:, :]), inc="stt")
        t_vstr_prev = tt_tr
        P.wait("vector", "stt", tt_tr)
        if t_suffix_last:
            P.wait("vector", "stt", t_suffix_last)
        P.wait("vector", "svv", t_vscnt)
        t_vs_copy = P.emit("vector", lambda e: e.tensor_copy(vs_sm[:, 0:128], psA[0:32, 3072:3200]), inc="svv")

        P.waitall("tensor", KQ[bufp])
        P.waitall("tensor", "vv")
        if k == 0:
            P.wait("tensor", "svv", t_vones)
        if k == 1:
            P.wait("tensor", "svv", t_vones2)

        def emit_pvs(grp, w, exp_tgt, sel_tgt):
            P.wait("tensor", "saa", exp_tgt)
            if sel_tgt:
                P.wait("tensor", "svv", sel_tgt)
            if grp[0] == 0 and t_oevac:
                P.wait("tensor", "svv", t_oevac)
            for j in grp:
                for i in range(max(j, 4 * w), 4 * w + 4):
                    P.emit("tensor", lambda e, j=j, i=i, w=w, vb=vb: e.matmul(
                        psA[:, 512 * (i % 4):512 * (i % 4) + 129],
                        P_sb[:, 512 * (j % 4) + 128 * (i - 4 * w):512 * (j % 4) + 128 * (i - 4 * w) + 128],
                        vb[:, 144 * j:144 * j + 129],
                        start=(j == 0), stop=False),
                        inc="stt" if (j == grp[-1] and i == 4 * w + 3) else None)
            for j in grp:
                last_pv_pb[j % 4] = P.cnt["stt"]

        for w in range(8):
            groups = []
            jj = 0
            while jj + 1 <= 4 * w - 1:
                groups.append((jj, jj + 1))
                jj += 2
            while jj <= 4 * w + 3:
                groups.append((jj,))
                jj += 1
            pend = None
            for grp in groups:
                tt_qk = 0
                for j in grp:
                    par = j % 2
                    c0 = max(128 * j, 512 * w)
                    N = 512 * (w + 1) - c0
                    P.wait("tensor", "saa", last_exp_par[par])
                    tt_qk = P.emit("tensor", lambda e, par=par, bufp=bufp, j=j, c0=c0, N=N: e.matmul(
                        psA[:, 512 * (4 + par):512 * (4 + par) + N],
                        K_sb[:, 4096 * bufp + 128 * j:4096 * bufp + 128 * (j + 1)],
                        Q_sb[:, 4096 * bufp + c0:4096 * bufp + c0 + N],
                        start=True, stop=True), inc="stt")
                if pend is not None:
                    emit_pvs(*pend)
                if first_P_use[0]:
                    P.wait("scalar", "svv", t_tern_lastadd)
                    for b in range(8):
                        P.waitall("scalar", TS[b])
                    first_P_use[0] = False
                P.wait("scalar", "stt", tt_qk)
                for j in grp:
                    P.wait("scalar", "stt", last_pv_pb[j % 4])
                if len(grp) == 2:
                    jA = grp[0]
                    t_exp = P.emit("scalar", lambda e, jA=jA: e.activation(
                        P_sb[:, 512 * (jA % 4):512 * (jA % 4) + 1024],
                        psA[:, 2048:3072], AF.Exp, scale=INV64), inc="saa")
                    last_exp_par[0] = t_exp
                    last_exp_par[1] = t_exp
                    sel_tgt = 0
                else:
                    j = grp[0]
                    par = j % 2
                    c0 = max(128 * j, 512 * w)
                    N = 512 * (w + 1) - c0
                    pcol0 = c0 - 512 * w
                    t_exp = P.emit("scalar", lambda e, j=j, par=par, pcol0=pcol0, N=N: e.activation(
                        P_sb[:, 512 * (j % 4) + pcol0:512 * (j % 4) + pcol0 + N],
                        psA[:, 512 * (4 + par):512 * (4 + par) + N], AF.Exp, scale=INV64), inc="saa")
                    last_exp_par[par] = t_exp
                    sel_tgt = 0
                    if j >= 4 * w:
                        P.wait("vector", "saa", t_exp)
                        sel_tgt = P.emit("vector", lambda e, j=j, pcol0=pcol0: e.copy_predicated(
                            P_sb[:, 512 * (j % 4) + pcol0:512 * (j % 4) + pcol0 + 128],
                            itril_bf[:, :], ones2d[:, :]), inc="svv")
                pend = (grp, w, t_exp, sel_tgt)
            emit_pvs(*pend)
            P.wait("tensor", "svv", t_vs_copy)
            if k == 0 and w == 0:
                P.waitall("tensor", "umx")
                P.wait("tensor", "svv", t_ones2d)
                P.wait("tensor", "svv", t_trilbf)
            for i in range(4 * w, 4 * w + 4):
                P.emit("tensor", lambda e, i=i: e.matmul(
                    psA[:, 512 * (i % 4):512 * (i % 4) + 129],
                    um_sb[:, 128 * i:128 * (i + 1)], vs_sm[:, :],
                    start=False, stop=True), inc="stt" if i == 4 * w + 3 else None)
            t_wave = P.cnt["stt"]
            if w == 7:
                t_suffix_last = t_wave
            ab = w % 2
            P.wait("vector", "stt", t_wave)
            if t_oevac:
                P.wait("vector", "svv", t_oevac)
            for u in range(4):
                P.emit("vector", lambda e, ab=ab, u=u: e.reciprocal(
                    rc_sb[:, 4 * ab + u:4 * ab + u + 1], psA[:, 512 * u + 128:512 * u + 129]),
                    inc="svv" if u == 3 else None)
            P.wait("vector", "svv", P.cnt["svv"])
            P.waitall("vector", ATTK[k][ab])
            if w < 2 and k >= 1:
                P.waitall("vector", ATTK[k - 1][ab])
            for u in range(4):
                P.emit("vector", lambda e, ab=ab, u=u: e.tensor_scalar(
                    att_sb[:, 512 * ab + 128 * u:512 * ab + 128 * (u + 1)],
                    psA[:, 512 * u:512 * u + 128], rc_sb[:, 4 * ab + u:4 * ab + u + 1],
                    None, ALU.mult), inc="svv" if u == 3 else None)
            t_oevac = P.cnt["svv"]
            P.wait("sync", "svv", t_oevac)
            for u in range(4):
                i = 4 * w + u
                P.dma("sync", ATTK[k][ab], g2i[k, i // 8][128 * (i % 8):128 * (i % 8 + 1), :],
                      att_sb[:, 512 * ab + 128 * u:512 * ab + 128 * (u + 1)])
            t_attdma_unused = 0
        head_done[k] = P.cnt["stt"]

    t_ag2k = []
    for k in range(4):
        P.waitall("gpsimd", ATTK[k][0])
        P.waitall("gpsimd", ATTK[k][1])
        t_ag2k.append(P.emit("gpsimd", lambda e, k=k: e.collective_compute(
            "AllGather", ALU.bypass, replica_groups=G4,
            ins=[g2i[k].opt()], outs=[g2o[k].opt()]), inc="scc"))

    def go_rs2(e):
        with e.register("ro2") as ro2:
            for k in range(4):
                e.wait_ge(P.sems["scc"], t_ag2k[k])
                e.reg_load(ro2, ms_sb[0:1, 0:1])
                e.reg_mul(ro2, ro2, 131072)
                if k:
                    e.reg_add(ro2, ro2, k * 2097152)
                e.dma_start(out=bass.AP(rs2, k * 524288, [[131072, 4], [1, 131072]]),
                            in_=bass.AP(g2o, ro2, [[524288, 4], [1, 131072]])
                            ).then_inc(P.sems["rs2s"], 16)
    P.cnt["rs2s"] += 4 * 16
    P.on("gpsimd", go_rs2)
    P.wait("gpsimd", "stt", head_done[3])
    wcast(wo_d)

    if stage == 2:
        if "t_rs2" in T:
            P.waitall("gpsimd", "rs2s")
            P.dma("gpsimd", "tapA", bass.AP(T["t_rs2"], 0, [[1024, 2048], [1, 1024]]),
                  bass.AP(rs2, 0, [[1024, 2048], [1, 1024]]))
            P.waitall("gpsimd", "tapA")
        if "t_scal" in T:
            P.wait("sync", "svv", tv_scal)
            P.dma("sync", "tapB", T["t_scal"][:, :], scal_sb[:, :])
            P.waitall("sync", "tapB")
        P.dma("sync", "yme", bass.AP(y, 0, [[2048, 128], [262144, 8], [1, 2048]]),
              bass.AP(x_res, 0, [[16384, 128], [2048, 8], [1, 2048]]))
        P.waitall("sync", "yme")
        return finish(nc, P)

    P.waitall("sync", "rs2s")
    for g in range(16):
        P.dma("sync", "fmao", xn_fm[:, 1024 * g:1024 * (g + 1)], rs2[g % 4, g // 4], transpose=True)

    t_wo_ev = {}
    t_womm = 0
    P.waitall("tensor", "wc")
    P.waitall("tensor", "fmao")
    ap2_prev = [0, 0]
    for tb in range(8):
        b0 = 4 * (tb % 2)
        if tb >= 2:
            P.wait("tensor", "svv", t_wo_ev[tb - 2])
        for ic in range(16):
            for og in range(4):
                P.emit("tensor", lambda e, b0=b0, ic=ic, og=og, tb=tb: e.matmul(
                    psA[:, 512 * (b0 + og):512 * (b0 + og + 1)],
                    xn_fm[:, 1024 * ic + 128 * tb:1024 * ic + 128 * (tb + 1)],
                    w_sb[:, 2048 * ic + 512 * og:2048 * ic + 512 * (og + 1)],
                    start=(ic == 0), stop=(ic == 15)),
                    inc="stt" if (ic == 15 and og == 3) else None)
        t_womm = P.cnt["stt"]
        P.wait("vector", "stt", t_womm)
        for og in range(4):
            P.emit("vector", lambda e, b0=b0, og=og, tb=tb: e.tensor_tensor(
                x_res[:, 2048 * tb + 512 * og:2048 * tb + 512 * (og + 1)],
                psA[:, 512 * (b0 + og):512 * (b0 + og + 1)],
                x_res[:, 2048 * tb + 512 * og:2048 * tb + 512 * (og + 1)], ALU.add),
                inc="svv" if og == 3 else None)
        t_wo_ev[tb] = P.cnt["svv"]
        ln_block(tb, (lambda tb=tb: P.wait("vector", "svv", t_wo_ev[tb])), "x2t", x2t, ap2_prev)

    P.wait("sync", "stt", head_done[3])
    P.wait("sync", "scc", t_agwf)
    for hc0 in range(2):
        P.dma("sync", ("fcA", "fcB")[hc0], fcw_sb[:, 2048 * hc0:2048 * (hc0 + 1)],
              bass.AP(q2fT_o, hc0 * 262144, [[2048, 128], [1, 2048]]))
    P.waitall("scalar", "x2t")
    P.wait("scalar", "stt", t_womm)
    for ic in range(16):
        P.dma("scalar", "fmx2", xn_fm[:, 1024 * ic:1024 * (ic + 1)],
              x2t[:, 128 * ic:128 * (ic + 1)], transpose=True)

    if stage == 3:
        P.wait("sync", "svv", t_wo_ev[7])
        P.dma("sync", "yme", bass.AP(y, 0, [[2048, 128], [262144, 8], [1, 2048]]),
              bass.AP(x_res, 0, [[16384, 128], [2048, 8], [1, 2048]]))
        P.waitall("sync", "yme")
        return finish(nc, P)

    FC = ("fcA", "fcB")
    t_fc_mm = {}
    t_sq = {}
    t_relu = {}
    P.wait("sync", "scc", t_agwf)
    for hc in range(40):
        slot = hc % 2
        r, l = hc // 5, hc % 5
        if hc >= 2:
            P.wait("sync", "stt", t_fc_mm[hc - 2])
            P.dma("sync", FC[slot], fcw_sb[:, 2048 * slot:2048 * (slot + 1)],
                  bass.AP(q2fT_o, (r * 5 + l) * 262144, [[2048, 128], [1, 2048]]))
        P.waitall("tensor", FC[slot])
        P.waitall("tensor", "fmx2")
        if hc >= 2:
            P.wait("tensor", "saa", t_relu[hc - 2])
        pr = hc % 2
        for ic in range(16):
            for hf in range(2):
                P.emit("tensor", lambda e, pr=pr, hf=hf, ic=ic, slot=slot: e.matmul(
                    psA[:, 512 * (2 * pr + hf):512 * (2 * pr + hf + 1)],
                    fcw_sb[:, 2048 * slot + 128 * ic:2048 * slot + 128 * (ic + 1)],
                    xn_fm[:, 1024 * ic + 512 * hf:1024 * ic + 512 * (hf + 1)],
                    start=(ic == 0), stop=(ic == 15)),
                    inc="stt" if (ic == 15 and hf == 1) else None)
        t_fc_mm[hc] = P.cnt["stt"]
        P.wait("scalar", "stt", t_fc_mm[hc])
        if hc >= 2:
            P.wait("scalar", "svv", t_sq[hc - 2])
        for hf in range(2):
            P.emit("scalar", lambda e, pr=pr, hf=hf, slot=slot: e.activation(
                rl_sb[:, 1024 * slot + 512 * hf:1024 * slot + 512 * (hf + 1)],
                psA[:, 512 * (2 * pr + hf):512 * (2 * pr + hf + 1)],
                AF.Relu, scale=scal_sb[:, C_SFC:C_SFC + 1]),
                inc="saa" if hf == 1 else None)
        t_relu[hc] = P.cnt["saa"]
        P.wait("vector", "saa", t_relu[hc])
        t_sq[hc] = P.emit("vector", lambda e, hc=hc, slot=slot: e.tensor_tensor(
            h_sb[:, 1024 * hc:1024 * (hc + 1)],
            rl_sb[:, 1024 * slot:1024 * (slot + 1)],
            rl_sb[:, 1024 * slot:1024 * (slot + 1)], ALU.mult), inc="svv")
    t_h_done = t_sq[39]

    PR = ("prA", "prB")
    OB = ("obA", "obB")
    P.wait("sync", "scc", t_agwp)
    P.wait("sync", "svv", t_h_done)
    t_quad_ev = 0
    t_prw_mm = {}
    for th in range(2):
        for dh2 in range(2):
            qd = 2 * th + dh2
            for hc in range(40):
                slot = hc % 2
                if hc >= 2:
                    P.wait("sync", "stt", t_prw_mm[(qd, hc - 2)])
                elif qd > 0:
                    P.wait("sync", "stt", t_prw_mm[(qd - 1, 38 + hc)])
                P.dma("sync", PR[slot], bass.AP(prw_sb, 1024 * slot, [[2048, 128], [256, 4], [1, 256]]),
                      bass.AP(q2pT_o, 4 * dh2 * 1310720 + 128 * hc * 256, [[256, 128], [1310720, 4], [1, 256]]))
                P.waitall("tensor", PR[slot])
                if hc == 0:
                    P.wait("tensor", "svv", t_h_done)
                    if t_quad_ev:
                        P.wait("tensor", "svv", t_quad_ev)
                for tb in range(4):
                    tbg = 4 * th + tb
                    for bd in range(2):
                        P.emit("tensor", lambda e, tb=tb, bd=bd, hc=hc, slot=slot, tbg=tbg: e.matmul(
                            psA[:, 512 * (2 * tb + bd):512 * (2 * tb + bd + 1)],
                            h_sb[:, 1024 * hc + 128 * tbg:1024 * hc + 128 * (tbg + 1)],
                            prw_sb[:, 1024 * slot + 512 * bd:1024 * slot + 512 * (bd + 1)],
                            start=(hc == 0), stop=(hc == 39)),
                            inc="stt" if (tb == 3 and bd == 1) else None)
                t_prw_mm[(qd, hc)] = P.cnt["stt"]
            P.wait("vector", "stt", t_prw_mm[(qd, 39)])
            for bi in range(8):
                tb, bd = bi // 2, bi % 2
                tbg = 4 * th + tb
                obs = bi % 2
                P.waitall("vector", OB[obs])
                tq = P.emit("vector", lambda e, bi=bi, tbg=tbg, bd=bd, dh2=dh2, obs=obs: e.scalar_tensor_tensor(
                    ob_sb[:, 512 * obs:512 * (obs + 1)],
                    psA[:, 512 * bi:512 * (bi + 1)],
                    scal_sb[:, C_SPR:C_SPR + 1],
                    x_res[:, 2048 * tbg + 1024 * dh2 + 512 * bd:2048 * tbg + 1024 * dh2 + 512 * (bd + 1)],
                    ALU.mult, ALU.add), inc="svv")
                P.wait("sync", "svv", tq)
                P.dma("sync", OB[obs], y[128 * tbg:128 * (tbg + 1),
                                         1024 * dh2 + 512 * bd:1024 * dh2 + 512 * (bd + 1)],
                      ob_sb[:, 512 * obs:512 * (obs + 1)])
            t_quad_ev = P.cnt["svv"]

    P.waitall("sync", "obA")
    P.waitall("sync", "obB")
    return finish(nc, P)


def finish(nc, P):
    with nc.Block() as block:
        for eng in ENGINES:
            fns = P.streams[eng]

            def runner(e, fns=fns):
                for f in fns:
                    f(e)
            getattr(block, eng)(runner)
    assert nc.sbuf_base <= 17408, f"bass bump allocator grew into overlay: {nc.sbuf_base}"
    return nc


def make_consts():
    s = np.arange(128)[:, None]
    q = np.arange(128)[None, :]
    tril_T = (s <= q).astype(np.float32)
    j = np.arange(32)[:, None]
    i = (np.arange(4096)[None, :] // 128)
    um = (j > i).astype(np.float32)
    return tril_T, um, np.eye(128, dtype=np.float32)


def make_in_maps(inputs):
    x = np.asarray(inputs["L0_attn_x"])
    wq = np.ascontiguousarray(np.asarray(inputs["L0_attn_Wq"], dtype=np.float32))
    wk = np.ascontiguousarray(np.asarray(inputs["L0_attn_Wk"], dtype=np.float32))
    wv = np.ascontiguousarray(np.asarray(inputs["L0_attn_Wv"], dtype=np.float32))
    wo = np.ascontiguousarray(np.asarray(inputs["L0_attn_Wo"], dtype=np.float32))
    wfc = np.asarray(inputs["L0_ffn_w_fc"], dtype=np.float32)
    wpr = np.asarray(inputs["L0_ffn_w_proj"], dtype=np.float32)
    tril_T, um, ident = make_consts()
    maps = []
    for c in range(8):
        r = c % 4
        b = c // 4
        maps.append({
            "x_q": np.ascontiguousarray(x[b, 1024 * r:1024 * (r + 1), :], dtype=np.float32),
            "wq": wq, "wk": wk, "wv": wv, "wo": wo,
            "wfc_sl": np.ascontiguousarray(wfc[640 * c:640 * (c + 1), :]),
            "wpr_sl": np.ascontiguousarray(wpr[256 * c:256 * (c + 1), :]),
            "meta": np.array([[r]], dtype=np.uint32),
            "tril": tril_T, "umask": um, "ident": ident,
        })
    return maps


from concourse.bass_utils import run_bass_kernel_spmd

_cache = {}


def kernel(**inputs):
    if "nc" not in _cache:
        _cache["nc"] = build(stage=4, taps=())
    nc = _cache["nc"]
    maps = make_in_maps(inputs)
    res = run_bass_kernel_spmd(nc, maps, core_ids=list(range(8)))
    out = np.empty((2, 4096, 2048), np.float32)
    for c in range(8):
        r = c % 4
        out[c // 4, 1024 * r:1024 * (r + 1), :] = np.asarray(res.results[c]["y"]).reshape(1024, 2048)
    return out
